# version 07
# baseline (speedup 1.0000x reference)
import numpy as np
import ml_dtypes
import concourse.bass as bass
import concourse.mybir as mybir
from concourse.bass_utils import run_bass_kernel_spmd
from contextlib import ExitStack

F32 = mybir.dt.float32
BF16 = mybir.dt.bfloat16
I32 = mybir.dt.int32
AF = mybir.ActivationFunctionType
ALU = mybir.AluOpType
AX = mybir.AxisListType
BF = ml_dtypes.bfloat16


class Cfg:
    def __init__(self, T, TILE, D, NKH, HID):
        self.T, self.TILE, self.D, self.NKH, self.HID = T, TILE, D, NKH, HID
        self.NVH = 2 * NKH
        self.NF = D // 128
        self.NT = T // TILE
        self.NCH = TILE // 128
        self.TQ = T // 4
        self.QW = NKH * 128
        self.VW = self.NVH * 128
        self.ABW = 2 * self.NVH
        self.WC = 2 * self.QW + self.VW
        self.WCOLS = self.WC + self.VW + self.ABW
        self.OUTC = D // 128
        self.NHC = HID // 128
        self.NHB = HID // 512
        self.GSZ = 4
        self.NGRP = (self.NHC + self.GSZ - 1) // self.GSZ
        self.NLEV = 4
        self.NQB = self.TQ // 128
        self.NTB = T // 128
        self.W2 = min(512, TILE)
        self.W4 = min(512, self.TQ)
        self.NZC = self.VW // 512 if self.VW >= 512 else 0
        self.ZW = 512 if self.VW >= 512 else self.VW
        if self.NZC == 0:
            self.NZC = 1


REAL = Cfg(T=4096, TILE=512, D=2048, NKH=4, HID=5632)
TINY = Cfg(T=512, TILE=128, D=256, NKH=2, HID=512)


class K:
    ENGS = ("tensor", "vector", "scalar", "gpsimd", "sync")
    ESEM = {"tensor": "s_pe", "vector": "s_dve", "scalar": "s_act",
            "gpsimd": "s_gp", "sync": "s_sp"}

    def __init__(self):
        self.streams = {e: [] for e in K.ENGS}
        self.count = {}
        self.mark = {e: {} for e in K.ENGS}
        self.writer = {}
        self.readers = {}
        self.dma_pend = {}
        self.sem_names = set(K.ESEM.values())

    def _wait(self, eng, sem, val):
        if val <= 0:
            return
        if self.mark[eng].get(sem, 0) >= val:
            return
        self.mark[eng][sem] = val
        self.streams[eng].append(("w", sem, val))

    def _deps(self, eng, reads, writes):
        for k in reads:
            for s, v in self.writer.get(k, {}).items():
                self._wait(eng, s, v)
        for k in writes:
            for s, v in self.writer.get(k, {}).items():
                self._wait(eng, s, v)
            for s, v in self.readers.get(k, {}).items():
                self._wait(eng, s, v)

    def _post(self, sem, v, reads, writes):
        for k in writes:
            self.writer.setdefault(k, {})[sem] = v
            self.readers[k] = {}
        for k in reads:
            self.readers.setdefault(k, {})[sem] = v

    PSUM_KEYS = frozenset(
        ["PP0", "PP1", "PA", "PB", "PC", "PD", "PF0", "PF1"])

    def _excl(self, reads, writes):
        pr = [k for k in reads if k in K.PSUM_KEYS]
        if pr:
            reads = [k for k in reads if k not in K.PSUM_KEYS]
            writes = list(writes) + pr
        return reads, writes

    def emit(self, eng, fns, reads=(), writes=()):
        if not isinstance(fns, (list, tuple)):
            fns = [fns]
        reads, writes = self._excl(reads, writes)
        self._deps(eng, reads, writes)
        sem = K.ESEM[eng]
        self.count[sem] = self.count.get(sem, 0) + 1
        v = self.count[sem]
        for i, fn in enumerate(fns):
            self.streams[eng].append(
                ("i", fn, sem if i == len(fns) - 1 else None, 1))
        self._post(sem, v, reads, writes)
        return v

    def dma(self, eng, sem, fns, reads=(), writes=()):
        if not isinstance(fns, (list, tuple)):
            fns = [fns]
        reads, writes = self._excl(reads, writes)
        self.sem_names.add(sem)
        if self.dma_pend.get(sem):
            self._wait(eng, sem, self.count.get(sem, 0))
        self._deps(eng, reads, writes)
        for fn in fns:
            self.count[sem] = self.count.get(sem, 0) + 16
            self.streams[eng].append(("i", fn, sem, 16))
        v = self.count[sem]
        self.dma_pend[sem] = True
        self._post(sem, v, reads, writes)
        return v

    def fence(self):
        for eng in K.ENGS:
            own = K.ESEM[eng]
            for sem, cnt in list(self.count.items()):
                if sem != own:
                    self._wait(eng, sem, cnt)

    def replay(self, nc, semh):
        with nc.Block() as block:
            for name in K.ENGS:
                stream = self.streams[name]
                if not stream:
                    continue

                def body(eng, stream=stream):
                    for it in stream:
                        if it[0] == "w":
                            eng.wait_ge(semh[it[1]], it[2])
                        else:
                            bi = it[1](eng)
                            if it[2] is not None:
                                bi.then_inc(semh[it[2]], it[3])
                getattr(block, name)(body)


def build(cfg, upto=5, p2lvl=10**9):
    c = cfg
    nc = bass.Bass("TRN2", target_bir_lowering=False, debug=False)
    k = K()

    dp = nc.declare_dram_parameter
    xfull = dp("xfull", [c.T, c.D], F32, isOutput=False)
    xq = dp("xq", [c.TQ, c.D], F32, isOutput=False)
    xqT = dp("xqT", [c.D, c.TQ], F32, isOutput=False)
    wgc = dp("wgc", [c.NF, 128, c.WC], F32, isOutput=False)
    wzc = dp("wzc", [c.NF, c.NZC, 128, c.ZW], F32, isOutput=False)
    wabc = dp("wabc", [c.NF, 128, c.ABW], F32, isOutput=False)
    convw = dp("convw", [c.WC, 4], F32, isOutput=False)
    alog = dp("alog", [c.NVH, 1], F32, isOutput=False)
    dtb = dp("dtb", [c.NVH, 1], F32, isOutput=False)
    normw4 = dp("normw4", [1, 512], F32, isOutput=False)
    wout = dp("wout", [c.VW, c.D], F32, isOutput=False)
    wgT = dp("wgT", [c.D, c.HID], I32, isOutput=False)
    wuT = dp("wuT", [c.D, c.HID], I32, isOutput=False)
    wdT = dp("wdT", [c.HID, c.D], I32, isOutput=False)
    sg2 = dp("sg2", [128, c.NHC], F32, isOutput=False)
    su2 = dp("su2", [128, c.NHC], F32, isOutput=False)
    sd2 = dp("sd2", [128, c.OUTC], F32, isOutput=False)
    eyebf = dp("eyebf", [128, 128], BF16, isOutput=False)
    eyef = dp("eyef", [128, 128], F32, isOutput=False)
    ltri = dp("ltri", [128, 128], F32, isOutput=False)
    maskE = dp("maskE", [128, 128], F32, isOutput=False)
    mask2 = dp("mask2", [128, 128], F32, isOutput=False)
    mask01 = dp("mask01", [128, 128], F32, isOutput=False)
    rowsel = dp("rowsel", [c.NVH, 128 * c.NVH], F32, isOutput=False)
    outfm = dp("outfm", [c.D, c.TQ], F32, isOutput=True)

    xn_sc = nc.dram_tensor("xn_sc", [c.T, c.D], BF16)
    xnq_sc = nc.dram_tensor("xnq_sc", [c.TQ, c.D], BF16)
    ofm_sc = nc.dram_tensor("ofm_sc", [c.NVH, 128, c.T], BF16)
    rsin = nc.dram_tensor("rsin", [4, c.D, c.TQ], F32)
    gu_sc = nc.dram_tensor("gu_sc", [c.NHC * 128, c.TQ], BF16)
    rsout = nc.dram_tensor("rsout", [c.D, c.TQ], F32)

    st = ExitStack()
    sb = lambda n, s, d: st.enter_context(nc.sbuf_tensor(n, s, d))
    ps = lambda n, s, d: st.enter_context(nc.psum_tensor(n, s, d))

    PP0 = ps("PP0", [128, 512], F32)
    PP1 = ps("PP1", [128, 512], F32)
    PA = ps("PA", [128, 512], F32)
    PB = ps("PB", [128, 512], F32)
    PC = ps("PC", [128, 512], F32)
    PD = ps("PD", [128, 512], F32)
    PF0 = ps("PF0", [128, 1024], BF16)
    PF1 = ps("PF1", [128, 1024], BF16)

    s_eyebf = sb("s_eyebf", [128, 128], BF16)
    s_eyef = sb("s_eyef", [128, 128], F32)
    s_ltri = sb("s_ltri", [128, 128], F32)
    s_maskE = sb("s_maskE", [128, 128], F32)
    s_mask2 = sb("s_mask2", [128, 128], F32)
    s_mask01 = sb("s_mask01", [128, 128], F32)
    s_normw = sb("s_normw", [1, 512], F32)
    s_alog = sb("s_alog", [c.NVH, 1], F32)
    s_dtb = sb("s_dtb", [c.NVH, 1], F32)
    s_nega = sb("s_nega", [c.NVH, 1], F32)
    s_cst = sb("s_cst", [128, 8], F32)
    s_ones1 = sb("s_ones1", [1, 128], F32)
    s_onesc = sb("s_onesc", [128, 1], F32)
    s_onesb = sb("s_onesb", [128, 1], BF16)
    s_m128 = sb("s_m128", [1, 128], BF16)
    s_normb = sb("s_normb", [128, 512], F32)
    s_onesq = sb("s_onesq", [128, 128], F32)
    s_rsel = sb("s_rsel", [c.NVH, 128 * c.NVH], F32)
    s_convw = sb("s_convw", [128, 4 * (c.WC // 128)], F32)
    s_sg2 = sb("s_sg2", [128, c.NHC], F32)
    s_su2 = sb("s_su2", [128, c.NHC], F32)
    s_sd2 = sb("s_sd2", [128, c.OUTC], F32)

    CST = {1e-6: 0, 128.0: 1, 128e-6: 2, 1.0 / c.D: 3, 1e-5: 4,
           1.0 / 128: 5, 1.0: 6, -1.0: 7}

    def cst(v, p=128):
        return s_cst[0:p, CST[v]:CST[v] + 1]

    k.dma("sync", "d_c", [
        (lambda e, o=o_, i=i_: e.dma_start(out=o, in_=i))
        for o_, i_ in [
            (s_eyebf[:, :], eyebf[:, :]), (s_eyef[:, :], eyef[:, :]),
            (s_ltri[:, :], ltri[:, :]), (s_maskE[:, :], maskE[:, :]),
            (s_mask2[:, :], mask2[:, :]), (s_mask01[:, :], mask01[:, :]),
            (s_normw[:, :], normw4[:, :]), (s_alog[:, :], alog[:, :]),
            (s_dtb[:, :], dtb[:, :]), (s_sg2[:, :], sg2[:, :]),
            (s_su2[:, :], su2[:, :]), (s_sd2[:, :], sd2[:, :]),
            (s_rsel[:, :], rowsel[:, :]),
        ] + [(s_convw[:, 4 * f:4 * f + 4], convw[128 * f:128 * (f + 1), :])
             for f in range(c.WC // 128)]
    ], writes=["consts"])
    for val, col in CST.items():
        k.emit("vector", (lambda e, v=val, cl=col: e.memset(
            s_cst[:, cl:cl + 1], v)), writes=["consts"])
    k.emit("vector", lambda e: e.memset(s_ones1[:, :], 1.0), writes=["consts"])
    k.emit("vector", lambda e: e.memset(s_onesc[:, :], 1.0), writes=["consts"])
    k.emit("vector", lambda e: e.memset(s_onesb[:, :], 1.0), writes=["consts"])
    k.emit("vector", lambda e: e.memset(s_m128[:, :], -128.0), writes=["consts"])
    k.emit("vector", lambda e: e.memset(s_onesq[:, :], 1.0), writes=["consts"])
    k.emit("scalar", lambda e: e.activation(s_nega[:, :], s_alog[:, :], AF.Exp),
           reads=["consts"], writes=["nega"])
    k.emit("vector", lambda e: e.tensor_scalar(
        s_nega[:, :], s_nega[:, :], -1.0, None, ALU.mult),
        reads=["nega"], writes=["nega"])
    k.emit("tensor", lambda e: e.matmul(
        PA[:, 0:512], s_ones1[:, :], s_normw[:, :], start=True, stop=True),
        reads=["consts"], writes=["PA"])
    k.emit("vector", lambda e: e.tensor_copy(s_normb[:, :], PA[:, 0:512]),
           reads=["PA"], writes=["normb"])

    stP1 = ExitStack()
    xl = [stP1.enter_context(nc.sbuf_tensor(f"xln{i}", [128, c.D], F32))
          for i in range(2)]
    xb = [stP1.enter_context(nc.sbuf_tensor(f"xnb{i}", [128, c.D], BF16))
          for i in range(2)]
    dumpb = stP1.enter_context(nc.sbuf_tensor("dumpb", [128, c.D], F32))
    lncol = stP1.enter_context(nc.sbuf_tensor("lncol", [128, 4], F32))

    def ln_block(src_ap, dst_ap, p, keys):
        xk, bk = f"xl{p}", f"xb{p}"
        xlp, xbp = xl[p], xb[p]
        k.dma("sync", f"d_x{p}", (lambda e: e.dma_start(
            out=xlp[:, :], in_=src_ap)), writes=[xk])
        k.emit("vector", lambda e: e.reduce_sum(
            lncol[:, 0:1], xlp[:, :], axis=AX.X),
            reads=[xk], writes=["lncol"])
        k.emit("scalar", lambda e: e.activation(
            lncol[:, 1:2], lncol[:, 0:1], AF.Copy, scale=cst(1.0 / c.D)),
            reads=["lncol", "consts"], writes=["lncol"])
        k.emit("vector", lambda e: e.tensor_scalar(
            xlp[:, :], xlp[:, :], lncol[:, 1:2], None, ALU.subtract),
            reads=[xk, "lncol"], writes=[xk])
        k.emit("scalar", lambda e: e.activation(
            dumpb[:, :], xlp[:, :], AF.Square, accum_out=lncol[:, 2:3]),
            reads=[xk], writes=["dumpb", "lncol"])
        k.emit("scalar", lambda e: e.activation(
            lncol[:, 3:4], lncol[:, 2:3], AF.Sqrt,
            scale=cst(1.0 / c.D), bias=cst(1e-5)),
            reads=["lncol", "consts"], writes=["lncol"])
        k.emit("vector", lambda e: e.reciprocal(
            lncol[:, 3:4], lncol[:, 3:4]),
            reads=["lncol"], writes=["lncol"])
        k.emit("vector", lambda e: e.tensor_scalar(
            xbp[:, :], xlp[:, :], lncol[:, 3:4], None, ALU.mult),
            reads=[xk, "lncol"], writes=[bk])
        k.dma("sync", f"d_xw{p}", (lambda e: e.dma_start(
            out=dst_ap, in_=xbp[:, :])), reads=[bk], writes=keys)

    for tb in range(c.NTB):
        ln_block(xfull[128 * tb:128 * (tb + 1), :],
                 xn_sc[128 * tb:128 * (tb + 1), :], tb % 2, ["xn_sc"])
    for qb in range(c.NQB):
        ln_block(xq[128 * qb:128 * (qb + 1), :],
                 xnq_sc[128 * qb:128 * (qb + 1), :], qb % 2, ["xnq_sc"])
    k.fence()
    stP1.close()

    def finish_dummy():
        sfin = st.enter_context(nc.sbuf_tensor("s_dummy", [128, c.TQ], F32))
        k.emit("vector", lambda e: e.memset(sfin[:, :], 0.0), writes=["dumz"])
        k.dma("sync", "d_out", [
            (lambda e, o=o: e.dma_start(
                out=outfm[128 * o:128 * (o + 1), :], in_=sfin[:, :]))
            for o in range(c.OUTC)], reads=["dumz"], writes=["outz"])
        k._wait("gpsimd", "d_out", k.count["d_out"])
        semh = {}
        for name in sorted(set(list(k.count.keys()) + list(K.ESEM.values()))):
            semh[name] = st.enter_context(nc.semaphore(name))
        k.replay(nc, semh)
        st.close()
        return nc

    if upto <= 1:
        return finish_dummy()

    QKV_OC = c.WC // 128
    NWG = (QKV_OC + 3) // 4

    stP2 = ExitStack()
    sbp = lambda n, s, d: stP2.enter_context(nc.sbuf_tensor(n, s, d))
    s_xnf = [sbp(f"s_xnf{f}", [128, c.TILE], BF16) for f in range(c.NF)]
    s_wgF = [sbp(f"s_wgF{f}", [128, c.WC], BF16) for f in range(c.NF)]
    s_wz = [[sbp(f"s_wz{z}_{f}", [128, c.ZW], BF16) for f in range(c.NF)]
            for z in range(c.NZC)]
    s_wab = [sbp(f"s_wab{f}", [128, c.ABW], BF16) for f in range(c.NF)]
    s_projb = sbp("s_projb", [128, QKV_OC * (c.TILE + 3)], BF16)
    s_qkvs = sbp("s_qkvs", [128, QKV_OC * c.TILE], BF16)
    s_cacc = sbp("s_cacc", [128, c.TILE], F32)
    s_zg = sbp("s_zg", [128, c.VW], F32)
    s_gc = sbp("s_gc", [c.NVH, c.TILE], F32)
    s_bc = sbp("s_bc", [c.NVH, c.TILE], F32)
    s_gtm = sbp("s_gtm", [128, c.NVH], F32)
    s_btm = sbp("s_btm", [128, c.NVH], F32)
    s_G = sbp("s_G", [128, c.NVH], F32)
    s_Gfm = sbp("s_Gfm", [c.NVH, 128], F32)
    s_gam = sbp("s_gam", [128, 4 * c.NVH], F32)
    s_nbf = sbp("s_nbf", [c.NVH, c.TILE], F32)
    s_sks = [sbp(f"s_sks{i}", [1, 256], F32) for i in range(2)]
    s_ktf = [sbp(f"s_ktf{i}", [128, 128], BF16) for i in range(2)]
    s_qtf = [sbp(f"s_qtf{i}", [128, 128], BF16) for i in range(2)]
    s_ktm = [sbp(f"s_ktm{i}", [128, 128], BF16) for i in range(2)]
    s_ksq = [sbp(f"s_ksq{i}", [128, 256], F32) for i in range(2)]
    s_kk = [sbp(f"s_kk{i}", [128, 256], F32) for i in range(2)]
    s_gbc = sbp("s_gbc", [128, 128 * c.NVH], F32)
    s_bbc = sbp("s_bbc", [128, 128 * c.NVH], F32)
    s_argE = [sbp(f"s_argE{i}", [128, 128], F32) for i in range(2)]
    s_argN = [sbp(f"s_argN{i}", [128, 128], F32) for i in range(2)]
    s_E = [sbp(f"s_E{i}", [128, 128], F32) for i in range(2)]
    s_E2 = [sbp(f"s_E2{i}", [128, 128], F32) for i in range(2)]
    s_es = [sbp(f"s_es{i}", [128, 128], F32) for i in range(2)]
    s_ntm = [sbp(f"s_ntm{i}", [128, 128], F32) for i in range(2)]
    s_NT = [sbp(f"s_NT{i}", [128, 128], BF16) for i in range(2)]
    s_N = [sbp(f"s_N{i}", [128, 128], BF16) for i in range(2)]
    s_TT = [[sbp(f"s_TT{i}_{j}", [128, 128], BF16) for j in range(2)]
            for i in range(2)]
    s_P = [[sbp(f"s_P{i}_{j}", [128, 128], BF16) for j in range(2)]
           for i in range(2)]
    s_PT = [[sbp(f"s_PT{i}_{j}", [128, 128], BF16) for j in range(2)]
            for i in range(2)]
    s_A2T = [sbp(f"s_A2T{i}", [128, 128], BF16) for i in range(2)]
    s_RHS = [sbp(f"s_RHS{i}", [128, 128], BF16) for i in range(2)]
    s_Xbf = [sbp(f"s_Xbf{i}", [128, 128], BF16) for i in range(2)]
    s_Xw = [sbp(f"s_Xw{i}", [128, 128], BF16) for i in range(2)]
    s_Otm = [sbp(f"s_Otm{i}", [128, 128], F32) for i in range(2)]
    s_dump = [sbp(f"s_dump{i}", [128, 128], F32) for i in range(2)]
    s_oc2 = [sbp(f"s_oc2{i}", [128, 2], F32) for i in range(2)]
    s_S = [sbp(f"s_S{h}", [128, 128], F32) for h in range(c.NVH)]
    s_Sbf = [sbp(f"s_Sbf{h}", [128, 128], BF16) for h in range(c.NVH)]
    s_ocb = [sbp(f"s_ocb{i}", [128, 128], BF16) for i in range(2)]
    ofc = [0]

    for h in range(c.NVH):
        k.emit("vector", (lambda e, h=h: e.memset(s_S[h][:, :], 0.0)),
               writes=[f"S{h}"])
        k.emit("vector", (lambda e, h=h: e.memset(s_Sbf[h][:, :], 0.0)),
               writes=[f"Sbf{h}"])
    for oc in range(QKV_OC):
        k.emit("vector", (lambda e, oc=oc: e.memset(
            s_projb[:, oc * (c.TILE + 3):oc * (c.TILE + 3) + 3], 0.0)),
            writes=[f"projb{oc}"])

    def load_xnf(t):
        k.dma("sync", "d_xnr", [
            (lambda e, f=f, t=t: e.dma_start(
                out=s_xnf[f][:, :],
                in_=xn_sc[t * c.TILE:(t + 1) * c.TILE, 128 * f:128 * (f + 1)],
                transpose=True))
            for f in range(c.NF)],
            reads=["xn_sc"], writes=[f"xnf{f}" for f in range(c.NF)])

    def load_wg_once():
        k.dma("gpsimd", "d_wq", [
            (lambda e, f=f: e.dma_start(out=s_wgF[f][:, :], in_=wgc[f]))
            for f in range(c.NF)], writes=["wgF"])

    def load_wz(z):
        k.dma("gpsimd", f"d_wz{z}", [
            (lambda e, f=f, z=z: e.dma_start(
                out=s_wz[z][f][:, :], in_=wzc[f, z]))
            for f in range(c.NF)], writes=[f"wz{z}"])

    def load_wab():
        k.dma("gpsimd", "d_wab", [
            (lambda e, f=f: e.dma_start(
                out=s_wab[f][:, :], in_=wabc[f]))
            for f in range(c.NF)], writes=["wab"])

    PBANK = [PP0, PP1, PA, PB]
    PBK = ["PP0", "PP1", "PA", "PB"]
    xnf_keys = [f"xnf{f}" for f in range(c.NF)]

    load_xnf(0)
    load_wg_once()
    for z in range(c.NZC):
        load_wz(z)
    load_wab()

    for t in range(c.NT):
        for g in range(NWG):
            ocs = list(range(4 * g, min(4 * g + 4, QKV_OC)))
            for j, oc in enumerate(ocs):
                bank, bkey = PBANK[j], PBK[j]
                ob = oc * (c.TILE + 3)
                k.emit("tensor", [
                    (lambda e, f=f, oc=oc, bank=bank: e.matmul(
                        bank[:, 0:c.TILE],
                        s_wgF[f][:, 128 * oc:128 * (oc + 1)],
                        s_xnf[f][:, :],
                        start=(f == 0), stop=(f == c.NF - 1)))
                    for f in range(c.NF)],
                    reads=["wgF"] + xnf_keys, writes=[bkey])
                k.emit("scalar", (lambda e, bank=bank, ob=ob: e.activation(
                    s_projb[:, ob + 3:ob + 3 + c.TILE], bank[:, 0:c.TILE],
                    AF.Copy)),
                    reads=[bkey], writes=[f"projb{oc}"])
                k.emit("vector", (lambda e, oc=oc, ob=ob: e.tensor_scalar(
                    s_cacc[:, :], s_projb[:, ob:ob + c.TILE],
                    s_convw[:, 4 * oc:4 * oc + 1], None, ALU.mult)),
                    reads=[f"projb{oc}", "consts"], writes=["cacc"])
                for tap in range(1, 4):
                    k.emit("vector", (lambda e, oc=oc, ob=ob, tap=tap:
                        e.scalar_tensor_tensor(
                            s_cacc[:, :],
                            s_projb[:, ob + tap:ob + tap + c.TILE],
                            s_convw[:, 4 * oc + tap:4 * oc + tap + 1],
                            s_cacc[:, :], ALU.mult, ALU.add)),
                        reads=[f"projb{oc}", "consts", "cacc"],
                        writes=["cacc"])
                k.emit("scalar", (lambda e, oc=oc: e.activation(
                    s_qkvs[:, oc * c.TILE:(oc + 1) * c.TILE], s_cacc[:, :],
                    AF.Silu)),
                    reads=["cacc"], writes=[f"qkvs{oc}"])
                k.emit("vector", (lambda e, oc=oc, ob=ob: e.tensor_copy(
                    s_projb[:, ob:ob + 3],
                    s_projb[:, ob + c.TILE:ob + c.TILE + 3])),
                    reads=[f"projb{oc}"], writes=[f"projb{oc}"])
        if p2lvl < 12:
            continue
        for h2 in range(c.TILE // c.W2):
            co = c.W2 * h2
            k.emit("tensor", [
                (lambda e, f=f, co=co: e.matmul(
                    PA[0:c.NVH, 0:c.W2], s_wab[f][:, 0:c.NVH],
                    s_xnf[f][:, co:co + c.W2],
                    start=(f == 0), stop=(f == c.NF - 1)))
                for f in range(c.NF)],
                reads=["wab"] + xnf_keys, writes=["PA"])
            k.emit("tensor", [
                (lambda e, f=f, co=co: e.matmul(
                    PB[0:c.NVH, 0:c.W2], s_wab[f][:, c.NVH:c.ABW],
                    s_xnf[f][:, co:co + c.W2],
                    start=(f == 0), stop=(f == c.NF - 1)))
                for f in range(c.NF)],
                reads=["wab"] + xnf_keys, writes=["PB"])
            k.emit("scalar", (lambda e, co=co: e.activation(
                s_gc[:, co:co + c.W2], PA[0:c.NVH, 0:c.W2], AF.Exp,
                bias=s_dtb[:, 0:1])),
                reads=["PA", "consts"], writes=["gbc"])
            k.emit("scalar", (lambda e, co=co: e.add(
                s_gc[:, co:co + c.W2], s_gc[:, co:co + c.W2], 1.0)),
                reads=["gbc", "consts"], writes=["gbc"])
            k.emit("scalar", (lambda e, co=co: e.activation(
                s_gc[:, co:co + c.W2], s_gc[:, co:co + c.W2],
                AF.Ln)), reads=["gbc"], writes=["gbc"])
            k.emit("vector", (lambda e, co=co: e.tensor_scalar(
                s_gc[:, co:co + c.W2], s_gc[:, co:co + c.W2],
                s_nega[:, 0:1], None, ALU.mult)),
                reads=["gbc", "nega"], writes=["gbc"])
            k.emit("scalar", (lambda e, co=co: e.activation(
                s_bc[:, co:co + c.W2], PB[0:c.NVH, 0:c.W2],
                AF.Sigmoid)), reads=["PB"], writes=["gbc"])
            k.emit("vector", (lambda e, co=co: e.tensor_scalar(
                s_nbf[:, co:co + c.W2], s_bc[:, co:co + c.W2],
                -1.0, None, ALU.mult)),
                reads=["gbc"], writes=["nbf"])
        if p2lvl < 13:
            continue
        for cb in range(c.NCH):
            tok0 = cb * 128
            for z in range(c.NZC):
                zb = PD if z % 2 == 0 else PP1
                zbk = "PD" if z % 2 == 0 else "PP1"
                k.emit("tensor", [
                    (lambda e, f=f, cb=cb, z=z, zb=zb: e.matmul(
                        zb[:, 0:c.ZW],
                        s_xnf[f][:, 128 * cb:128 * (cb + 1)],
                        s_wz[z][f][:, :],
                        start=(f == 0), stop=(f == c.NF - 1)))
                    for f in range(c.NF)],
                    reads=[f"wz{z}"] + xnf_keys, writes=[zbk])
                k.emit("scalar", (lambda e, z=z, zb=zb: e.activation(
                    s_zg[:, c.ZW * z:c.ZW * (z + 1)],
                    zb[:, 0:c.ZW], AF.Silu)),
                    reads=[zbk], writes=["zg"])
                k.emit("vector", (lambda e, z=z: e.tensor_tensor(
                    s_zg[:, c.ZW * z:c.ZW * (z + 1)],
                    s_zg[:, c.ZW * z:c.ZW * (z + 1)],
                    s_normb[:, 0:c.ZW], ALU.mult)),
                    reads=["zg", "normb"], writes=["zg"])
            if cb == c.NCH - 1 and t < c.NT - 1:
                load_xnf(t + 1)
            k.emit("tensor", (lambda e, tok0=tok0: e.transpose(
                PB[0:128, 0:c.NVH], s_gc[:, tok0:tok0 + 128],
                s_eyef[0:c.NVH, 0:c.NVH])),
                reads=["gbc", "consts"], writes=["PB"])
            k.emit("tensor", (lambda e, tok0=tok0: e.transpose(
                PB[0:128, 64:64 + c.NVH], s_bc[:, tok0:tok0 + 128],
                s_eyef[0:c.NVH, 0:c.NVH])),
                reads=["gbc", "consts"], writes=["PB"])
            k.emit("vector", lambda e: e.tensor_copy(
                s_gtm[:, :], PB[0:128, 0:c.NVH]),
                reads=["PB"], writes=["gbtm"])
            k.emit("vector", lambda e: e.tensor_copy(
                s_btm[:, :], PB[0:128, 64:64 + c.NVH]),
                reads=["PB"], writes=["gbtm"])
            k.emit("tensor", lambda e: e.matmul(
                PB[:, 128:128 + c.NVH], s_ltri[:, :], s_gtm[:, :],
                start=True, stop=True),
                reads=["gbtm", "consts"], writes=["PB"])
            k.emit("vector", lambda e: e.tensor_copy(
                s_G[:, :], PB[:, 128:128 + c.NVH]),
                reads=["PB"], writes=["G"])
            k.emit("tensor", lambda e: e.matmul(
                PB[0:c.NVH, 256:384], s_gtm[:, :], s_ltri[:, :],
                start=True, stop=True),
                reads=["gbtm", "consts"], writes=["PB"])
            k.emit("vector", lambda e: e.tensor_copy(
                s_Gfm[:, :], PB[0:c.NVH, 256:384]),
                reads=["PB"], writes=["Gfm"])
            k.emit("tensor", lambda e: e.matmul(
                PB[:, 384:384 + c.NVH], s_onesq[:, :],
                s_gtm[:, :], start=True, stop=True),
                reads=["gbtm", "consts"], writes=["PB"])
            k.emit("vector", lambda e: e.tensor_tensor(
                s_gam[:, 0:c.NVH], PB[:, 384:384 + c.NVH], s_G[:, :],
                ALU.subtract), reads=["G", "PB"], writes=["gam"])
            k.emit("scalar", lambda e: e.activation(
                s_gam[:, 0:c.NVH], s_gam[:, 0:c.NVH], AF.Exp),
                reads=["gam"], writes=["gam"])
            k.emit("scalar", lambda e: e.activation(
                s_gam[:, c.NVH:2 * c.NVH], PB[:, 384:384 + c.NVH], AF.Exp),
                reads=["PB"], writes=["gam"])
            k.emit("scalar", lambda e: e.activation(
                s_gam[:, 2 * c.NVH:3 * c.NVH], s_G[:, :], AF.Exp),
                reads=["G"], writes=["gam"])
            k.emit("vector", lambda e: e.tensor_tensor(
                s_gam[:, 3 * c.NVH:4 * c.NVH], s_gam[:, 2 * c.NVH:3 * c.NVH],
                s_btm[:, :], ALU.mult),
                reads=["gam", "gbtm"], writes=["gam"])
            k.emit("vector", lambda e: e.tensor_scalar(
                s_gam[:, 3 * c.NVH:4 * c.NVH], s_gam[:, 3 * c.NVH:4 * c.NVH],
                -1.0, None, ALU.mult), reads=["gam"], writes=["gam"])

            if p2lvl < 14:
                continue
            for half in range(c.NVH // 4):
                k.emit("tensor", [
                    (lambda e, hh=hh, half=half: e.matmul(
                        PB[:, 128 * (hh - 4 * half):128 * (hh - 4 * half + 1)],
                        s_rsel[:, 128 * hh:128 * (hh + 1)], s_Gfm[:, :],
                        start=True, stop=True))
                    for hh in range(4 * half, 4 * half + 4)],
                    reads=["Gfm", "consts"], writes=["PB"])
                k.emit("vector", (lambda e, half=half: e.tensor_copy(
                    s_gbc[:, 512 * half:512 * (half + 1)], PB[:, 0:512])),
                    reads=["PB"], writes=["gbc"])
                k.emit("tensor", [
                    (lambda e, hh=hh, half=half, tok0=tok0: e.matmul(
                        PB[:, 128 * (hh - 4 * half):128 * (hh - 4 * half + 1)],
                        s_rsel[:, 128 * hh:128 * (hh + 1)],
                        s_nbf[:, tok0:tok0 + 128],
                        start=True, stop=True))
                    for hh in range(4 * half, 4 * half + 4)],
                    reads=["nbf", "consts"], writes=["PB"])
                k.emit("vector", (lambda e, half=half: e.tensor_copy(
                    s_bbc[:, 512 * half:512 * (half + 1)], PB[:, 0:512])),
                    reads=["PB"], writes=["bbc"])
            NB = [PA, PB]
            NBK = ["PA", "PB"]
            DBL = [PC, PP0]
            DBK = ["PC", "PP0"]
            SOB = [PD, PP1]
            SOK = ["PD", "PP1"]
            TRB = [PF0, PF1]
            TRK = ["PF0", "PF1"]
            for kh in range(c.NKH):
                kp = kh % 2
                nb, nbk = NB[kp], NBK[kp]
                qoc, koc = kh, c.NKH + kh
                qsl = s_qkvs[:, qoc * c.TILE + tok0:qoc * c.TILE + tok0 + 128]
                ksl = s_qkvs[:, koc * c.TILE + tok0:koc * c.TILE + tok0 + 128]
                k.emit("scalar", (lambda e, ksl=ksl, kp=kp: e.activation(
                    s_ksq[kp][:, 0:128], ksl, AF.Square)),
                    reads=[f"qkvs{koc}"], writes=[f"ksq{kp}"])
                k.emit("scalar", (lambda e, qsl=qsl, kp=kp: e.activation(
                    s_ksq[kp][:, 128:256], qsl, AF.Square)),
                    reads=[f"qkvs{qoc}"], writes=[f"ksq{kp}"])
                k.emit("tensor", [
                    (lambda e, kp=kp, nb=nb: e.matmul(
                        nb[0:1, 256:384], s_onesc[:, :],
                        s_ksq[kp][:, 0:128], start=True, stop=True)),
                    (lambda e, kp=kp, nb=nb: e.matmul(
                        nb[0:1, 384:512], s_onesc[:, :],
                        s_ksq[kp][:, 128:256], start=True, stop=True))],
                    reads=[f"ksq{kp}", "consts"], writes=[nbk])
                k.emit("scalar", (lambda e, kp=kp, nb=nb: e.activation(
                    s_sks[kp][:, 0:128], nb[0:1, 256:384], AF.Sqrt,
                    bias=cst(1e-6, 1))),
                    reads=[nbk, "consts"], writes=[f"sks{kp}"])
                k.emit("scalar", (lambda e, kp=kp, nb=nb: e.activation(
                    s_sks[kp][:, 128:256], nb[0:1, 384:512], AF.Sqrt,
                    scale=cst(128.0, 1), bias=cst(128e-6, 1))),
                    reads=[nbk, "consts"], writes=[f"sks{kp}"])
                k.emit("vector", (lambda e, kp=kp: e.reciprocal(
                    s_sks[kp][:, :], s_sks[kp][:, :])),
                    reads=[f"sks{kp}"], writes=[f"sks{kp}"])
                k.emit("tensor", [
                    (lambda e, kp=kp, nb=nb: e.matmul(
                        nb[:, 256:384], s_ones1[:, :],
                        s_sks[kp][:, 0:128], start=True, stop=True)),
                    (lambda e, kp=kp, nb=nb: e.matmul(
                        nb[:, 384:512], s_ones1[:, :],
                        s_sks[kp][:, 128:256], start=True, stop=True))],
                    reads=[f"sks{kp}", "consts"], writes=[nbk])
                k.emit("vector", (lambda e, ksl=ksl, kp=kp, nb=nb:
                    e.tensor_tensor(
                        s_ktf[kp][:, :], ksl, nb[:, 256:384], ALU.mult)),
                    reads=[f"qkvs{koc}", nbk], writes=[f"ktf{kp}"])
                k.emit("vector", (lambda e, qsl=qsl, kp=kp, nb=nb:
                    e.tensor_tensor(
                        s_qtf[kp][:, :], qsl, nb[:, 384:512], ALU.mult)),
                    reads=[f"qkvs{qoc}", nbk], writes=[f"qtf{kp}"])
                k.emit("tensor", (lambda e, kp=kp: e.transpose(
                    TRB[kp][:, 0:128], s_ktf[kp][:, :], s_eyebf[:, :])),
                    reads=[f"ktf{kp}", "consts"], writes=[TRK[kp]])
                k.emit("vector", (lambda e, kp=kp: e.tensor_copy(
                    s_ktm[kp][:, :], TRB[kp][:, 0:128])),
                    reads=[TRK[kp]], writes=[f"ktm{kp}"])
                k.emit("tensor", [
                    (lambda e, kp=kp, nb=nb: e.matmul(
                        nb[:, 0:128], s_ktf[kp][:, :], s_ktf[kp][:, :],
                        start=True, stop=True)),
                    (lambda e, kp=kp, nb=nb: e.matmul(
                        nb[:, 128:256], s_ktf[kp][:, :], s_qtf[kp][:, :],
                        start=True, stop=True))],
                    reads=[f"ktf{kp}", f"qtf{kp}"], writes=[nbk])
                k.emit("vector", (lambda e, kp=kp, nb=nb: e.tensor_copy(
                    s_kk[kp][:, :], nb[:, 0:256])),
                    reads=[nbk], writes=[f"kk{kp}"])

                if p2lvl < 15:
                    continue
                for hh in (2 * kh, 2 * kh + 1):
                    hp = hh % 2
                    db, dbk = DBL[hp], DBK[hp]
                    so, sok = SOB[hp], SOK[hp]
                    tr, trk = TRB[hp], TRK[hp]
                    voc = 2 * c.NKH + hh
                    gsl = s_gbc[:, 128 * hh:128 * (hh + 1)]
                    bsl = s_bbc[:, 128 * hh:128 * (hh + 1)]
                    k.emit("vector", (lambda e, hh=hh, hp=hp, gsl=gsl:
                        e.scalar_tensor_tensor(
                            s_argE[hp][:, :], gsl, s_G[:, hh:hh + 1],
                            s_maskE[:, :], ALU.subtract, ALU.add)),
                        reads=["gbc", "G", "consts"], writes=[f"argE{hp}"])
                    k.emit("vector", (lambda e, hh=hh, hp=hp, gsl=gsl:
                        e.scalar_tensor_tensor(
                            s_argN[hp][:, :], gsl, s_G[:, hh:hh + 1],
                            s_mask2[:, :], ALU.subtract, ALU.add)),
                        reads=["gbc", "G", "consts"], writes=[f"argN{hp}"])
                    k.emit("scalar", (lambda e, hp=hp: e.activation(
                        s_E[hp][:, :], s_argE[hp][:, :], AF.Exp)),
                        reads=[f"argE{hp}"], writes=[f"E{hp}"])
                    k.emit("scalar", (lambda e, hp=hp: e.activation(
                        s_E2[hp][:, :], s_argN[hp][:, :], AF.Exp,
                        scale=cst(-1.0))),
                        reads=[f"argN{hp}", "consts"], writes=[f"E2{hp}"])
                    if p2lvl < 1520:
                        continue
                    k.emit("vector", (lambda e, hp=hp, kp=kp: e.tensor_tensor(
                        s_A2T[hp][:, :], s_E[hp][:, :], s_kk[kp][:, 128:256],
                        ALU.mult)),
                        reads=[f"E{hp}", f"kk{kp}"], writes=[f"A2T{hp}"])
                    k.emit("vector", (lambda e, hp=hp: e.tensor_tensor(
                        s_es[hp][:, :], s_E[hp][:, :], s_mask01[:, :],
                        ALU.mult)),
                        reads=[f"E{hp}", "consts"], writes=[f"es{hp}"])
                    k.emit("vector", (lambda e, hp=hp, kp=kp: e.tensor_tensor(
                        s_ntm[hp][:, :], s_es[hp][:, :], s_kk[kp][:, 0:128],
                        ALU.mult)),
                        reads=[f"es{hp}", f"kk{kp}"], writes=[f"ntm{hp}"])
                    k.emit("vector", (lambda e, hp=hp, bsl=bsl: e.tensor_tensor(
                        s_NT[hp][:, :], s_ntm[hp][:, :], bsl, ALU.mult)),
                        reads=[f"ntm{hp}", "bbc"], writes=[f"NT{hp}"])
                    k.emit("vector", (lambda e, hp=hp, kp=kp: e.tensor_tensor(
                        s_es[hp][:, :], s_E2[hp][:, :], s_kk[kp][:, 0:128],
                        ALU.mult)),
                        reads=[f"E2{hp}", f"kk{kp}"], writes=[f"es{hp}"])
                    k.emit("vector", (lambda e, hh=hh, hp=hp: e.tensor_scalar(
                        s_N[hp][:, :], s_es[hp][:, :],
                        s_btm[:, hh:hh + 1], -1.0,
                        ALU.mult, ALU.mult)),
                        reads=[f"es{hp}", "gbtm"], writes=[f"N{hp}"])
                    if p2lvl < 1530:
                        continue
                    k.emit("vector", (lambda e, hp=hp: e.tensor_tensor(
                        s_TT[hp][0][:, :], s_NT[hp][:, :], s_eyebf[:, :],
                        ALU.add)),
                        reads=[f"NT{hp}", "consts"], writes=[f"TT{hp}0"])
                    k.emit("vector", (lambda e, hp=hp: e.tensor_copy(
                        s_P[hp][0][:, :], s_N[hp][:, :])),
                        reads=[f"N{hp}"], writes=[f"P{hp}0"])
                    k.emit("vector", (lambda e, hp=hp: e.tensor_copy(
                        s_PT[hp][0][:, :], s_NT[hp][:, :])),
                        reads=[f"NT{hp}"], writes=[f"PT{hp}0"])
                    pp = 0
                    for lv in range(1, c.NLEV + 1):
                        np_, op_ = 1 - pp, pp
                        last = (lv == c.NLEV)
                        k.emit("tensor", (lambda e, op_=op_, hp=hp, db=db:
                            e.matmul(
                                db[:, 0:128], s_PT[hp][op_][:, :],
                                s_P[hp][op_][:, :], start=True, stop=True)),
                            reads=[f"P{hp}{op_}", f"PT{hp}{op_}"], writes=[dbk])
                        if not last:
                            k.emit("tensor", (lambda e, op_=op_, hp=hp, db=db:
                                e.matmul(
                                    db[:, 128:256], s_P[hp][op_][:, :],
                                    s_PT[hp][op_][:, :],
                                    start=True, stop=True)),
                                reads=[f"P{hp}{op_}", f"PT{hp}{op_}"],
                                writes=[dbk])
                        k.emit("scalar", (lambda e, np_=np_, hp=hp, db=db:
                            e.activation(
                                s_P[hp][np_][:, :], db[:, 0:128], AF.Copy)),
                            reads=[dbk], writes=[f"P{hp}{np_}"])
                        if not last:
                            k.emit("vector", (lambda e, np_=np_, hp=hp, db=db:
                                e.tensor_copy(
                                    s_PT[hp][np_][:, :], db[:, 128:256])),
                                reads=[dbk], writes=[f"PT{hp}{np_}"])
                        k.emit("tensor", (lambda e, np_=np_, op_=op_, hp=hp,
                                          db=db: e.matmul(
                            db[:, 256:384], s_P[hp][np_][:, :],
                            s_TT[hp][op_][:, :], start=True, stop=True)),
                            reads=[f"P{hp}{np_}", f"TT{hp}{op_}"], writes=[dbk])
                        k.emit("vector", (lambda e, np_=np_, op_=op_, hp=hp,
                                          db=db: e.tensor_tensor(
                            s_TT[hp][np_][:, :], db[:, 256:384],
                            s_TT[hp][op_][:, :], ALU.add)),
                            reads=[dbk, f"TT{hp}{op_}"], writes=[f"TT{hp}{np_}"])
                        pp = np_
                    if p2lvl < 1540:
                        continue
                    k.emit("tensor", (lambda e, hh=hh, hp=hp, kp=kp, so=so:
                        e.matmul(
                            so[:, 0:128], s_ktf[kp][:, :], s_Sbf[hh][:, :],
                            start=True, stop=True)),
                        reads=[f"ktf{kp}", f"Sbf{hh}"], writes=[sok])
                    k.emit("tensor", (lambda e, voc=voc, tok0=tok0, tr=tr:
                        e.transpose(
                            tr[:, 128:256],
                            s_qkvs[:, voc * c.TILE + tok0:
                                   voc * c.TILE + tok0 + 128],
                            s_eyebf[:, :])),
                        reads=[f"qkvs{voc}", "consts"], writes=[trk])
                    k.emit("scalar", (lambda e, hh=hh, hp=hp, tr=tr:
                        e.activation(
                            s_dump[hp][:, :], tr[:, 128:256], AF.Copy,
                            scale=s_btm[:, hh:hh + 1])),
                        reads=[trk, "gbtm"], writes=[f"dump{hp}"])
                    k.emit("vector", (lambda e, hh=hh, hp=hp, so=so:
                        e.scalar_tensor_tensor(
                            s_RHS[hp][:, :], so[:, 0:128],
                            s_gam[:, 3 * c.NVH + hh:3 * c.NVH + hh + 1],
                            s_dump[hp][:, :], ALU.mult, ALU.add)),
                        reads=[sok, "gam", f"dump{hp}"], writes=[f"RHS{hp}"])
                    if p2lvl < 1550:
                        continue
                    k.emit("tensor", (lambda e, pp=pp, hp=hp, db=db: e.matmul(
                        db[:, 384:512], s_TT[hp][pp][:, :], s_RHS[hp][:, :],
                        start=True, stop=True)),
                        reads=[f"TT{hp}{pp}", f"RHS{hp}"], writes=[dbk])
                    k.emit("scalar", (lambda e, hp=hp, db=db: e.activation(
                        s_Xbf[hp][:, :], db[:, 384:512], AF.Copy)),
                        reads=[dbk], writes=[f"Xbf{hp}"])
                    k.emit("vector", (lambda e, hh=hh, hp=hp, db=db:
                        e.tensor_scalar(
                            s_Xw[hp][:, :], db[:, 384:512],
                            s_gam[:, hh:hh + 1], None, ALU.mult)),
                        reads=[dbk, "gam"], writes=[f"Xw{hp}"])
                    if p2lvl < 1560:
                        continue
                    k.emit("tensor", (lambda e, hh=hh, hp=hp, kp=kp, so=so:
                        e.matmul(
                            so[:, 128:256], s_qtf[kp][:, :], s_Sbf[hh][:, :],
                            start=True, stop=True)),
                        reads=[f"qtf{kp}", f"Sbf{hh}"], writes=[sok])
                    k.emit("tensor", (lambda e, hp=hp, so=so: e.matmul(
                        so[:, 256:384], s_A2T[hp][:, :], s_Xbf[hp][:, :],
                        start=True, stop=True)),
                        reads=[f"A2T{hp}", f"Xbf{hp}"], writes=[sok])
                    k.emit("scalar", (lambda e, hh=hh, hp=hp, so=so:
                        e.activation(
                            s_es[hp][:, :], so[:, 128:256], AF.Copy,
                            scale=s_gam[:, 2 * c.NVH + hh:
                                        2 * c.NVH + hh + 1])),
                        reads=[sok, "gam"], writes=[f"es{hp}"])
                    k.emit("vector", (lambda e, hp=hp, so=so: e.tensor_tensor(
                        s_Otm[hp][:, :], s_es[hp][:, :], so[:, 256:384],
                        ALU.add)),
                        reads=[f"es{hp}", sok], writes=[f"Otm{hp}"])
                    if p2lvl < 1570:
                        continue
                    k.emit("tensor", (lambda e, hp=hp, kp=kp, so=so: e.matmul(
                        so[:, 384:512], s_ktm[kp][:, :], s_Xw[hp][:, :],
                        start=True, stop=True)),
                        reads=[f"ktm{kp}", f"Xw{hp}"], writes=[sok])
                    k.emit("vector", (lambda e, hh=hh, so=so:
                        e.scalar_tensor_tensor(
                            s_S[hh][:, :], s_S[hh][:, :],
                            s_gam[:, c.NVH + hh:c.NVH + hh + 1],
                            so[:, 384:512], ALU.mult, ALU.add)),
                        reads=[f"S{hh}", "gam", sok], writes=[f"S{hh}"])
                    k.emit("scalar", (lambda e, hh=hh: e.activation(
                        s_Sbf[hh][:, :], s_S[hh][:, :], AF.Copy)),
                        reads=[f"S{hh}"], writes=[f"Sbf{hh}"])
                    if p2lvl < 1580:
                        continue
                    k.emit("scalar", (lambda e, hp=hp: e.activation(
                        s_dump[hp][:, :], s_Otm[hp][:, :], AF.Square,
                        accum_out=s_oc2[hp][:, 0:1])),
                        reads=[f"Otm{hp}"], writes=[f"dump{hp}", f"oc2{hp}"])
                    k.emit("scalar", (lambda e, hp=hp: e.activation(
                        s_oc2[hp][:, 1:2], s_oc2[hp][:, 0:1], AF.Sqrt,
                        scale=cst(1.0 / 128), bias=cst(1e-6))),
                        reads=[f"oc2{hp}", "consts"], writes=[f"oc2{hp}"])
                    k.emit("vector", (lambda e, hp=hp: e.reciprocal(
                        s_oc2[hp][:, 1:2], s_oc2[hp][:, 1:2])),
                        reads=[f"oc2{hp}"], writes=[f"oc2{hp}"])
                    k.emit("vector", (lambda e, hh=hh, hp=hp:
                        e.scalar_tensor_tensor(
                            s_Xbf[hp][:, :], s_Otm[hp][:, :],
                            s_oc2[hp][:, 1:2],
                            s_zg[:, 128 * hh:128 * (hh + 1)],
                            ALU.mult, ALU.mult)),
                        reads=[f"Otm{hp}", f"oc2{hp}", "zg"],
                        writes=[f"Xbf{hp}"])
                    k.emit("tensor", (lambda e, hp=hp, tr=tr: e.transpose(
                        tr[:, 0:128], s_Xbf[hp][:, :], s_eyebf[:, :])),
                        reads=[f"Xbf{hp}", "consts"], writes=[trk])
                    opar = ofc[0] % 2
                    ofc[0] += 1
                    k.emit("vector", (lambda e, opar=opar, tr=tr: e.tensor_copy(
                        s_ocb[opar][:, :], tr[:, 0:128])),
                        reads=[trk], writes=[f"ocb{opar}"])
                    k.dma("sync", f"d_of{opar}", (lambda e, hh=hh, tok0=tok0,
                                                  t=t, opar=opar: e.dma_start(
                        out=ofm_sc[hh, :, t * c.TILE + tok0:
                                   t * c.TILE + tok0 + 128],
                        in_=s_ocb[opar][:, :])),
                        reads=[f"ocb{opar}"], writes=["ofm_sc"])
    if upto <= 2:
        k.fence()
        stP2.close()
        return finish_dummy()

    k.fence()
    stP2.close()
    st25 = ExitStack()
    s_wout = [st25.enter_context(
        nc.sbuf_tensor(f"s_wout{v}", [128, c.D], BF16))
        for v in range(c.NVH)]
    s_ofm2 = [[st25.enter_context(
        nc.sbuf_tensor(f"s_ofm2{p}_{v}", [128, c.TILE], BF16))
        for v in range(c.NVH)] for p in range(2)]
    s_sfm = [st25.enter_context(
        nc.sbuf_tensor(f"s_sfm{i}", [128, c.TILE], F32)) for i in range(2)]

    st2 = ExitStack()
    s_rs = st2.enter_context(nc.sbuf_tensor("s_rs", [1, c.TQ], BF16))
    s_gus = st2.enter_context(nc.sbuf_tensor("s_gus", [1, c.TQ], BF16))

    st3 = ExitStack()
    s_gts = st3.enter_context(nc.sbuf_tensor("s_gts", [128, c.TQ], F32))
    s_gud = [st3.enter_context(nc.sbuf_tensor(f"s_gud{i}", [128, c.W4], BF16))
             for i in range(2)]
    s_xnq = [st3.enter_context(nc.sbuf_tensor(f"s_xnq{f}", [128, c.TQ], BF16))
             for f in range(c.NF)]
    NWSL = 3
    s_wgu = [[[st3.enter_context(
        nc.sbuf_tensor(f"s_wgu{p}_{m}_{f}", [128, 512], BF16))
        for f in range(c.NF)] for m in range(2)] for p in range(NWSL)]

    def load_wgu(hb, par):
        for m, wsrc in ((0, wgT), (1, wuT)):
            k.dma("gpsimd", f"d_f{m}{par}", [
                (lambda e, m=m, f=f, hb=hb, par=par, wsrc=wsrc:
                 e.dma_start(
                     out=s_wgu[par][m][f][:, :],
                     in_=wsrc[128 * f:128 * (f + 1),
                              512 * hb:512 * (hb + 1)]))
                for f in range(c.NF)], writes=[f"wgu{par}"])

    k.dma("gpsimd", "d_wo", [
        (lambda e, v=v: e.dma_start(out=s_wout[v][:, :],
                                    in_=wout[128 * v:128 * (v + 1), :]))
        for v in range(c.NVH)], writes=["woutsb"])
    k.dma("sync", "d_xnq", [
        (lambda e, f=f: e.dma_start(
            out=s_xnq[f][:, :],
            in_=xnq_sc[:, 128 * f:128 * (f + 1)], transpose=True))
        for f in range(c.NF)],
        reads=["xnq_sc"], writes=["xnq"])
    for pp_ in range(min(NWSL, c.NHB)):
        load_wgu(pp_, pp_)

    def load_ofm(t, par):
        k.dma("sync", f"d_or{par}", [
            (lambda e, v=v, t=t, par=par: e.dma_start(
                out=s_ofm2[par][v][:, :],
                in_=ofm_sc[v, :, t * c.TILE:(t + 1) * c.TILE]))
            for v in range(c.NVH)],
            reads=["ofm_sc"], writes=[f"ofm2{par}"])

    load_ofm(0, 0)
    if c.NT > 1:
        load_ofm(1, 1)
    for t in range(c.NT):
        par = t % 2
        blk = t // (c.NT // 4)
        colo = (t % (c.NT // 4)) * c.TILE
        for outc in range(c.OUTC):
            opar = outc % 2
            bank, bkey = (PC, "PC") if opar == 0 else (PD, "PD")
            k.emit("tensor", [
                (lambda e, v=v, outc=outc, par=par, bank=bank: e.matmul(
                    bank[:, 0:c.TILE],
                    s_wout[v][:, 128 * outc:128 * (outc + 1)],
                    s_ofm2[par][v][:, :],
                    start=(v == 0), stop=(v == c.NVH - 1)))
                for v in range(c.NVH)],
                reads=["woutsb", f"ofm2{par}"], writes=[bkey])
            k.emit("scalar", (lambda e, bank=bank, opar=opar: e.activation(
                s_sfm[opar][:, 0:c.TILE], bank[:, 0:c.TILE], AF.Copy)),
                reads=[bkey], writes=[f"sfm{opar}"])
            k.dma("sync", f"d_ri{opar}", (lambda e, outc=outc, blk=blk,
                                          colo=colo, opar=opar: e.dma_start(
                out=rsin[blk, 128 * outc:128 * (outc + 1),
                         colo:colo + c.TILE],
                in_=s_sfm[opar][:, 0:c.TILE])),
                reads=[f"sfm{opar}"], writes=["rsin"])
        if t + 2 < c.NT:
            load_ofm(t + 2, par)

    k.emit("gpsimd", (lambda e: e.collective_compute(
        "ReduceScatter", ALU.add,
        replica_groups=[[0, 1, 2, 3], [4, 5, 6, 7]],
        ins=[rsin.ap().opt()], outs=[rsout.ap().opt()])),
        reads=["rsin"], writes=["rsout"])

    if upto <= 3:
        return finish_dummy()

    TQH = c.TQ // c.W4
    GUPB = [(PP0, "PP0"), (PP1, "PP1"), (PA, "PA"), (PB, "PB")]
    for h2 in range(TQH):
        bank, bkey = (PC, "PC") if h2 % 2 == 0 else (PD, "PD")
        k.emit("tensor", [
            (lambda e, f=f, h2=h2, bank=bank: e.matmul(
                bank[0:1, 0:c.W4], s_onesb[:, :],
                s_xnq[f][:, c.W4 * h2:c.W4 * (h2 + 1)],
                start=(f == 0), stop=(f == c.NF - 1)))
            for f in range(c.NF)],
            reads=["xnq", "consts"], writes=[bkey])
        k.emit("vector", (lambda e, h2=h2, bank=bank: e.tensor_copy(
            s_rs[:, c.W4 * h2:c.W4 * (h2 + 1)], bank[0:1, 0:c.W4])),
            reads=[bkey], writes=["rs"])

    for hb in range(c.NHB):
        par = hb % NWSL
        for oc in range(4 if c.HID >= 512 else c.NHC):
            hc = 4 * hb + oc
            for h2 in range(TQH):
                gb, gk = GUPB[2 * (h2 % 2)]
                ub, uk = GUPB[2 * (h2 % 2) + 1]
                k.emit("tensor", [
                    (lambda e, f=f, oc=oc, h2=h2, par=par, gb=gb: e.matmul(
                        gb[:, 0:c.W4],
                        s_wgu[par][0][f][:, 128 * oc:128 * (oc + 1)],
                        s_xnq[f][:, c.W4 * h2:c.W4 * (h2 + 1)],
                        start=(f == 0), stop=False))
                    for f in range(c.NF)] + [
                    (lambda e, h2=h2, gb=gb: e.matmul(
                        gb[:, 0:c.W4], s_m128[:, :],
                        s_rs[:, c.W4 * h2:c.W4 * (h2 + 1)],
                        start=False, stop=True))],
                    reads=[f"wgu{par}", "xnq", "rs", "consts"], writes=[gk])
                k.emit("tensor", [
                    (lambda e, f=f, oc=oc, h2=h2, par=par, ub=ub: e.matmul(
                        ub[:, 0:c.W4],
                        s_wgu[par][1][f][:, 128 * oc:128 * (oc + 1)],
                        s_xnq[f][:, c.W4 * h2:c.W4 * (h2 + 1)],
                        start=(f == 0), stop=False))
                    for f in range(c.NF)] + [
                    (lambda e, h2=h2, ub=ub: e.matmul(
                        ub[:, 0:c.W4], s_m128[:, :],
                        s_rs[:, c.W4 * h2:c.W4 * (h2 + 1)],
                        start=False, stop=True))],
                    reads=[f"wgu{par}", "xnq", "rs", "consts"], writes=[uk])
                k.emit("scalar", (lambda e, hc=hc, h2=h2, gb=gb: e.activation(
                    s_gts[:, c.W4 * h2:c.W4 * (h2 + 1)], gb[:, 0:c.W4],
                    AF.Silu, scale=s_sg2[:, hc:hc + 1])),
                    reads=[gk, "consts"], writes=["gts"])
                gpar = (hc * TQH + h2) % 2
                k.emit("vector", (lambda e, hc=hc, h2=h2, ub=ub, gpar=gpar:
                    e.scalar_tensor_tensor(
                        s_gud[gpar][:, :],
                        ub[:, 0:c.W4], s_su2[:, hc:hc + 1],
                        s_gts[:, c.W4 * h2:c.W4 * (h2 + 1)],
                        ALU.mult, ALU.mult)),
                    reads=[uk, "consts", "gts"], writes=[f"gud{gpar}"])
                gusb, gusk = (PC, "PC") if h2 % 2 == 0 else (PD, "PD")
                k.emit("tensor", (lambda e, gpar=gpar, hc=hc, gusb=gusb:
                    e.matmul(
                        gusb[0:1, 0:c.W4], s_onesb[:, :], s_gud[gpar][:, :],
                        start=(hc == 0), stop=(hc == c.NHC - 1))),
                    reads=[f"gud{gpar}", "consts"], writes=[gusk])
                k.dma("sync", f"d_gw{gpar}", (lambda e, gpar=gpar, hc=hc,
                                              h2=h2: e.dma_start(
                    out=gu_sc[128 * hc:128 * (hc + 1),
                              c.W4 * h2:c.W4 * (h2 + 1)],
                    in_=s_gud[gpar][:, :])),
                    reads=[f"gud{gpar}"], writes=["gu_sc"])
        if hb + NWSL < c.NHB:
            load_wgu(hb + NWSL, par)
    for h2 in range(TQH):
        bank, bkey = (PC, "PC") if h2 % 2 == 0 else (PD, "PD")
        k.emit("vector", (lambda e, h2=h2, bank=bank: e.tensor_copy(
            s_gus[:, c.W4 * h2:c.W4 * (h2 + 1)], bank[0:1, 0:c.W4])),
            reads=[bkey], writes=["gus"])
    k.fence()
    st3.close()

    if upto <= 4:
        return finish_dummy()
    st4 = ExitStack()
    s_acc = st4.enter_context(
        nc.sbuf_tensor("s_acc", [128, c.OUTC * c.TQ], F32))
    s_wd = [[st4.enter_context(
        nc.sbuf_tensor(f"s_wd{p}_{j}", [128, c.D], BF16))
        for j in range(c.GSZ)] for p in range(2)]
    s_gur = [[st4.enter_context(
        nc.sbuf_tensor(f"s_gur{p}_{j}", [128, c.TQ], BF16))
        for j in range(c.GSZ)] for p in range(2)]
    s_fin = [st4.enter_context(
        nc.sbuf_tensor(f"s_fin{i}", [128, c.TQ], F32)) for i in range(4)]

    def load_wd(g, par):
        hcs = list(range(c.GSZ * g, min(c.GSZ * (g + 1), c.NHC)))
        k.dma("gpsimd", f"d_wd{par}", [
            (lambda e, j=j, hc=hc, par=par: e.dma_start(
                out=s_wd[par][j][:, :], in_=wdT[128 * hc:128 * (hc + 1), :]))
            for j, hc in enumerate(hcs)], writes=[f"wd{par}"])
        k.dma("sync", f"d_gr{par}", [
            (lambda e, j=j, hc=hc, par=par: e.dma_start(
                out=s_gur[par][j][:, :],
                in_=gu_sc[128 * hc:128 * (hc + 1), :]))
            for j, hc in enumerate(hcs)],
            reads=["gu_sc"], writes=[f"gur{par}"])

    load_wd(0, 0)
    for g in range(c.NGRP):
        par = g % 2
        if g + 1 < c.NGRP:
            load_wd(g + 1, 1 - par)
        hcs = list(range(c.GSZ * g, min(c.GSZ * (g + 1), c.NHC)))
        for outc in range(c.OUTC):
            for h2 in range(TQH):
                bank, bkey = GUPB[2 * (outc % 2) + h2 % 2]
                fns = [
                    (lambda e, j=j, hc=hc, outc=outc, h2=h2, par=par,
                     bank=bank, hn=len(hcs), g=g: e.matmul(
                        bank[:, 0:c.W4],
                        s_wd[par][j][:, 128 * outc:128 * (outc + 1)],
                        s_gur[par][j][:, c.W4 * h2:c.W4 * (h2 + 1)],
                        start=(j == 0),
                        stop=(j == hn - 1 and g != c.NGRP - 1)))
                    for j, hc in enumerate(hcs)]
                if g == c.NGRP - 1:
                    fns.append(lambda e, h2=h2, bank=bank: e.matmul(
                        bank[:, 0:c.W4], s_m128[:, :],
                        s_gus[:, c.W4 * h2:c.W4 * (h2 + 1)],
                        start=False, stop=True))
                k.emit("tensor", fns,
                       reads=[f"wd{par}", f"gur{par}", "gus", "consts"],
                       writes=[bkey])
                ao = outc * c.TQ + c.W4 * h2
                if g == 0:
                    k.emit("vector", (lambda e, ao=ao, bank=bank: e.tensor_copy(
                        s_acc[:, ao:ao + c.W4], bank[:, 0:c.W4])),
                        reads=[bkey], writes=[f"acc{outc}"])
                else:
                    k.emit("vector", (lambda e, ao=ao, bank=bank, outc=outc:
                        e.tensor_tensor(
                            s_acc[:, ao:ao + c.W4], bank[:, 0:c.W4],
                            s_acc[:, ao:ao + c.W4], ALU.add)),
                        reads=[bkey, f"acc{outc}"], writes=[f"acc{outc}"])

    for outc in range(c.OUTC):
        k.dma("sync", "d_ro", (lambda e, outc=outc: e.dma_start(
            out=s_fin[0][:, :], in_=rsout[128 * outc:128 * (outc + 1), :])),
            reads=["rsout"], writes=["fin0"])
        k.dma("sync", "d_xt", (lambda e, outc=outc: e.dma_start(
            out=s_fin[1][:, :], in_=xqT[128 * outc:128 * (outc + 1), :])),
            writes=["fin1"])
        k.emit("scalar", (lambda e, outc=outc: e.activation(
            s_fin[2][:, :], s_acc[:, outc * c.TQ:(outc + 1) * c.TQ], AF.Silu,
            scale=s_sd2[:, outc:outc + 1])),
            reads=[f"acc{outc}", "consts"], writes=["fin2"])
        k.emit("vector", lambda e: e.tensor_tensor(
            s_fin[3][:, :], s_fin[0][:, :], s_fin[2][:, :], ALU.mult),
            reads=["fin0", "fin2"], writes=["fin3"])
        k.emit("vector", lambda e: e.tensor_tensor(
            s_fin[3][:, :], s_fin[3][:, :], s_fin[1][:, :], ALU.add),
            reads=["fin3", "fin1"], writes=["fin3"])
        k.dma("sync", "d_out", (lambda e, outc=outc: e.dma_start(
            out=outfm[128 * outc:128 * (outc + 1), :], in_=s_fin[3][:, :])),
            reads=["fin3"], writes=[f"out{outc}"])

    k._wait("gpsimd", "d_out", k.count["d_out"])

    semh = {}
    for name in sorted(set(list(k.count.keys()) + list(K.ESEM.values()))):
        semh[name] = st.enter_context(nc.semaphore(name))
    k.replay(nc, semh)
    st4.close()
    st2.close()
    st25.close()
    st.close()
    return nc


def stage(cfg, A):
    c = cfg
    x = np.ascontiguousarray(np.asarray(A["L0_attn_hidden_states"], np.float32))
    Wqkv = np.asarray(A["L0_attn_W_qkv"], np.float32)
    Wz = np.asarray(A["L0_attn_W_z"], np.float32)
    Wb = np.asarray(A["L0_attn_W_b"], np.float32)
    Wa = np.asarray(A["L0_attn_W_a"], np.float32)
    convw = np.asarray(A["L0_attn_conv_w"], np.float32)
    dtb = np.asarray(A["L0_attn_dt_bias"], np.float32)
    alog = np.asarray(A["L0_attn_A_log"], np.float32)
    normw = np.asarray(A["L0_attn_norm_w"], np.float32)
    Wout = np.asarray(A["L0_attn_W_out"], np.float32)
    wqg = np.asarray(A["L0_ffn_wq_gate"], np.int32)
    sg = np.asarray(A["L0_ffn_s_gate"], np.float32)
    wqu = np.asarray(A["L0_ffn_wq_up"], np.int32)
    su = np.asarray(A["L0_ffn_s_up"], np.float32)
    wqd = np.asarray(A["L0_ffn_wq_down"], np.int32)
    sd = np.asarray(A["L0_ffn_s_down"], np.float32)

    KD = 4 * c.QW
    wgT = np.ascontiguousarray(wqg.T)
    wuT = np.ascontiguousarray(wqu.T)
    wdT = np.ascontiguousarray(wqd.T)
    sg2 = np.ascontiguousarray(sg.reshape(c.NHC, 128).T)
    su2 = np.ascontiguousarray(su.reshape(c.NHC, 128).T)
    sd2 = np.ascontiguousarray(sd.reshape(c.OUTC, 128).T)
    idx = np.arange(128)
    consts = dict(
        eyebf=np.eye(128, dtype=BF),
        eyef=np.eye(128, dtype=np.float32),
        ltri=np.triu(np.ones((128, 128), np.float32)),
        maskE=np.where(idx[None, :] >= idx[:, None], 0.0, -1e30).astype(np.float32),
        mask2=np.where(idx[None, :] < idx[:, None], 0.0, 1e30).astype(np.float32),
        mask01=np.triu(np.ones((128, 128), np.float32), 1),
        rowsel=np.ascontiguousarray(np.repeat(
            np.eye(cfg.NVH, dtype=np.float32), 128, axis=1)),
        normw4=np.ascontiguousarray(np.tile(normw, 512 // normw.shape[0])[None, :]).astype(np.float32),
    )
    in_maps = []
    for core in range(8):
        b, hg = core // 4, core % 4
        qsl = slice(c.QW * hg, c.QW * (hg + 1))
        ksl = slice(KD + c.QW * hg, KD + c.QW * (hg + 1))
        vsl = slice(2 * KD + c.VW * hg, 2 * KD + c.VW * (hg + 1))
        zsl = slice(c.VW * hg, c.VW * (hg + 1))
        hsl = slice(c.NVH * hg, c.NVH * (hg + 1))
        wg = np.concatenate(
            [Wqkv[:, qsl], Wqkv[:, ksl], Wqkv[:, vsl]], axis=1)
        wgc = np.ascontiguousarray(wg.reshape(c.NF, 128, c.WC))
        wzc = np.ascontiguousarray(
            Wz[:, zsl].reshape(c.NF, 128, c.NZC, c.ZW).transpose(0, 2, 1, 3))
        wab = np.concatenate([Wa[:, hsl], Wb[:, hsl]], axis=1)
        wabc = np.ascontiguousarray(wab.reshape(c.NF, 128, c.ABW))
        cw = np.concatenate([convw[qsl], convw[ksl], convw[vsl]], axis=0)
        xb = x[b]
        m = dict(
            xfull=xb,
            xq=np.ascontiguousarray(xb[c.TQ * hg:c.TQ * (hg + 1)]),
            xqT=np.ascontiguousarray(xb[c.TQ * hg:c.TQ * (hg + 1)].T),
            wgc=wgc, wzc=wzc, wabc=wabc,
            convw=np.ascontiguousarray(cw),
            alog=np.ascontiguousarray(alog[hsl][:, None]),
            dtb=np.ascontiguousarray(dtb[hsl][:, None]),
            wout=np.ascontiguousarray(Wout[c.VW * hg:c.VW * (hg + 1)]),
            wgT=wgT, wuT=wuT, wdT=wdT, sg2=sg2, su2=su2, sd2=sd2,
            **consts)
        in_maps.append(m)
    return in_maps


def assemble(cfg, outs):
    c = cfg
    out = np.empty((2, c.T, c.D), np.float32)
    for core in range(8):
        b, hg = core // 4, core % 4
        out[b, c.TQ * hg:c.TQ * (hg + 1), :] = outs[core].T
    return out


def kernel(**inputs):
    cfg = REAL
    in_maps = stage(cfg, inputs)
    nc = build(cfg)
    res = run_bass_kernel_spmd(nc, in_maps, core_ids=list(range(8)))
    return assemble(cfg, [np.asarray(r["outfm"]) for r in res.results])


# version 08
# speedup vs baseline: 1.3066x; 1.3066x over previous
import numpy as np
import ml_dtypes
import concourse.bass as bass
import concourse.mybir as mybir
from concourse.bass_utils import run_bass_kernel_spmd
from contextlib import ExitStack

F32 = mybir.dt.float32
BF16 = mybir.dt.bfloat16
I32 = mybir.dt.int32
AF = mybir.ActivationFunctionType
ALU = mybir.AluOpType
AX = mybir.AxisListType
BF = ml_dtypes.bfloat16


class Cfg:
    def __init__(self, T, TILE, D, NKH, HID):
        self.T, self.TILE, self.D, self.NKH, self.HID = T, TILE, D, NKH, HID
        self.NVH = 2 * NKH
        self.NF = D // 128
        self.NT = T // TILE
        self.NCH = TILE // 128
        self.TQ = T // 4
        self.QW = NKH * 128
        self.VW = self.NVH * 128
        self.ABW = 2 * self.NVH
        self.WC = 2 * self.QW + self.VW
        self.WCOLS = self.WC + self.VW + self.ABW
        self.OUTC = D // 128
        self.NHC = HID // 128
        self.NHB = HID // 512
        self.GSZ = 4
        self.NGRP = (self.NHC + self.GSZ - 1) // self.GSZ
        self.NLEV = 4
        self.NQB = self.TQ // 128
        self.NTB = T // 128
        self.W2 = min(512, TILE)
        self.W4 = min(512, self.TQ)
        self.NZC = self.VW // 512 if self.VW >= 512 else 0
        self.ZW = 512 if self.VW >= 512 else self.VW
        if self.NZC == 0:
            self.NZC = 1


REAL = Cfg(T=4096, TILE=512, D=2048, NKH=4, HID=5632)
TINY = Cfg(T=512, TILE=128, D=256, NKH=2, HID=512)


class K:
    ENGS = ("tensor", "vector", "scalar", "gpsimd", "sync")
    ESEM = {"tensor": "s_pe", "vector": "s_dve", "scalar": "s_act",
            "gpsimd": "s_gp", "sync": "s_sp"}

    def __init__(self):
        self.streams = {e: [] for e in K.ENGS}
        self.count = {}
        self.mark = {e: {} for e in K.ENGS}
        self.writer = {}
        self.readers = {}
        self.dma_pend = {}
        self.sem_names = set(K.ESEM.values())

    def _wait(self, eng, sem, val):
        if val <= 0:
            return
        if self.mark[eng].get(sem, 0) >= val:
            return
        self.mark[eng][sem] = val
        self.streams[eng].append(("w", sem, val))

    def _deps(self, eng, reads, writes):
        for k in reads:
            for s, v in self.writer.get(k, {}).items():
                self._wait(eng, s, v)
        for k in writes:
            for s, v in self.writer.get(k, {}).items():
                self._wait(eng, s, v)
            for s, v in self.readers.get(k, {}).items():
                self._wait(eng, s, v)

    def _post(self, sem, v, reads, writes):
        for k in writes:
            self.writer.setdefault(k, {})[sem] = v
            self.readers[k] = {}
        for k in reads:
            self.readers.setdefault(k, {})[sem] = v

    PSUM_KEYS = frozenset(
        ["PP0", "PP1", "PA", "PB", "PC", "PD", "PF0", "PF1"])

    def _excl(self, reads, writes):
        pr = [k for k in reads if k in K.PSUM_KEYS]
        if pr:
            reads = [k for k in reads if k not in K.PSUM_KEYS]
            writes = list(writes) + pr
        return reads, writes

    def emit(self, eng, fns, reads=(), writes=()):
        if not isinstance(fns, (list, tuple)):
            fns = [fns]
        reads, writes = self._excl(reads, writes)
        self._deps(eng, reads, writes)
        sem = K.ESEM[eng]
        self.count[sem] = self.count.get(sem, 0) + 1
        v = self.count[sem]
        for i, fn in enumerate(fns):
            self.streams[eng].append(
                ("i", fn, sem if i == len(fns) - 1 else None, 1))
        self._post(sem, v, reads, writes)
        return v

    def dma(self, eng, sem, fns, reads=(), writes=()):
        if not isinstance(fns, (list, tuple)):
            fns = [fns]
        reads, writes = self._excl(reads, writes)
        self.sem_names.add(sem)
        if self.dma_pend.get(sem):
            self._wait(eng, sem, self.count.get(sem, 0))
        self._deps(eng, reads, writes)
        for fn in fns:
            self.count[sem] = self.count.get(sem, 0) + 16
            self.streams[eng].append(("i", fn, sem, 16))
        v = self.count[sem]
        self.dma_pend[sem] = True
        self._post(sem, v, reads, writes)
        return v

    def fence(self):
        for eng in K.ENGS:
            own = K.ESEM[eng]
            for sem, cnt in list(self.count.items()):
                if sem != own:
                    self._wait(eng, sem, cnt)

    def replay(self, nc, semh):
        with nc.Block() as block:
            for name in K.ENGS:
                stream = self.streams[name]
                if not stream:
                    continue

                def body(eng, stream=stream):
                    for it in stream:
                        if it[0] == "w":
                            eng.wait_ge(semh[it[1]], it[2])
                        else:
                            bi = it[1](eng)
                            if it[2] is not None:
                                bi.then_inc(semh[it[2]], it[3])
                getattr(block, name)(body)


def build(cfg, upto=5, p2lvl=10**9):
    c = cfg
    nc = bass.Bass("TRN2", target_bir_lowering=False, debug=False)
    k = K()

    dp = nc.declare_dram_parameter
    xfull = dp("xfull", [c.T, c.D], F32, isOutput=False)
    xq = dp("xq", [c.TQ, c.D], F32, isOutput=False)
    xqT = dp("xqT", [c.D, c.TQ], F32, isOutput=False)
    wgc = dp("wgc", [c.NF, 128, c.WC], F32, isOutput=False)
    wzc = dp("wzc", [c.NF, c.NZC, 128, c.ZW], F32, isOutput=False)
    wabc = dp("wabc", [c.NF, 128, c.ABW], F32, isOutput=False)
    convw = dp("convw", [c.WC, 4], F32, isOutput=False)
    alog = dp("alog", [c.NVH, 1], F32, isOutput=False)
    dtb = dp("dtb", [c.NVH, 1], F32, isOutput=False)
    normw4 = dp("normw4", [1, 512], F32, isOutput=False)
    wout = dp("wout", [c.VW, c.D], F32, isOutput=False)
    wgT = dp("wgT", [c.D, c.HID], I32, isOutput=False)
    wuT = dp("wuT", [c.D, c.HID], I32, isOutput=False)
    wdT = dp("wdT", [c.HID, c.D], I32, isOutput=False)
    sg2 = dp("sg2", [128, c.NHC], F32, isOutput=False)
    su2 = dp("su2", [128, c.NHC], F32, isOutput=False)
    sd2 = dp("sd2", [128, c.OUTC], F32, isOutput=False)
    eyebf = dp("eyebf", [128, 128], BF16, isOutput=False)
    eyef = dp("eyef", [128, 128], F32, isOutput=False)
    ltri = dp("ltri", [128, 128], F32, isOutput=False)
    maskE = dp("maskE", [128, 128], F32, isOutput=False)
    mask2 = dp("mask2", [128, 128], F32, isOutput=False)
    mask01 = dp("mask01", [128, 128], F32, isOutput=False)
    rowsel = dp("rowsel", [c.NVH, 128 * c.NVH], F32, isOutput=False)
    outfm = dp("outfm", [c.D, c.TQ], F32, isOutput=True)

    xn_sc = nc.dram_tensor("xn_sc", [c.T, c.D], BF16)
    xnq_sc = nc.dram_tensor("xnq_sc", [c.TQ, c.D], BF16)
    ofm_sc = nc.dram_tensor("ofm_sc", [c.NVH, 128, c.T], BF16)
    rsin = nc.dram_tensor("rsin", [4, c.D, c.TQ], F32)
    gu_sc = nc.dram_tensor("gu_sc", [c.NHC * 128, c.TQ], BF16)
    rsout = nc.dram_tensor("rsout", [c.D, c.TQ], F32)

    st = ExitStack()
    sb = lambda n, s, d: st.enter_context(nc.sbuf_tensor(n, s, d))
    ps = lambda n, s, d: st.enter_context(nc.psum_tensor(n, s, d))

    PP0 = ps("PP0", [128, 512], F32)
    PP1 = ps("PP1", [128, 512], F32)
    PA = ps("PA", [128, 512], F32)
    PB = ps("PB", [128, 512], F32)
    PC = ps("PC", [128, 512], F32)
    PD = ps("PD", [128, 512], F32)
    PF0 = ps("PF0", [128, 1024], BF16)
    PF1 = ps("PF1", [128, 1024], BF16)

    s_eyebf = sb("s_eyebf", [128, 128], BF16)
    s_eyef = sb("s_eyef", [128, 128], F32)
    s_ltri = sb("s_ltri", [128, 128], F32)
    s_maskE = sb("s_maskE", [128, 128], F32)
    s_mask2 = sb("s_mask2", [128, 128], F32)
    s_mask01 = sb("s_mask01", [128, 128], F32)
    s_normw = sb("s_normw", [1, 512], F32)
    s_alog = sb("s_alog", [c.NVH, 1], F32)
    s_dtb = sb("s_dtb", [c.NVH, 1], F32)
    s_nega = sb("s_nega", [c.NVH, 1], F32)
    s_cst = sb("s_cst", [128, 8], F32)
    s_ones1 = sb("s_ones1", [1, 128], F32)
    s_onesc = sb("s_onesc", [128, 1], F32)
    s_onesb = sb("s_onesb", [128, 1], BF16)
    s_m128 = sb("s_m128", [1, 128], BF16)
    s_normb = sb("s_normb", [128, 512], F32)
    s_onesq = sb("s_onesq", [128, 128], F32)
    s_rsel = sb("s_rsel", [c.NVH, 128 * c.NVH], F32)
    s_convw = sb("s_convw", [128, 4 * (c.WC // 128)], F32)
    s_sg2 = sb("s_sg2", [128, c.NHC], F32)
    s_su2 = sb("s_su2", [128, c.NHC], F32)
    s_sd2 = sb("s_sd2", [128, c.OUTC], F32)

    CST = {1e-6: 0, 128.0: 1, 128e-6: 2, 1.0 / c.D: 3, 1e-5: 4,
           1.0 / 128: 5, 1.0: 6, -1.0: 7}

    def cst(v, p=128):
        return s_cst[0:p, CST[v]:CST[v] + 1]

    k.dma("sync", "d_c", [
        (lambda e, o=o_, i=i_: e.dma_start(out=o, in_=i))
        for o_, i_ in [
            (s_eyebf[:, :], eyebf[:, :]), (s_eyef[:, :], eyef[:, :]),
            (s_ltri[:, :], ltri[:, :]), (s_maskE[:, :], maskE[:, :]),
            (s_mask2[:, :], mask2[:, :]), (s_mask01[:, :], mask01[:, :]),
            (s_normw[:, :], normw4[:, :]), (s_alog[:, :], alog[:, :]),
            (s_dtb[:, :], dtb[:, :]), (s_sg2[:, :], sg2[:, :]),
            (s_su2[:, :], su2[:, :]), (s_sd2[:, :], sd2[:, :]),
            (s_rsel[:, :], rowsel[:, :]),
        ] + [(s_convw[:, 4 * f:4 * f + 4], convw[128 * f:128 * (f + 1), :])
             for f in range(c.WC // 128)]
    ], writes=["consts"])
    for val, col in CST.items():
        k.emit("vector", (lambda e, v=val, cl=col: e.memset(
            s_cst[:, cl:cl + 1], v)), writes=["consts"])
    k.emit("vector", lambda e: e.memset(s_ones1[:, :], 1.0), writes=["consts"])
    k.emit("vector", lambda e: e.memset(s_onesc[:, :], 1.0), writes=["consts"])
    k.emit("vector", lambda e: e.memset(s_onesb[:, :], 1.0), writes=["consts"])
    k.emit("vector", lambda e: e.memset(s_m128[:, :], -128.0), writes=["consts"])
    k.emit("vector", lambda e: e.memset(s_onesq[:, :], 1.0), writes=["consts"])
    k.emit("scalar", lambda e: e.activation(s_nega[:, :], s_alog[:, :], AF.Exp),
           reads=["consts"], writes=["nega"])
    k.emit("vector", lambda e: e.tensor_scalar(
        s_nega[:, :], s_nega[:, :], -1.0, None, ALU.mult),
        reads=["nega"], writes=["nega"])
    k.emit("tensor", lambda e: e.matmul(
        PA[:, 0:512], s_ones1[:, :], s_normw[:, :], start=True, stop=True),
        reads=["consts"], writes=["PA"])
    k.emit("vector", lambda e: e.tensor_copy(s_normb[:, :], PA[:, 0:512]),
           reads=["PA"], writes=["normb"])

    stP1 = ExitStack()
    xl = [stP1.enter_context(nc.sbuf_tensor(f"xln{i}", [128, c.D], F32))
          for i in range(2)]
    xb = [stP1.enter_context(nc.sbuf_tensor(f"xnb{i}", [128, c.D], BF16))
          for i in range(2)]
    dumpb = stP1.enter_context(nc.sbuf_tensor("dumpb", [128, c.D], F32))
    lncol = stP1.enter_context(nc.sbuf_tensor("lncol", [128, 4], F32))

    def ln_block(src_ap, dst_ap, p, keys):
        xk, bk = f"xl{p}", f"xb{p}"
        xlp, xbp = xl[p], xb[p]
        k.dma("sync", f"d_x{p}", (lambda e: e.dma_start(
            out=xlp[:, :], in_=src_ap)), writes=[xk])
        k.emit("vector", lambda e: e.reduce_sum(
            lncol[:, 0:1], xlp[:, :], axis=AX.X),
            reads=[xk], writes=["lncol"])
        k.emit("scalar", lambda e: e.activation(
            lncol[:, 1:2], lncol[:, 0:1], AF.Copy, scale=cst(1.0 / c.D)),
            reads=["lncol", "consts"], writes=["lncol"])
        k.emit("vector", lambda e: e.tensor_scalar(
            xlp[:, :], xlp[:, :], lncol[:, 1:2], None, ALU.subtract),
            reads=[xk, "lncol"], writes=[xk])
        k.emit("scalar", lambda e: e.activation(
            dumpb[:, :], xlp[:, :], AF.Square, accum_out=lncol[:, 2:3]),
            reads=[xk], writes=["dumpb", "lncol"])
        k.emit("scalar", lambda e: e.activation(
            lncol[:, 3:4], lncol[:, 2:3], AF.Sqrt,
            scale=cst(1.0 / c.D), bias=cst(1e-5)),
            reads=["lncol", "consts"], writes=["lncol"])
        k.emit("vector", lambda e: e.reciprocal(
            lncol[:, 3:4], lncol[:, 3:4]),
            reads=["lncol"], writes=["lncol"])
        k.emit("vector", lambda e: e.tensor_scalar(
            xbp[:, :], xlp[:, :], lncol[:, 3:4], None, ALU.mult),
            reads=[xk, "lncol"], writes=[bk])
        k.dma("sync", f"d_xw{p}", (lambda e: e.dma_start(
            out=dst_ap, in_=xbp[:, :])), reads=[bk], writes=keys)

    for tb in range(c.NTB):
        ln_block(xfull[128 * tb:128 * (tb + 1), :],
                 xn_sc[128 * tb:128 * (tb + 1), :], tb % 2, ["xn_sc"])
    for qb in range(c.NQB):
        ln_block(xq[128 * qb:128 * (qb + 1), :],
                 xnq_sc[128 * qb:128 * (qb + 1), :], qb % 2, ["xnq_sc"])
    k.fence()
    stP1.close()

    def finish_dummy():
        sfin = st.enter_context(nc.sbuf_tensor("s_dummy", [128, c.TQ], F32))
        k.emit("vector", lambda e: e.memset(sfin[:, :], 0.0), writes=["dumz"])
        k.dma("sync", "d_out", [
            (lambda e, o=o: e.dma_start(
                out=outfm[128 * o:128 * (o + 1), :], in_=sfin[:, :]))
            for o in range(c.OUTC)], reads=["dumz"], writes=["outz"])
        k._wait("gpsimd", "d_out", k.count["d_out"])
        semh = {}
        for name in sorted(set(list(k.count.keys()) + list(K.ESEM.values()))):
            semh[name] = st.enter_context(nc.semaphore(name))
        k.replay(nc, semh)
        st.close()
        return nc

    if upto <= 1:
        return finish_dummy()

    QKV_OC = c.WC // 128
    NWG = (QKV_OC + 3) // 4

    stP2 = ExitStack()
    sbp = lambda n, s, d: stP2.enter_context(nc.sbuf_tensor(n, s, d))
    s_xnf = [sbp(f"s_xnf{f}", [128, c.TILE], BF16) for f in range(c.NF)]
    s_wgF = [sbp(f"s_wgF{f}", [128, c.WC], BF16) for f in range(c.NF)]
    s_wz = [[sbp(f"s_wz{z}_{f}", [128, c.ZW], BF16) for f in range(c.NF)]
            for z in range(c.NZC)]
    s_wab = [sbp(f"s_wab{f}", [128, c.ABW], BF16) for f in range(c.NF)]
    s_projb = sbp("s_projb", [128, QKV_OC * (c.TILE + 3)], BF16)
    s_qkvs = sbp("s_qkvs", [128, QKV_OC * c.TILE], BF16)
    s_cacc = sbp("s_cacc", [128, c.TILE], F32)
    s_zg = sbp("s_zg", [128, c.VW], F32)
    s_gc = sbp("s_gc", [c.NVH, c.TILE], F32)
    s_bc = sbp("s_bc", [c.NVH, c.TILE], F32)
    s_gtm = sbp("s_gtm", [128, c.NVH], F32)
    s_btm = sbp("s_btm", [128, c.NVH], F32)
    s_G = sbp("s_G", [128, c.NVH], F32)
    s_Gfm = sbp("s_Gfm", [c.NVH, 128], F32)
    s_gam = sbp("s_gam", [128, 4 * c.NVH], F32)
    s_nbf = sbp("s_nbf", [c.NVH, c.TILE], F32)
    s_sks = [sbp(f"s_sks{i}", [1, 256], F32) for i in range(2)]
    s_ktf = [sbp(f"s_ktf{i}", [128, 128], BF16) for i in range(2)]
    s_qtf = [sbp(f"s_qtf{i}", [128, 128], BF16) for i in range(2)]
    s_ktm = [sbp(f"s_ktm{i}", [128, 128], BF16) for i in range(2)]
    s_ksq = [sbp(f"s_ksq{i}", [128, 256], F32) for i in range(2)]
    s_kk = [sbp(f"s_kk{i}", [128, 256], F32) for i in range(2)]
    s_gbc = sbp("s_gbc", [128, 128 * c.NVH], F32)
    s_bbc = sbp("s_bbc", [128, 128 * c.NVH], F32)
    s_argE = [sbp(f"s_argE{i}", [128, 128], F32) for i in range(2)]
    s_argN = [sbp(f"s_argN{i}", [128, 128], F32) for i in range(2)]
    s_E = [sbp(f"s_E{i}", [128, 128], F32) for i in range(2)]
    s_E2 = [sbp(f"s_E2{i}", [128, 128], F32) for i in range(2)]
    s_es = [sbp(f"s_es{i}", [128, 128], F32) for i in range(2)]
    s_ntm = [sbp(f"s_ntm{i}", [128, 128], F32) for i in range(2)]
    s_NT = [sbp(f"s_NT{i}", [128, 128], BF16) for i in range(2)]
    s_N = [sbp(f"s_N{i}", [128, 128], BF16) for i in range(2)]
    s_TT = [[sbp(f"s_TT{i}_{j}", [128, 128], BF16) for j in range(2)]
            for i in range(2)]
    s_P = [[sbp(f"s_P{i}_{j}", [128, 128], BF16) for j in range(2)]
           for i in range(2)]
    s_PT = [[sbp(f"s_PT{i}_{j}", [128, 128], BF16) for j in range(2)]
            for i in range(2)]
    s_A2T = [sbp(f"s_A2T{i}", [128, 128], BF16) for i in range(2)]
    s_RHS = [sbp(f"s_RHS{i}", [128, 128], BF16) for i in range(2)]
    s_Xbf = [sbp(f"s_Xbf{i}", [128, 128], BF16) for i in range(2)]
    s_Xw = [sbp(f"s_Xw{i}", [128, 128], BF16) for i in range(2)]
    s_Otm = [sbp(f"s_Otm{i}", [128, 128], F32) for i in range(2)]
    s_dump = [sbp(f"s_dump{i}", [128, 128], F32) for i in range(2)]
    s_oc2 = [sbp(f"s_oc2{i}", [128, 2], F32) for i in range(2)]
    s_S = [sbp(f"s_S{h}", [128, 128], F32) for h in range(c.NVH)]
    s_Sbf = [sbp(f"s_Sbf{h}", [128, 128], BF16) for h in range(c.NVH)]
    s_ocb = [sbp(f"s_ocb{i}", [128, 128], BF16) for i in range(2)]
    ofc = [0]

    for h in range(c.NVH):
        k.emit("vector", (lambda e, h=h: e.memset(s_S[h][:, :], 0.0)),
               writes=[f"S{h}"])
        k.emit("vector", (lambda e, h=h: e.memset(s_Sbf[h][:, :], 0.0)),
               writes=[f"Sbf{h}"])
    for oc in range(QKV_OC):
        k.emit("vector", (lambda e, oc=oc: e.memset(
            s_projb[:, oc * (c.TILE + 3):oc * (c.TILE + 3) + 3], 0.0)),
            writes=[f"projb{oc}"])

    def load_xnf(t):
        k.dma("sync", "d_xnr", [
            (lambda e, f=f, t=t: e.dma_start(
                out=s_xnf[f][:, :],
                in_=xn_sc[t * c.TILE:(t + 1) * c.TILE, 128 * f:128 * (f + 1)],
                transpose=True))
            for f in range(c.NF)],
            reads=["xn_sc"], writes=[f"xnf{f}" for f in range(c.NF)])

    def load_wg_once():
        k.dma("gpsimd", "d_wq", [
            (lambda e, f=f: e.dma_start(out=s_wgF[f][:, :], in_=wgc[f]))
            for f in range(c.NF)], writes=["wgF"])

    def load_wz(z):
        k.dma("gpsimd", f"d_wz{z}", [
            (lambda e, f=f, z=z: e.dma_start(
                out=s_wz[z][f][:, :], in_=wzc[f, z]))
            for f in range(c.NF)], writes=[f"wz{z}"])

    def load_wab():
        k.dma("gpsimd", "d_wab", [
            (lambda e, f=f: e.dma_start(
                out=s_wab[f][:, :], in_=wabc[f]))
            for f in range(c.NF)], writes=["wab"])

    PBANK = [PP0, PP1, PA, PB]
    PBK = ["PP0", "PP1", "PA", "PB"]
    xnf_keys = [f"xnf{f}" for f in range(c.NF)]

    load_xnf(0)
    load_wg_once()
    for z in range(c.NZC):
        load_wz(z)
    load_wab()

    for t in range(c.NT):
        for g in range(NWG):
            ocs = list(range(4 * g, min(4 * g + 4, QKV_OC)))
            for j, oc in enumerate(ocs):
                bank, bkey = PBANK[j], PBK[j]
                ob = oc * (c.TILE + 3)
                k.emit("tensor", [
                    (lambda e, f=f, oc=oc, bank=bank: e.matmul(
                        bank[:, 0:c.TILE],
                        s_wgF[f][:, 128 * oc:128 * (oc + 1)],
                        s_xnf[f][:, :],
                        start=(f == 0), stop=(f == c.NF - 1)))
                    for f in range(c.NF)],
                    reads=["wgF"] + xnf_keys, writes=[bkey])
                k.emit("scalar", (lambda e, bank=bank, ob=ob: e.activation(
                    s_projb[:, ob + 3:ob + 3 + c.TILE], bank[:, 0:c.TILE],
                    AF.Copy)),
                    reads=[bkey], writes=[f"projb{oc}"])
                k.emit("vector", (lambda e, oc=oc, ob=ob: e.tensor_scalar(
                    s_cacc[:, :], s_projb[:, ob:ob + c.TILE],
                    s_convw[:, 4 * oc:4 * oc + 1], None, ALU.mult)),
                    reads=[f"projb{oc}", "consts"], writes=["cacc"])
                for tap in range(1, 4):
                    k.emit("vector", (lambda e, oc=oc, ob=ob, tap=tap:
                        e.scalar_tensor_tensor(
                            s_cacc[:, :],
                            s_projb[:, ob + tap:ob + tap + c.TILE],
                            s_convw[:, 4 * oc + tap:4 * oc + tap + 1],
                            s_cacc[:, :], ALU.mult, ALU.add)),
                        reads=[f"projb{oc}", "consts", "cacc"],
                        writes=["cacc"])
                k.emit("scalar", (lambda e, oc=oc: e.activation(
                    s_qkvs[:, oc * c.TILE:(oc + 1) * c.TILE], s_cacc[:, :],
                    AF.Silu)),
                    reads=["cacc"], writes=[f"qkvs{oc}"])
                k.emit("vector", (lambda e, oc=oc, ob=ob: e.tensor_copy(
                    s_projb[:, ob:ob + 3],
                    s_projb[:, ob + c.TILE:ob + c.TILE + 3])),
                    reads=[f"projb{oc}"], writes=[f"projb{oc}"])
        if p2lvl < 12:
            continue
        for h2 in range(c.TILE // c.W2):
            co = c.W2 * h2
            k.emit("tensor", [
                (lambda e, f=f, co=co: e.matmul(
                    PA[0:c.NVH, 0:c.W2], s_wab[f][:, 0:c.NVH],
                    s_xnf[f][:, co:co + c.W2],
                    start=(f == 0), stop=(f == c.NF - 1)))
                for f in range(c.NF)],
                reads=["wab"] + xnf_keys, writes=["PA"])
            k.emit("tensor", [
                (lambda e, f=f, co=co: e.matmul(
                    PB[0:c.NVH, 0:c.W2], s_wab[f][:, c.NVH:c.ABW],
                    s_xnf[f][:, co:co + c.W2],
                    start=(f == 0), stop=(f == c.NF - 1)))
                for f in range(c.NF)],
                reads=["wab"] + xnf_keys, writes=["PB"])
            k.emit("scalar", (lambda e, co=co: e.activation(
                s_gc[:, co:co + c.W2], PA[0:c.NVH, 0:c.W2], AF.Exp,
                bias=s_dtb[:, 0:1])),
                reads=["PA", "consts"], writes=["gbc"])
            k.emit("scalar", (lambda e, co=co: e.add(
                s_gc[:, co:co + c.W2], s_gc[:, co:co + c.W2], 1.0)),
                reads=["gbc", "consts"], writes=["gbc"])
            k.emit("scalar", (lambda e, co=co: e.activation(
                s_gc[:, co:co + c.W2], s_gc[:, co:co + c.W2],
                AF.Ln)), reads=["gbc"], writes=["gbc"])
            k.emit("vector", (lambda e, co=co: e.tensor_scalar(
                s_gc[:, co:co + c.W2], s_gc[:, co:co + c.W2],
                s_nega[:, 0:1], None, ALU.mult)),
                reads=["gbc", "nega"], writes=["gbc"])
            k.emit("scalar", (lambda e, co=co: e.activation(
                s_bc[:, co:co + c.W2], PB[0:c.NVH, 0:c.W2],
                AF.Sigmoid)), reads=["PB"], writes=["gbc"])
            k.emit("vector", (lambda e, co=co: e.tensor_scalar(
                s_nbf[:, co:co + c.W2], s_bc[:, co:co + c.W2],
                -1.0, None, ALU.mult)),
                reads=["gbc"], writes=["nbf"])
        if p2lvl < 13:
            continue
        for cb in range(c.NCH):
            tok0 = cb * 128
            for z in range(c.NZC):
                zb = PD if z % 2 == 0 else PP1
                zbk = "PD" if z % 2 == 0 else "PP1"
                k.emit("tensor", [
                    (lambda e, f=f, cb=cb, z=z, zb=zb: e.matmul(
                        zb[:, 0:c.ZW],
                        s_xnf[f][:, 128 * cb:128 * (cb + 1)],
                        s_wz[z][f][:, :],
                        start=(f == 0), stop=(f == c.NF - 1)))
                    for f in range(c.NF)],
                    reads=[f"wz{z}"] + xnf_keys, writes=[zbk])
                k.emit("scalar", (lambda e, z=z, zb=zb: e.activation(
                    s_zg[:, c.ZW * z:c.ZW * (z + 1)],
                    zb[:, 0:c.ZW], AF.Silu)),
                    reads=[zbk], writes=["zg"])
                k.emit("vector", (lambda e, z=z: e.tensor_tensor(
                    s_zg[:, c.ZW * z:c.ZW * (z + 1)],
                    s_zg[:, c.ZW * z:c.ZW * (z + 1)],
                    s_normb[:, 0:c.ZW], ALU.mult)),
                    reads=["zg", "normb"], writes=["zg"])
            if cb == c.NCH - 1 and t < c.NT - 1:
                load_xnf(t + 1)
            k.emit("tensor", (lambda e, tok0=tok0: e.transpose(
                PB[0:128, 0:c.NVH], s_gc[:, tok0:tok0 + 128],
                s_eyef[0:c.NVH, 0:c.NVH])),
                reads=["gbc", "consts"], writes=["PB"])
            k.emit("tensor", (lambda e, tok0=tok0: e.transpose(
                PB[0:128, 64:64 + c.NVH], s_bc[:, tok0:tok0 + 128],
                s_eyef[0:c.NVH, 0:c.NVH])),
                reads=["gbc", "consts"], writes=["PB"])
            k.emit("vector", lambda e: e.tensor_copy(
                s_gtm[:, :], PB[0:128, 0:c.NVH]),
                reads=["PB"], writes=["gbtm"])
            k.emit("vector", lambda e: e.tensor_copy(
                s_btm[:, :], PB[0:128, 64:64 + c.NVH]),
                reads=["PB"], writes=["gbtm"])
            k.emit("tensor", lambda e: e.matmul(
                PB[:, 128:128 + c.NVH], s_ltri[:, :], s_gtm[:, :],
                start=True, stop=True),
                reads=["gbtm", "consts"], writes=["PB"])
            k.emit("vector", lambda e: e.tensor_copy(
                s_G[:, :], PB[:, 128:128 + c.NVH]),
                reads=["PB"], writes=["G"])
            k.emit("tensor", lambda e: e.matmul(
                PB[0:c.NVH, 256:384], s_gtm[:, :], s_ltri[:, :],
                start=True, stop=True),
                reads=["gbtm", "consts"], writes=["PB"])
            k.emit("vector", lambda e: e.tensor_copy(
                s_Gfm[:, :], PB[0:c.NVH, 256:384]),
                reads=["PB"], writes=["Gfm"])
            k.emit("tensor", lambda e: e.matmul(
                PB[:, 384:384 + c.NVH], s_onesq[:, :],
                s_gtm[:, :], start=True, stop=True),
                reads=["gbtm", "consts"], writes=["PB"])
            k.emit("vector", lambda e: e.tensor_tensor(
                s_gam[:, 0:c.NVH], PB[:, 384:384 + c.NVH], s_G[:, :],
                ALU.subtract), reads=["G", "PB"], writes=["gam"])
            k.emit("scalar", lambda e: e.activation(
                s_gam[:, 0:c.NVH], s_gam[:, 0:c.NVH], AF.Exp),
                reads=["gam"], writes=["gam"])
            k.emit("scalar", lambda e: e.activation(
                s_gam[:, c.NVH:2 * c.NVH], PB[:, 384:384 + c.NVH], AF.Exp),
                reads=["PB"], writes=["gam"])
            k.emit("scalar", lambda e: e.activation(
                s_gam[:, 2 * c.NVH:3 * c.NVH], s_G[:, :], AF.Exp),
                reads=["G"], writes=["gam"])
            k.emit("vector", lambda e: e.tensor_tensor(
                s_gam[:, 3 * c.NVH:4 * c.NVH], s_gam[:, 2 * c.NVH:3 * c.NVH],
                s_btm[:, :], ALU.mult),
                reads=["gam", "gbtm"], writes=["gam"])
            k.emit("vector", lambda e: e.tensor_scalar(
                s_gam[:, 3 * c.NVH:4 * c.NVH], s_gam[:, 3 * c.NVH:4 * c.NVH],
                -1.0, None, ALU.mult), reads=["gam"], writes=["gam"])

            if p2lvl < 14:
                continue
            for half in range(c.NVH // 4):
                k.emit("tensor", [
                    (lambda e, hh=hh, half=half: e.matmul(
                        PB[:, 128 * (hh - 4 * half):128 * (hh - 4 * half + 1)],
                        s_rsel[:, 128 * hh:128 * (hh + 1)], s_Gfm[:, :],
                        start=True, stop=True))
                    for hh in range(4 * half, 4 * half + 4)],
                    reads=["Gfm", "consts"], writes=["PB"])
                k.emit("vector", (lambda e, half=half: e.tensor_copy(
                    s_gbc[:, 512 * half:512 * (half + 1)], PB[:, 0:512])),
                    reads=["PB"], writes=["gbc"])
                k.emit("tensor", [
                    (lambda e, hh=hh, half=half, tok0=tok0: e.matmul(
                        PB[:, 128 * (hh - 4 * half):128 * (hh - 4 * half + 1)],
                        s_rsel[:, 128 * hh:128 * (hh + 1)],
                        s_nbf[:, tok0:tok0 + 128],
                        start=True, stop=True))
                    for hh in range(4 * half, 4 * half + 4)],
                    reads=["nbf", "consts"], writes=["PB"])
                k.emit("vector", (lambda e, half=half: e.tensor_copy(
                    s_bbc[:, 512 * half:512 * (half + 1)], PB[:, 0:512])),
                    reads=["PB"], writes=["bbc"])
            NB = [PA, PB]
            NBK = ["PA", "PB"]
            DBL = [PC, PP0]
            DBK = ["PC", "PP0"]
            SOB = [PD, PP1]
            SOK = ["PD", "PP1"]
            TRB = [PF0, PF1]
            TRK = ["PF0", "PF1"]
            for kh in range(c.NKH):
                kp = kh % 2
                nb, nbk = NB[kp], NBK[kp]
                qoc, koc = kh, c.NKH + kh
                qsl = s_qkvs[:, qoc * c.TILE + tok0:qoc * c.TILE + tok0 + 128]
                ksl = s_qkvs[:, koc * c.TILE + tok0:koc * c.TILE + tok0 + 128]
                k.emit("scalar", (lambda e, ksl=ksl, kp=kp: e.activation(
                    s_ksq[kp][:, 0:128], ksl, AF.Square)),
                    reads=[f"qkvs{koc}"], writes=[f"ksq{kp}"])
                k.emit("scalar", (lambda e, qsl=qsl, kp=kp: e.activation(
                    s_ksq[kp][:, 128:256], qsl, AF.Square)),
                    reads=[f"qkvs{qoc}"], writes=[f"ksq{kp}"])
                k.emit("tensor", [
                    (lambda e, kp=kp, nb=nb: e.matmul(
                        nb[0:1, 256:384], s_onesc[:, :],
                        s_ksq[kp][:, 0:128], start=True, stop=True)),
                    (lambda e, kp=kp, nb=nb: e.matmul(
                        nb[0:1, 384:512], s_onesc[:, :],
                        s_ksq[kp][:, 128:256], start=True, stop=True))],
                    reads=[f"ksq{kp}", "consts"], writes=[nbk])
                k.emit("scalar", (lambda e, kp=kp, nb=nb: e.activation(
                    s_sks[kp][:, 0:128], nb[0:1, 256:384], AF.Sqrt,
                    bias=cst(1e-6, 1))),
                    reads=[nbk, "consts"], writes=[f"sks{kp}"])
                k.emit("scalar", (lambda e, kp=kp, nb=nb: e.activation(
                    s_sks[kp][:, 128:256], nb[0:1, 384:512], AF.Sqrt,
                    scale=cst(128.0, 1), bias=cst(128e-6, 1))),
                    reads=[nbk, "consts"], writes=[f"sks{kp}"])
                k.emit("vector", (lambda e, kp=kp: e.reciprocal(
                    s_sks[kp][:, :], s_sks[kp][:, :])),
                    reads=[f"sks{kp}"], writes=[f"sks{kp}"])
                k.emit("tensor", [
                    (lambda e, kp=kp, nb=nb: e.matmul(
                        nb[:, 256:384], s_ones1[:, :],
                        s_sks[kp][:, 0:128], start=True, stop=True)),
                    (lambda e, kp=kp, nb=nb: e.matmul(
                        nb[:, 384:512], s_ones1[:, :],
                        s_sks[kp][:, 128:256], start=True, stop=True))],
                    reads=[f"sks{kp}", "consts"], writes=[nbk])
                k.emit("vector", (lambda e, ksl=ksl, kp=kp, nb=nb:
                    e.tensor_tensor(
                        s_ktf[kp][:, :], ksl, nb[:, 256:384], ALU.mult)),
                    reads=[f"qkvs{koc}", nbk], writes=[f"ktf{kp}"])
                k.emit("vector", (lambda e, qsl=qsl, kp=kp, nb=nb:
                    e.tensor_tensor(
                        s_qtf[kp][:, :], qsl, nb[:, 384:512], ALU.mult)),
                    reads=[f"qkvs{qoc}", nbk], writes=[f"qtf{kp}"])
                k.emit("tensor", (lambda e, kp=kp: e.transpose(
                    TRB[kp][:, 0:128], s_ktf[kp][:, :], s_eyebf[:, :])),
                    reads=[f"ktf{kp}", "consts"], writes=[TRK[kp]])
                k.emit("vector", (lambda e, kp=kp: e.tensor_copy(
                    s_ktm[kp][:, :], TRB[kp][:, 0:128])),
                    reads=[TRK[kp]], writes=[f"ktm{kp}"])
                k.emit("tensor", [
                    (lambda e, kp=kp, nb=nb: e.matmul(
                        nb[:, 0:128], s_ktf[kp][:, :], s_ktf[kp][:, :],
                        start=True, stop=True)),
                    (lambda e, kp=kp, nb=nb: e.matmul(
                        nb[:, 128:256], s_ktf[kp][:, :], s_qtf[kp][:, :],
                        start=True, stop=True))],
                    reads=[f"ktf{kp}", f"qtf{kp}"], writes=[nbk])
                k.emit("vector", (lambda e, kp=kp, nb=nb: e.tensor_copy(
                    s_kk[kp][:, :], nb[:, 0:256])),
                    reads=[nbk], writes=[f"kk{kp}"])

                if p2lvl < 15:
                    continue
                def head_gen(hh):
                    hp = hh % 2
                    db, dbk = DBL[hp], DBK[hp]
                    so, sok = SOB[hp], SOK[hp]
                    tr, trk = TRB[hp], TRK[hp]
                    voc = 2 * c.NKH + hh
                    gsl = s_gbc[:, 128 * hh:128 * (hh + 1)]
                    bsl = s_bbc[:, 128 * hh:128 * (hh + 1)]
                    k.emit("vector", (lambda e, hh=hh, hp=hp, gsl=gsl:
                        e.scalar_tensor_tensor(
                            s_argE[hp][:, :], gsl, s_G[:, hh:hh + 1],
                            s_maskE[:, :], ALU.subtract, ALU.add)),
                        reads=["gbc", "G", "consts"], writes=[f"argE{hp}"])
                    yield
                    k.emit("vector", (lambda e, hh=hh, hp=hp, gsl=gsl:
                        e.scalar_tensor_tensor(
                            s_argN[hp][:, :], gsl, s_G[:, hh:hh + 1],
                            s_mask2[:, :], ALU.subtract, ALU.add)),
                        reads=["gbc", "G", "consts"], writes=[f"argN{hp}"])
                    yield
                    k.emit("scalar", (lambda e, hp=hp: e.activation(
                        s_E[hp][:, :], s_argE[hp][:, :], AF.Exp)),
                        reads=[f"argE{hp}"], writes=[f"E{hp}"])
                    yield
                    k.emit("scalar", (lambda e, hp=hp: e.activation(
                        s_E2[hp][:, :], s_argN[hp][:, :], AF.Exp,
                        scale=cst(-1.0))),
                        reads=[f"argN{hp}", "consts"], writes=[f"E2{hp}"])
                    yield
                    if p2lvl < 1520:
                        return
                    k.emit("vector", (lambda e, hp=hp, kp=kp: e.tensor_tensor(
                        s_A2T[hp][:, :], s_E[hp][:, :], s_kk[kp][:, 128:256],
                        ALU.mult)),
                        reads=[f"E{hp}", f"kk{kp}"], writes=[f"A2T{hp}"])
                    yield
                    k.emit("vector", (lambda e, hp=hp: e.tensor_tensor(
                        s_es[hp][:, :], s_E[hp][:, :], s_mask01[:, :],
                        ALU.mult)),
                        reads=[f"E{hp}", "consts"], writes=[f"es{hp}"])
                    yield
                    k.emit("vector", (lambda e, hp=hp, kp=kp: e.tensor_tensor(
                        s_ntm[hp][:, :], s_es[hp][:, :], s_kk[kp][:, 0:128],
                        ALU.mult)),
                        reads=[f"es{hp}", f"kk{kp}"], writes=[f"ntm{hp}"])
                    yield
                    k.emit("vector", (lambda e, hp=hp, bsl=bsl: e.tensor_tensor(
                        s_NT[hp][:, :], s_ntm[hp][:, :], bsl, ALU.mult)),
                        reads=[f"ntm{hp}", "bbc"], writes=[f"NT{hp}"])
                    yield
                    k.emit("vector", (lambda e, hp=hp, kp=kp: e.tensor_tensor(
                        s_es[hp][:, :], s_E2[hp][:, :], s_kk[kp][:, 0:128],
                        ALU.mult)),
                        reads=[f"E2{hp}", f"kk{kp}"], writes=[f"es{hp}"])
                    yield
                    k.emit("vector", (lambda e, hh=hh, hp=hp: e.tensor_scalar(
                        s_N[hp][:, :], s_es[hp][:, :],
                        s_btm[:, hh:hh + 1], -1.0,
                        ALU.mult, ALU.mult)),
                        reads=[f"es{hp}", "gbtm"], writes=[f"N{hp}"])
                    yield
                    if p2lvl < 1530:
                        return
                    k.emit("vector", (lambda e, hp=hp: e.tensor_tensor(
                        s_TT[hp][0][:, :], s_NT[hp][:, :], s_eyebf[:, :],
                        ALU.add)),
                        reads=[f"NT{hp}", "consts"], writes=[f"TT{hp}0"])
                    yield
                    k.emit("vector", (lambda e, hp=hp: e.tensor_copy(
                        s_P[hp][0][:, :], s_N[hp][:, :])),
                        reads=[f"N{hp}"], writes=[f"P{hp}0"])
                    yield
                    k.emit("vector", (lambda e, hp=hp: e.tensor_copy(
                        s_PT[hp][0][:, :], s_NT[hp][:, :])),
                        reads=[f"NT{hp}"], writes=[f"PT{hp}0"])
                    yield
                    pp = 0
                    for lv in range(1, c.NLEV + 1):
                        np_, op_ = 1 - pp, pp
                        last = (lv == c.NLEV)
                        k.emit("tensor", (lambda e, op_=op_, hp=hp, db=db:
                            e.matmul(
                                db[:, 0:128], s_PT[hp][op_][:, :],
                                s_P[hp][op_][:, :], start=True, stop=True)),
                            reads=[f"P{hp}{op_}", f"PT{hp}{op_}"], writes=[dbk])
                        yield
                        if not last:
                            k.emit("tensor", (lambda e, op_=op_, hp=hp, db=db:
                                e.matmul(
                                    db[:, 128:256], s_P[hp][op_][:, :],
                                    s_PT[hp][op_][:, :],
                                    start=True, stop=True)),
                                reads=[f"P{hp}{op_}", f"PT{hp}{op_}"],
                                writes=[dbk])
                            yield
                        k.emit("scalar", (lambda e, np_=np_, hp=hp, db=db:
                            e.activation(
                                s_P[hp][np_][:, :], db[:, 0:128], AF.Copy)),
                            reads=[dbk], writes=[f"P{hp}{np_}"])
                        yield
                        if not last:
                            k.emit("vector", (lambda e, np_=np_, hp=hp, db=db:
                                e.tensor_copy(
                                    s_PT[hp][np_][:, :], db[:, 128:256])),
                                reads=[dbk], writes=[f"PT{hp}{np_}"])
                            yield
                        k.emit("tensor", (lambda e, np_=np_, op_=op_, hp=hp,
                                          db=db: e.matmul(
                            db[:, 256:384], s_P[hp][np_][:, :],
                            s_TT[hp][op_][:, :], start=True, stop=True)),
                            reads=[f"P{hp}{np_}", f"TT{hp}{op_}"], writes=[dbk])
                        yield
                        k.emit("vector", (lambda e, np_=np_, op_=op_, hp=hp,
                                          db=db: e.tensor_tensor(
                            s_TT[hp][np_][:, :], db[:, 256:384],
                            s_TT[hp][op_][:, :], ALU.add)),
                            reads=[dbk, f"TT{hp}{op_}"], writes=[f"TT{hp}{np_}"])
                        yield
                        pp = np_
                    if p2lvl < 1540:
                        return
                    k.emit("tensor", (lambda e, hh=hh, hp=hp, kp=kp, so=so:
                        e.matmul(
                            so[:, 0:128], s_ktf[kp][:, :], s_Sbf[hh][:, :],
                            start=True, stop=True)),
                        reads=[f"ktf{kp}", f"Sbf{hh}"], writes=[sok])
                    yield
                    k.emit("tensor", (lambda e, voc=voc, tok0=tok0, tr=tr:
                        e.transpose(
                            tr[:, 128:256],
                            s_qkvs[:, voc * c.TILE + tok0:
                                   voc * c.TILE + tok0 + 128],
                            s_eyebf[:, :])),
                        reads=[f"qkvs{voc}", "consts"], writes=[trk])
                    yield
                    k.emit("scalar", (lambda e, hh=hh, hp=hp, tr=tr:
                        e.activation(
                            s_dump[hp][:, :], tr[:, 128:256], AF.Copy,
                            scale=s_btm[:, hh:hh + 1])),
                        reads=[trk, "gbtm"], writes=[f"dump{hp}"])
                    yield
                    k.emit("vector", (lambda e, hh=hh, hp=hp, so=so:
                        e.scalar_tensor_tensor(
                            s_RHS[hp][:, :], so[:, 0:128],
                            s_gam[:, 3 * c.NVH + hh:3 * c.NVH + hh + 1],
                            s_dump[hp][:, :], ALU.mult, ALU.add)),
                        reads=[sok, "gam", f"dump{hp}"], writes=[f"RHS{hp}"])
                    yield
                    if p2lvl < 1550:
                        return
                    k.emit("tensor", (lambda e, pp=pp, hp=hp, db=db: e.matmul(
                        db[:, 384:512], s_TT[hp][pp][:, :], s_RHS[hp][:, :],
                        start=True, stop=True)),
                        reads=[f"TT{hp}{pp}", f"RHS{hp}"], writes=[dbk])
                    yield
                    k.emit("scalar", (lambda e, hp=hp, db=db: e.activation(
                        s_Xbf[hp][:, :], db[:, 384:512], AF.Copy)),
                        reads=[dbk], writes=[f"Xbf{hp}"])
                    yield
                    k.emit("vector", (lambda e, hh=hh, hp=hp, db=db:
                        e.tensor_scalar(
                            s_Xw[hp][:, :], db[:, 384:512],
                            s_gam[:, hh:hh + 1], None, ALU.mult)),
                        reads=[dbk, "gam"], writes=[f"Xw{hp}"])
                    yield
                    if p2lvl < 1560:
                        return
                    k.emit("tensor", (lambda e, hh=hh, hp=hp, kp=kp, so=so:
                        e.matmul(
                            so[:, 128:256], s_qtf[kp][:, :], s_Sbf[hh][:, :],
                            start=True, stop=True)),
                        reads=[f"qtf{kp}", f"Sbf{hh}"], writes=[sok])
                    yield
                    k.emit("tensor", (lambda e, hp=hp, so=so: e.matmul(
                        so[:, 256:384], s_A2T[hp][:, :], s_Xbf[hp][:, :],
                        start=True, stop=True)),
                        reads=[f"A2T{hp}", f"Xbf{hp}"], writes=[sok])
                    yield
                    k.emit("scalar", (lambda e, hh=hh, hp=hp, so=so:
                        e.activation(
                            s_es[hp][:, :], so[:, 128:256], AF.Copy,
                            scale=s_gam[:, 2 * c.NVH + hh:
                                        2 * c.NVH + hh + 1])),
                        reads=[sok, "gam"], writes=[f"es{hp}"])
                    yield
                    k.emit("vector", (lambda e, hp=hp, so=so: e.tensor_tensor(
                        s_Otm[hp][:, :], s_es[hp][:, :], so[:, 256:384],
                        ALU.add)),
                        reads=[f"es{hp}", sok], writes=[f"Otm{hp}"])
                    yield
                    if p2lvl < 1570:
                        return
                    k.emit("tensor", (lambda e, hp=hp, kp=kp, so=so: e.matmul(
                        so[:, 384:512], s_ktm[kp][:, :], s_Xw[hp][:, :],
                        start=True, stop=True)),
                        reads=[f"ktm{kp}", f"Xw{hp}"], writes=[sok])
                    yield
                    k.emit("vector", (lambda e, hh=hh, so=so:
                        e.scalar_tensor_tensor(
                            s_S[hh][:, :], s_S[hh][:, :],
                            s_gam[:, c.NVH + hh:c.NVH + hh + 1],
                            so[:, 384:512], ALU.mult, ALU.add)),
                        reads=[f"S{hh}", "gam", sok], writes=[f"S{hh}"])
                    yield
                    k.emit("scalar", (lambda e, hh=hh: e.activation(
                        s_Sbf[hh][:, :], s_S[hh][:, :], AF.Copy)),
                        reads=[f"S{hh}"], writes=[f"Sbf{hh}"])
                    yield
                    if p2lvl < 1580:
                        return
                    k.emit("scalar", (lambda e, hp=hp: e.activation(
                        s_dump[hp][:, :], s_Otm[hp][:, :], AF.Square,
                        accum_out=s_oc2[hp][:, 0:1])),
                        reads=[f"Otm{hp}"], writes=[f"dump{hp}", f"oc2{hp}"])
                    yield
                    k.emit("scalar", (lambda e, hp=hp: e.activation(
                        s_oc2[hp][:, 1:2], s_oc2[hp][:, 0:1], AF.Sqrt,
                        scale=cst(1.0 / 128), bias=cst(1e-6))),
                        reads=[f"oc2{hp}", "consts"], writes=[f"oc2{hp}"])
                    yield
                    k.emit("vector", (lambda e, hp=hp: e.reciprocal(
                        s_oc2[hp][:, 1:2], s_oc2[hp][:, 1:2])),
                        reads=[f"oc2{hp}"], writes=[f"oc2{hp}"])
                    yield
                    k.emit("vector", (lambda e, hh=hh, hp=hp:
                        e.scalar_tensor_tensor(
                            s_Xbf[hp][:, :], s_Otm[hp][:, :],
                            s_oc2[hp][:, 1:2],
                            s_zg[:, 128 * hh:128 * (hh + 1)],
                            ALU.mult, ALU.mult)),
                        reads=[f"Otm{hp}", f"oc2{hp}", "zg"],
                        writes=[f"Xbf{hp}"])
                    yield
                    k.emit("tensor", (lambda e, hp=hp, tr=tr: e.transpose(
                        tr[:, 0:128], s_Xbf[hp][:, :], s_eyebf[:, :])),
                        reads=[f"Xbf{hp}", "consts"], writes=[trk])
                    yield
                    opar = ofc[0] % 2
                    ofc[0] += 1
                    k.emit("vector", (lambda e, opar=opar, tr=tr: e.tensor_copy(
                        s_ocb[opar][:, :], tr[:, 0:128])),
                        reads=[trk], writes=[f"ocb{opar}"])
                    yield
                    k.dma("sync", f"d_of{opar}", (lambda e, hh=hh, tok0=tok0,
                                                  t=t, opar=opar: e.dma_start(
                        out=ofm_sc[hh, :, t * c.TILE + tok0:
                                   t * c.TILE + tok0 + 128],
                        in_=s_ocb[opar][:, :])),
                        reads=[f"ocb{opar}"], writes=["ofm_sc"])
                    yield

                if p2lvl < 15:
                    continue
                gens = [head_gen(2 * kh), head_gen(2 * kh + 1)]
                alive = list(gens)
                while alive:
                    for g_ in list(alive):
                        try:
                            next(g_)
                        except StopIteration:
                            alive.remove(g_)
    if upto <= 2:
        k.fence()
        stP2.close()
        return finish_dummy()

    k.fence()
    stP2.close()
    st25 = ExitStack()
    s_wout = [st25.enter_context(
        nc.sbuf_tensor(f"s_wout{v}", [128, c.D], BF16))
        for v in range(c.NVH)]
    s_ofm2 = [[st25.enter_context(
        nc.sbuf_tensor(f"s_ofm2{p}_{v}", [128, c.TILE], BF16))
        for v in range(c.NVH)] for p in range(2)]
    s_sfm = [st25.enter_context(
        nc.sbuf_tensor(f"s_sfm{i}", [128, c.TILE], F32)) for i in range(2)]

    st2 = ExitStack()
    s_rs = st2.enter_context(nc.sbuf_tensor("s_rs", [1, c.TQ], BF16))
    s_gus = st2.enter_context(nc.sbuf_tensor("s_gus", [1, c.TQ], BF16))

    st3 = ExitStack()
    s_gts = st3.enter_context(nc.sbuf_tensor("s_gts", [128, c.TQ], F32))
    s_gud = [st3.enter_context(nc.sbuf_tensor(f"s_gud{i}", [128, c.W4], BF16))
             for i in range(2)]
    s_xnq = [st3.enter_context(nc.sbuf_tensor(f"s_xnq{f}", [128, c.TQ], BF16))
             for f in range(c.NF)]
    NWSL = 3
    s_wgu = [[[st3.enter_context(
        nc.sbuf_tensor(f"s_wgu{p}_{m}_{f}", [128, 512], BF16))
        for f in range(c.NF)] for m in range(2)] for p in range(NWSL)]

    def load_wgu(hb, par):
        for m, wsrc in ((0, wgT), (1, wuT)):
            k.dma("gpsimd", f"d_f{m}{par}", [
                (lambda e, m=m, f=f, hb=hb, par=par, wsrc=wsrc:
                 e.dma_start(
                     out=s_wgu[par][m][f][:, :],
                     in_=wsrc[128 * f:128 * (f + 1),
                              512 * hb:512 * (hb + 1)]))
                for f in range(c.NF)], writes=[f"wgu{par}"])

    k.dma("gpsimd", "d_wo", [
        (lambda e, v=v: e.dma_start(out=s_wout[v][:, :],
                                    in_=wout[128 * v:128 * (v + 1), :]))
        for v in range(c.NVH)], writes=["woutsb"])
    k.dma("sync", "d_xnq", [
        (lambda e, f=f: e.dma_start(
            out=s_xnq[f][:, :],
            in_=xnq_sc[:, 128 * f:128 * (f + 1)], transpose=True))
        for f in range(c.NF)],
        reads=["xnq_sc"], writes=["xnq"])
    for pp_ in range(min(NWSL, c.NHB)):
        load_wgu(pp_, pp_)

    def load_ofm(t, par):
        k.dma("sync", f"d_or{par}", [
            (lambda e, v=v, t=t, par=par: e.dma_start(
                out=s_ofm2[par][v][:, :],
                in_=ofm_sc[v, :, t * c.TILE:(t + 1) * c.TILE]))
            for v in range(c.NVH)],
            reads=["ofm_sc"], writes=[f"ofm2{par}"])

    load_ofm(0, 0)
    if c.NT > 1:
        load_ofm(1, 1)
    for t in range(c.NT):
        par = t % 2
        blk = t // (c.NT // 4)
        colo = (t % (c.NT // 4)) * c.TILE
        for outc in range(c.OUTC):
            opar = outc % 2
            bank, bkey = (PC, "PC") if opar == 0 else (PD, "PD")
            k.emit("tensor", [
                (lambda e, v=v, outc=outc, par=par, bank=bank: e.matmul(
                    bank[:, 0:c.TILE],
                    s_wout[v][:, 128 * outc:128 * (outc + 1)],
                    s_ofm2[par][v][:, :],
                    start=(v == 0), stop=(v == c.NVH - 1)))
                for v in range(c.NVH)],
                reads=["woutsb", f"ofm2{par}"], writes=[bkey])
            k.emit("scalar", (lambda e, bank=bank, opar=opar: e.activation(
                s_sfm[opar][:, 0:c.TILE], bank[:, 0:c.TILE], AF.Copy)),
                reads=[bkey], writes=[f"sfm{opar}"])
            k.dma("sync", f"d_ri{opar}", (lambda e, outc=outc, blk=blk,
                                          colo=colo, opar=opar: e.dma_start(
                out=rsin[blk, 128 * outc:128 * (outc + 1),
                         colo:colo + c.TILE],
                in_=s_sfm[opar][:, 0:c.TILE])),
                reads=[f"sfm{opar}"], writes=["rsin"])
        if t + 2 < c.NT:
            load_ofm(t + 2, par)

    k.emit("gpsimd", (lambda e: e.collective_compute(
        "ReduceScatter", ALU.add,
        replica_groups=[[0, 1, 2, 3], [4, 5, 6, 7]],
        ins=[rsin.ap().opt()], outs=[rsout.ap().opt()])),
        reads=["rsin"], writes=["rsout"])

    if upto <= 3:
        return finish_dummy()

    TQH = c.TQ // c.W4
    GUPB = [(PP0, "PP0"), (PP1, "PP1"), (PA, "PA"), (PB, "PB")]
    for h2 in range(TQH):
        bank, bkey = (PC, "PC") if h2 % 2 == 0 else (PD, "PD")
        k.emit("tensor", [
            (lambda e, f=f, h2=h2, bank=bank: e.matmul(
                bank[0:1, 0:c.W4], s_onesb[:, :],
                s_xnq[f][:, c.W4 * h2:c.W4 * (h2 + 1)],
                start=(f == 0), stop=(f == c.NF - 1)))
            for f in range(c.NF)],
            reads=["xnq", "consts"], writes=[bkey])
        k.emit("vector", (lambda e, h2=h2, bank=bank: e.tensor_copy(
            s_rs[:, c.W4 * h2:c.W4 * (h2 + 1)], bank[0:1, 0:c.W4])),
            reads=[bkey], writes=["rs"])

    for hb in range(c.NHB):
        par = hb % NWSL
        for oc in range(4 if c.HID >= 512 else c.NHC):
            hc = 4 * hb + oc
            for h2 in range(TQH):
                gb, gk = GUPB[2 * (h2 % 2)]
                ub, uk = GUPB[2 * (h2 % 2) + 1]
                k.emit("tensor", [
                    (lambda e, f=f, oc=oc, h2=h2, par=par, gb=gb: e.matmul(
                        gb[:, 0:c.W4],
                        s_wgu[par][0][f][:, 128 * oc:128 * (oc + 1)],
                        s_xnq[f][:, c.W4 * h2:c.W4 * (h2 + 1)],
                        start=(f == 0), stop=False))
                    for f in range(c.NF)] + [
                    (lambda e, h2=h2, gb=gb: e.matmul(
                        gb[:, 0:c.W4], s_m128[:, :],
                        s_rs[:, c.W4 * h2:c.W4 * (h2 + 1)],
                        start=False, stop=True))],
                    reads=[f"wgu{par}", "xnq", "rs", "consts"], writes=[gk])
                k.emit("tensor", [
                    (lambda e, f=f, oc=oc, h2=h2, par=par, ub=ub: e.matmul(
                        ub[:, 0:c.W4],
                        s_wgu[par][1][f][:, 128 * oc:128 * (oc + 1)],
                        s_xnq[f][:, c.W4 * h2:c.W4 * (h2 + 1)],
                        start=(f == 0), stop=False))
                    for f in range(c.NF)] + [
                    (lambda e, h2=h2, ub=ub: e.matmul(
                        ub[:, 0:c.W4], s_m128[:, :],
                        s_rs[:, c.W4 * h2:c.W4 * (h2 + 1)],
                        start=False, stop=True))],
                    reads=[f"wgu{par}", "xnq", "rs", "consts"], writes=[uk])
                k.emit("scalar", (lambda e, hc=hc, h2=h2, gb=gb: e.activation(
                    s_gts[:, c.W4 * h2:c.W4 * (h2 + 1)], gb[:, 0:c.W4],
                    AF.Silu, scale=s_sg2[:, hc:hc + 1])),
                    reads=[gk, "consts"], writes=["gts"])
                gpar = (hc * TQH + h2) % 2
                k.emit("vector", (lambda e, hc=hc, h2=h2, ub=ub, gpar=gpar:
                    e.scalar_tensor_tensor(
                        s_gud[gpar][:, :],
                        ub[:, 0:c.W4], s_su2[:, hc:hc + 1],
                        s_gts[:, c.W4 * h2:c.W4 * (h2 + 1)],
                        ALU.mult, ALU.mult)),
                    reads=[uk, "consts", "gts"], writes=[f"gud{gpar}"])
                gusb, gusk = (PC, "PC") if h2 % 2 == 0 else (PD, "PD")
                k.emit("tensor", (lambda e, gpar=gpar, hc=hc, gusb=gusb:
                    e.matmul(
                        gusb[0:1, 0:c.W4], s_onesb[:, :], s_gud[gpar][:, :],
                        start=(hc == 0), stop=(hc == c.NHC - 1))),
                    reads=[f"gud{gpar}", "consts"], writes=[gusk])
                k.dma("sync", f"d_gw{gpar}", (lambda e, gpar=gpar, hc=hc,
                                              h2=h2: e.dma_start(
                    out=gu_sc[128 * hc:128 * (hc + 1),
                              c.W4 * h2:c.W4 * (h2 + 1)],
                    in_=s_gud[gpar][:, :])),
                    reads=[f"gud{gpar}"], writes=["gu_sc"])
        if hb + NWSL < c.NHB:
            load_wgu(hb + NWSL, par)
    for h2 in range(TQH):
        bank, bkey = (PC, "PC") if h2 % 2 == 0 else (PD, "PD")
        k.emit("vector", (lambda e, h2=h2, bank=bank: e.tensor_copy(
            s_gus[:, c.W4 * h2:c.W4 * (h2 + 1)], bank[0:1, 0:c.W4])),
            reads=[bkey], writes=["gus"])
    k.fence()
    st3.close()

    if upto <= 4:
        return finish_dummy()
    st4 = ExitStack()
    s_acc = st4.enter_context(
        nc.sbuf_tensor("s_acc", [128, c.OUTC * c.TQ], F32))
    s_wd = [[st4.enter_context(
        nc.sbuf_tensor(f"s_wd{p}_{j}", [128, c.D], BF16))
        for j in range(c.GSZ)] for p in range(2)]
    s_gur = [[st4.enter_context(
        nc.sbuf_tensor(f"s_gur{p}_{j}", [128, c.TQ], BF16))
        for j in range(c.GSZ)] for p in range(2)]
    s_fin = [st4.enter_context(
        nc.sbuf_tensor(f"s_fin{i}", [128, c.TQ], F32)) for i in range(4)]

    def load_wd(g, par):
        hcs = list(range(c.GSZ * g, min(c.GSZ * (g + 1), c.NHC)))
        k.dma("gpsimd", f"d_wd{par}", [
            (lambda e, j=j, hc=hc, par=par: e.dma_start(
                out=s_wd[par][j][:, :], in_=wdT[128 * hc:128 * (hc + 1), :]))
            for j, hc in enumerate(hcs)], writes=[f"wd{par}"])
        k.dma("sync", f"d_gr{par}", [
            (lambda e, j=j, hc=hc, par=par: e.dma_start(
                out=s_gur[par][j][:, :],
                in_=gu_sc[128 * hc:128 * (hc + 1), :]))
            for j, hc in enumerate(hcs)],
            reads=["gu_sc"], writes=[f"gur{par}"])

    load_wd(0, 0)
    for g in range(c.NGRP):
        par = g % 2
        if g + 1 < c.NGRP:
            load_wd(g + 1, 1 - par)
        hcs = list(range(c.GSZ * g, min(c.GSZ * (g + 1), c.NHC)))
        for outc in range(c.OUTC):
            for h2 in range(TQH):
                bank, bkey = GUPB[2 * (outc % 2) + h2 % 2]
                fns = [
                    (lambda e, j=j, hc=hc, outc=outc, h2=h2, par=par,
                     bank=bank, hn=len(hcs), g=g: e.matmul(
                        bank[:, 0:c.W4],
                        s_wd[par][j][:, 128 * outc:128 * (outc + 1)],
                        s_gur[par][j][:, c.W4 * h2:c.W4 * (h2 + 1)],
                        start=(j == 0),
                        stop=(j == hn - 1 and g != c.NGRP - 1)))
                    for j, hc in enumerate(hcs)]
                if g == c.NGRP - 1:
                    fns.append(lambda e, h2=h2, bank=bank: e.matmul(
                        bank[:, 0:c.W4], s_m128[:, :],
                        s_gus[:, c.W4 * h2:c.W4 * (h2 + 1)],
                        start=False, stop=True))
                k.emit("tensor", fns,
                       reads=[f"wd{par}", f"gur{par}", "gus", "consts"],
                       writes=[bkey])
                ao = outc * c.TQ + c.W4 * h2
                if g == 0:
                    k.emit("vector", (lambda e, ao=ao, bank=bank: e.tensor_copy(
                        s_acc[:, ao:ao + c.W4], bank[:, 0:c.W4])),
                        reads=[bkey], writes=[f"acc{outc}"])
                else:
                    k.emit("vector", (lambda e, ao=ao, bank=bank, outc=outc:
                        e.tensor_tensor(
                            s_acc[:, ao:ao + c.W4], bank[:, 0:c.W4],
                            s_acc[:, ao:ao + c.W4], ALU.add)),
                        reads=[bkey, f"acc{outc}"], writes=[f"acc{outc}"])

    for outc in range(c.OUTC):
        k.dma("sync", "d_ro", (lambda e, outc=outc: e.dma_start(
            out=s_fin[0][:, :], in_=rsout[128 * outc:128 * (outc + 1), :])),
            reads=["rsout"], writes=["fin0"])
        k.dma("sync", "d_xt", (lambda e, outc=outc: e.dma_start(
            out=s_fin[1][:, :], in_=xqT[128 * outc:128 * (outc + 1), :])),
            writes=["fin1"])
        k.emit("scalar", (lambda e, outc=outc: e.activation(
            s_fin[2][:, :], s_acc[:, outc * c.TQ:(outc + 1) * c.TQ], AF.Silu,
            scale=s_sd2[:, outc:outc + 1])),
            reads=[f"acc{outc}", "consts"], writes=["fin2"])
        k.emit("vector", lambda e: e.tensor_tensor(
            s_fin[3][:, :], s_fin[0][:, :], s_fin[2][:, :], ALU.mult),
            reads=["fin0", "fin2"], writes=["fin3"])
        k.emit("vector", lambda e: e.tensor_tensor(
            s_fin[3][:, :], s_fin[3][:, :], s_fin[1][:, :], ALU.add),
            reads=["fin3", "fin1"], writes=["fin3"])
        k.dma("sync", "d_out", (lambda e, outc=outc: e.dma_start(
            out=outfm[128 * outc:128 * (outc + 1), :], in_=s_fin[3][:, :])),
            reads=["fin3"], writes=[f"out{outc}"])

    k._wait("gpsimd", "d_out", k.count["d_out"])

    semh = {}
    for name in sorted(set(list(k.count.keys()) + list(K.ESEM.values()))):
        semh[name] = st.enter_context(nc.semaphore(name))
    k.replay(nc, semh)
    st4.close()
    st2.close()
    st25.close()
    st.close()
    return nc


def stage(cfg, A):
    c = cfg
    x = np.ascontiguousarray(np.asarray(A["L0_attn_hidden_states"], np.float32))
    Wqkv = np.asarray(A["L0_attn_W_qkv"], np.float32)
    Wz = np.asarray(A["L0_attn_W_z"], np.float32)
    Wb = np.asarray(A["L0_attn_W_b"], np.float32)
    Wa = np.asarray(A["L0_attn_W_a"], np.float32)
    convw = np.asarray(A["L0_attn_conv_w"], np.float32)
    dtb = np.asarray(A["L0_attn_dt_bias"], np.float32)
    alog = np.asarray(A["L0_attn_A_log"], np.float32)
    normw = np.asarray(A["L0_attn_norm_w"], np.float32)
    Wout = np.asarray(A["L0_attn_W_out"], np.float32)
    wqg = np.asarray(A["L0_ffn_wq_gate"], np.int32)
    sg = np.asarray(A["L0_ffn_s_gate"], np.float32)
    wqu = np.asarray(A["L0_ffn_wq_up"], np.int32)
    su = np.asarray(A["L0_ffn_s_up"], np.float32)
    wqd = np.asarray(A["L0_ffn_wq_down"], np.int32)
    sd = np.asarray(A["L0_ffn_s_down"], np.float32)

    KD = 4 * c.QW
    wgT = np.ascontiguousarray(wqg.T)
    wuT = np.ascontiguousarray(wqu.T)
    wdT = np.ascontiguousarray(wqd.T)
    sg2 = np.ascontiguousarray(sg.reshape(c.NHC, 128).T)
    su2 = np.ascontiguousarray(su.reshape(c.NHC, 128).T)
    sd2 = np.ascontiguousarray(sd.reshape(c.OUTC, 128).T)
    idx = np.arange(128)
    consts = dict(
        eyebf=np.eye(128, dtype=BF),
        eyef=np.eye(128, dtype=np.float32),
        ltri=np.triu(np.ones((128, 128), np.float32)),
        maskE=np.where(idx[None, :] >= idx[:, None], 0.0, -1e30).astype(np.float32),
        mask2=np.where(idx[None, :] < idx[:, None], 0.0, 1e30).astype(np.float32),
        mask01=np.triu(np.ones((128, 128), np.float32), 1),
        rowsel=np.ascontiguousarray(np.repeat(
            np.eye(cfg.NVH, dtype=np.float32), 128, axis=1)),
        normw4=np.ascontiguousarray(np.tile(normw, 512 // normw.shape[0])[None, :]).astype(np.float32),
    )
    in_maps = []
    for core in range(8):
        b, hg = core // 4, core % 4
        qsl = slice(c.QW * hg, c.QW * (hg + 1))
        ksl = slice(KD + c.QW * hg, KD + c.QW * (hg + 1))
        vsl = slice(2 * KD + c.VW * hg, 2 * KD + c.VW * (hg + 1))
        zsl = slice(c.VW * hg, c.VW * (hg + 1))
        hsl = slice(c.NVH * hg, c.NVH * (hg + 1))
        wg = np.concatenate(
            [Wqkv[:, qsl], Wqkv[:, ksl], Wqkv[:, vsl]], axis=1)
        wgc = np.ascontiguousarray(wg.reshape(c.NF, 128, c.WC))
        wzc = np.ascontiguousarray(
            Wz[:, zsl].reshape(c.NF, 128, c.NZC, c.ZW).transpose(0, 2, 1, 3))
        wab = np.concatenate([Wa[:, hsl], Wb[:, hsl]], axis=1)
        wabc = np.ascontiguousarray(wab.reshape(c.NF, 128, c.ABW))
        cw = np.concatenate([convw[qsl], convw[ksl], convw[vsl]], axis=0)
        xb = x[b]
        m = dict(
            xfull=xb,
            xq=np.ascontiguousarray(xb[c.TQ * hg:c.TQ * (hg + 1)]),
            xqT=np.ascontiguousarray(xb[c.TQ * hg:c.TQ * (hg + 1)].T),
            wgc=wgc, wzc=wzc, wabc=wabc,
            convw=np.ascontiguousarray(cw),
            alog=np.ascontiguousarray(alog[hsl][:, None]),
            dtb=np.ascontiguousarray(dtb[hsl][:, None]),
            wout=np.ascontiguousarray(Wout[c.VW * hg:c.VW * (hg + 1)]),
            wgT=wgT, wuT=wuT, wdT=wdT, sg2=sg2, su2=su2, sd2=sd2,
            **consts)
        in_maps.append(m)
    return in_maps


def assemble(cfg, outs):
    c = cfg
    out = np.empty((2, c.T, c.D), np.float32)
    for core in range(8):
        b, hg = core // 4, core % 4
        out[b, c.TQ * hg:c.TQ * (hg + 1), :] = outs[core].T
    return out


def kernel(**inputs):
    cfg = REAL
    in_maps = stage(cfg, inputs)
    nc = build(cfg)
    res = run_bass_kernel_spmd(nc, in_maps, core_ids=list(range(8)))
    return assemble(cfg, [np.asarray(r["outfm"]) for r in res.results])


# version 10
# speedup vs baseline: 1.5751x; 1.2055x over previous
import numpy as np
import ml_dtypes
import concourse.bass as bass
import concourse.mybir as mybir
from concourse.bass_utils import run_bass_kernel_spmd
from contextlib import ExitStack

F32 = mybir.dt.float32
BF16 = mybir.dt.bfloat16
I32 = mybir.dt.int32
AF = mybir.ActivationFunctionType
ALU = mybir.AluOpType
AX = mybir.AxisListType
BF = ml_dtypes.bfloat16


class Cfg:
    def __init__(self, T, TILE, D, NKH, HID):
        self.T, self.TILE, self.D, self.NKH, self.HID = T, TILE, D, NKH, HID
        self.NVH = 2 * NKH
        self.NF = D // 128
        self.NT = T // TILE
        self.NCH = TILE // 128
        self.TQ = T // 4
        self.QW = NKH * 128
        self.VW = self.NVH * 128
        self.ABW = 2 * self.NVH
        self.WC = 2 * self.QW + self.VW
        self.WCOLS = self.WC + self.VW + self.ABW
        self.OUTC = D // 128
        self.NHC = HID // 128
        self.NHB = HID // 512
        self.GSZ = 4
        self.NGRP = (self.NHC + self.GSZ - 1) // self.GSZ
        self.NLEV = 4
        self.NQB = self.TQ // 128
        self.NTB = T // 128
        self.W2 = min(512, TILE)
        self.W4 = min(512, self.TQ)
        self.NZC = self.VW // 512 if self.VW >= 512 else 0
        self.ZW = 512 if self.VW >= 512 else self.VW
        if self.NZC == 0:
            self.NZC = 1


REAL = Cfg(T=4096, TILE=512, D=2048, NKH=4, HID=5632)
TINY = Cfg(T=512, TILE=128, D=256, NKH=2, HID=512)


class K:
    ENGS = ("tensor", "vector", "scalar", "gpsimd", "sync")
    ESEM = {"tensor": "s_pe", "vector": "s_dve", "scalar": "s_act",
            "gpsimd": "s_gp", "sync": "s_sp"}

    def __init__(self):
        self.streams = {e: [] for e in K.ENGS}
        self.count = {}
        self.mark = {e: {} for e in K.ENGS}
        self.writer = {}
        self.readers = {}
        self.dma_pend = {}
        self.sem_names = set(K.ESEM.values())

    def _wait(self, eng, sem, val):
        if val <= 0:
            return
        if self.mark[eng].get(sem, 0) >= val:
            return
        self.mark[eng][sem] = val
        self.streams[eng].append(("w", sem, val))

    def _deps(self, eng, reads, writes):
        for k in reads:
            for s, v in self.writer.get(k, {}).items():
                self._wait(eng, s, v)
        for k in writes:
            for s, v in self.writer.get(k, {}).items():
                self._wait(eng, s, v)
            for s, v in self.readers.get(k, {}).items():
                self._wait(eng, s, v)

    def _post(self, sem, v, reads, writes):
        for k in writes:
            self.writer.setdefault(k, {})[sem] = v
            self.readers[k] = {}
        for k in reads:
            self.readers.setdefault(k, {})[sem] = v

    PSUM_KEYS = frozenset(
        ["PP0", "PP1", "PA", "PB", "PC", "PD", "PF0", "PF1"])

    def _excl(self, reads, writes):
        pr = [k for k in reads if k in K.PSUM_KEYS]
        if pr:
            reads = [k for k in reads if k not in K.PSUM_KEYS]
            writes = list(writes) + pr
        return reads, writes

    def emit(self, eng, fns, reads=(), writes=()):
        if not isinstance(fns, (list, tuple)):
            fns = [fns]
        reads, writes = self._excl(reads, writes)
        self._deps(eng, reads, writes)
        sem = K.ESEM[eng]
        self.count[sem] = self.count.get(sem, 0) + 1
        v = self.count[sem]
        for i, fn in enumerate(fns):
            self.streams[eng].append(
                ("i", fn, sem if i == len(fns) - 1 else None, 1))
        self._post(sem, v, reads, writes)
        return v

    def dma(self, eng, sem, fns, reads=(), writes=()):
        if not isinstance(fns, (list, tuple)):
            fns = [fns]
        reads, writes = self._excl(reads, writes)
        self.sem_names.add(sem)
        if self.dma_pend.get(sem):
            self._wait(eng, sem, self.count.get(sem, 0))
        self._deps(eng, reads, writes)
        for fn in fns:
            self.count[sem] = self.count.get(sem, 0) + 16
            self.streams[eng].append(("i", fn, sem, 16))
        v = self.count[sem]
        self.dma_pend[sem] = True
        self._post(sem, v, reads, writes)
        return v

    def fence(self):
        for eng in K.ENGS:
            own = K.ESEM[eng]
            for sem, cnt in list(self.count.items()):
                if sem != own:
                    self._wait(eng, sem, cnt)

    def replay(self, nc, semh):
        with nc.Block() as block:
            for name in K.ENGS:
                stream = self.streams[name]
                if not stream:
                    continue

                def body(eng, stream=stream):
                    for it in stream:
                        if it[0] == "w":
                            eng.wait_ge(semh[it[1]], it[2])
                        else:
                            bi = it[1](eng)
                            if it[2] is not None:
                                bi.then_inc(semh[it[2]], it[3])
                getattr(block, name)(body)


def build(cfg, upto=5, p2lvl=10**9):
    c = cfg
    nc = bass.Bass("TRN2", target_bir_lowering=False, debug=False)
    k = K()

    dp = nc.declare_dram_parameter
    xfull = dp("xfull", [c.T, c.D], F32, isOutput=False)
    xq = dp("xq", [c.TQ, c.D], F32, isOutput=False)
    xqT = dp("xqT", [c.D, c.TQ], F32, isOutput=False)
    wgc = dp("wgc", [c.NF, 128, c.WC], F32, isOutput=False)
    wzc = dp("wzc", [c.NF, c.NZC, 128, c.ZW], F32, isOutput=False)
    wabc = dp("wabc", [c.NF, 128, c.ABW], F32, isOutput=False)
    convw = dp("convw", [c.WC, 4], F32, isOutput=False)
    alog = dp("alog", [c.NVH, 1], F32, isOutput=False)
    dtb = dp("dtb", [c.NVH, 1], F32, isOutput=False)
    normw4 = dp("normw4", [1, 512], F32, isOutput=False)
    wout = dp("wout", [c.VW, c.D], F32, isOutput=False)
    wgT = dp("wgT", [c.D, c.HID], I32, isOutput=False)
    wuT = dp("wuT", [c.D, c.HID], I32, isOutput=False)
    wdT = dp("wdT", [c.HID, c.D], I32, isOutput=False)
    sg2 = dp("sg2", [128, c.NHC], F32, isOutput=False)
    su2 = dp("su2", [128, c.NHC], F32, isOutput=False)
    sd2 = dp("sd2", [128, c.OUTC], F32, isOutput=False)
    eyebf = dp("eyebf", [128, 128], BF16, isOutput=False)
    eyef = dp("eyef", [128, 128], F32, isOutput=False)
    ltri = dp("ltri", [128, 128], F32, isOutput=False)
    maskE = dp("maskE", [128, 128], F32, isOutput=False)
    mask2 = dp("mask2", [128, 128], F32, isOutput=False)
    mask01 = dp("mask01", [128, 128], F32, isOutput=False)
    rowsel = dp("rowsel", [c.NVH, 128 * c.NVH], F32, isOutput=False)
    outfm = dp("outfm", [c.D, c.TQ], F32, isOutput=True)

    xn_sc = nc.dram_tensor("xn_sc", [c.T, c.D], BF16)
    xnq_sc = nc.dram_tensor("xnq_sc", [c.TQ, c.D], BF16)
    ofm_sc = nc.dram_tensor("ofm_sc", [c.NVH, 128, c.T], BF16)
    rsin = nc.dram_tensor("rsin", [4, c.D, c.TQ], F32)
    gu_sc = nc.dram_tensor("gu_sc", [c.NHC * 128, c.TQ], BF16)
    rsout = nc.dram_tensor("rsout", [c.D, c.TQ], F32)

    st = ExitStack()
    sb = lambda n, s, d: st.enter_context(nc.sbuf_tensor(n, s, d))
    ps = lambda n, s, d: st.enter_context(nc.psum_tensor(n, s, d))

    PP0 = ps("PP0", [128, 512], F32)
    PP1 = ps("PP1", [128, 512], F32)
    PA = ps("PA", [128, 512], F32)
    PB = ps("PB", [128, 512], F32)
    PC = ps("PC", [128, 512], F32)
    PD = ps("PD", [128, 512], F32)
    PF0 = ps("PF0", [128, 1024], BF16)
    PF1 = ps("PF1", [128, 1024], BF16)

    s_eyebf = sb("s_eyebf", [128, 128], BF16)
    s_eyef = sb("s_eyef", [128, 128], F32)
    s_ltri = sb("s_ltri", [128, 128], F32)
    s_maskE = sb("s_maskE", [128, 128], F32)
    s_mask2 = sb("s_mask2", [128, 128], F32)
    s_mask01 = sb("s_mask01", [128, 128], F32)
    s_normw = sb("s_normw", [1, 512], F32)
    s_alog = sb("s_alog", [c.NVH, 1], F32)
    s_dtb = sb("s_dtb", [c.NVH, 1], F32)
    s_nega = sb("s_nega", [c.NVH, 1], F32)
    s_cst = sb("s_cst", [128, 8], F32)
    s_ones1 = sb("s_ones1", [1, 128], F32)
    s_onesc = sb("s_onesc", [128, 1], F32)
    s_onesb = sb("s_onesb", [128, 1], BF16)
    s_m128 = sb("s_m128", [1, 128], BF16)
    s_normb = sb("s_normb", [128, 512], F32)
    s_onesq = sb("s_onesq", [128, 128], F32)
    s_rsel = sb("s_rsel", [c.NVH, 128 * c.NVH], F32)
    s_convw = sb("s_convw", [128, 4 * (c.WC // 128)], F32)
    s_sg2 = sb("s_sg2", [128, c.NHC], F32)
    s_su2 = sb("s_su2", [128, c.NHC], F32)
    s_sd2 = sb("s_sd2", [128, c.OUTC], F32)

    CST = {1e-6: 0, 128.0: 1, 128e-6: 2, 1.0 / c.D: 3, 1e-5: 4,
           1.0 / 128: 5, 1.0: 6, -1.0: 7}

    def cst(v, p=128):
        return s_cst[0:p, CST[v]:CST[v] + 1]

    k.dma("sync", "d_c", [
        (lambda e, o=o_, i=i_: e.dma_start(out=o, in_=i))
        for o_, i_ in [
            (s_eyebf[:, :], eyebf[:, :]), (s_eyef[:, :], eyef[:, :]),
            (s_ltri[:, :], ltri[:, :]), (s_maskE[:, :], maskE[:, :]),
            (s_mask2[:, :], mask2[:, :]), (s_mask01[:, :], mask01[:, :]),
            (s_normw[:, :], normw4[:, :]), (s_alog[:, :], alog[:, :]),
            (s_dtb[:, :], dtb[:, :]), (s_sg2[:, :], sg2[:, :]),
            (s_su2[:, :], su2[:, :]), (s_sd2[:, :], sd2[:, :]),
            (s_rsel[:, :], rowsel[:, :]),
        ] + [(s_convw[:, 4 * f:4 * f + 4], convw[128 * f:128 * (f + 1), :])
             for f in range(c.WC // 128)]
    ], writes=["consts"])
    for val, col in CST.items():
        k.emit("vector", (lambda e, v=val, cl=col: e.memset(
            s_cst[:, cl:cl + 1], v)), writes=["consts"])
    k.emit("vector", lambda e: e.memset(s_ones1[:, :], 1.0), writes=["consts"])
    k.emit("vector", lambda e: e.memset(s_onesc[:, :], 1.0), writes=["consts"])
    k.emit("vector", lambda e: e.memset(s_onesb[:, :], 1.0), writes=["consts"])
    k.emit("vector", lambda e: e.memset(s_m128[:, :], -128.0), writes=["consts"])
    k.emit("vector", lambda e: e.memset(s_onesq[:, :], 1.0), writes=["consts"])
    k.emit("scalar", lambda e: e.activation(s_nega[:, :], s_alog[:, :], AF.Exp),
           reads=["consts"], writes=["nega"])
    k.emit("vector", lambda e: e.tensor_scalar(
        s_nega[:, :], s_nega[:, :], -1.0, None, ALU.mult),
        reads=["nega"], writes=["nega"])
    k.emit("tensor", lambda e: e.matmul(
        PA[:, 0:512], s_ones1[:, :], s_normw[:, :], start=True, stop=True),
        reads=["consts"], writes=["PA"])
    k.emit("vector", lambda e: e.tensor_copy(s_normb[:, :], PA[:, 0:512]),
           reads=["PA"], writes=["normb"])

    stP1 = ExitStack()
    xl = [stP1.enter_context(nc.sbuf_tensor(f"xln{i}", [128, c.D], F32))
          for i in range(2)]
    xb = [stP1.enter_context(nc.sbuf_tensor(f"xnb{i}", [128, c.D], BF16))
          for i in range(2)]
    dumpb = stP1.enter_context(nc.sbuf_tensor("dumpb", [128, c.D], F32))
    lncol = stP1.enter_context(nc.sbuf_tensor("lncol", [128, 4], F32))

    def ln_block(src_ap, dst_ap, p, keys):
        xk, bk = f"xl{p}", f"xb{p}"
        xlp, xbp = xl[p], xb[p]
        k.dma("sync", f"d_x{p}", (lambda e: e.dma_start(
            out=xlp[:, :], in_=src_ap)), writes=[xk])
        k.emit("vector", lambda e: e.reduce_sum(
            lncol[:, 0:1], xlp[:, :], axis=AX.X),
            reads=[xk], writes=["lncol"])
        k.emit("scalar", lambda e: e.activation(
            lncol[:, 1:2], lncol[:, 0:1], AF.Copy, scale=cst(1.0 / c.D)),
            reads=["lncol", "consts"], writes=["lncol"])
        k.emit("vector", lambda e: e.tensor_scalar(
            xlp[:, :], xlp[:, :], lncol[:, 1:2], None, ALU.subtract),
            reads=[xk, "lncol"], writes=[xk])
        k.emit("scalar", lambda e: e.activation(
            dumpb[:, :], xlp[:, :], AF.Square, accum_out=lncol[:, 2:3]),
            reads=[xk], writes=["dumpb", "lncol"])
        k.emit("scalar", lambda e: e.activation(
            lncol[:, 3:4], lncol[:, 2:3], AF.Sqrt,
            scale=cst(1.0 / c.D), bias=cst(1e-5)),
            reads=["lncol", "consts"], writes=["lncol"])
        k.emit("vector", lambda e: e.reciprocal(
            lncol[:, 3:4], lncol[:, 3:4]),
            reads=["lncol"], writes=["lncol"])
        k.emit("vector", lambda e: e.tensor_scalar(
            xbp[:, :], xlp[:, :], lncol[:, 3:4], None, ALU.mult),
            reads=[xk, "lncol"], writes=[bk])
        k.dma("sync", f"d_xw{p}", (lambda e: e.dma_start(
            out=dst_ap, in_=xbp[:, :])), reads=[bk], writes=keys)

    for tb in range(c.NTB):
        ln_block(xfull[128 * tb:128 * (tb + 1), :],
                 xn_sc[128 * tb:128 * (tb + 1), :], tb % 2, ["xn_sc"])
    for qb in range(c.NQB):
        ln_block(xq[128 * qb:128 * (qb + 1), :],
                 xnq_sc[128 * qb:128 * (qb + 1), :], qb % 2, ["xnq_sc"])
    k.fence()
    stP1.close()

    def finish_dummy():
        sfin = st.enter_context(nc.sbuf_tensor("s_dummy", [128, c.TQ], F32))
        k.emit("vector", lambda e: e.memset(sfin[:, :], 0.0), writes=["dumz"])
        k.dma("sync", "d_out", [
            (lambda e, o=o: e.dma_start(
                out=outfm[128 * o:128 * (o + 1), :], in_=sfin[:, :]))
            for o in range(c.OUTC)], reads=["dumz"], writes=["outz"])
        k._wait("gpsimd", "d_out", k.count["d_out"])
        semh = {}
        for name in sorted(set(list(k.count.keys()) + list(K.ESEM.values()))):
            semh[name] = st.enter_context(nc.semaphore(name))
        k.replay(nc, semh)
        st.close()
        return nc

    if upto <= 1:
        return finish_dummy()

    QKV_OC = c.WC // 128
    NWG = (QKV_OC + 3) // 4

    stP2 = ExitStack()
    sbp = lambda n, s, d: stP2.enter_context(nc.sbuf_tensor(n, s, d))
    s_xnf = [sbp(f"s_xnf{f}", [128, c.TILE], BF16) for f in range(c.NF)]
    s_wgF = [sbp(f"s_wgF{f}", [128, c.WC], BF16) for f in range(c.NF)]
    s_wz = [[sbp(f"s_wz{z}_{f}", [128, c.ZW], BF16) for f in range(c.NF)]
            for z in range(c.NZC)]
    s_wab = [sbp(f"s_wab{f}", [128, c.ABW], BF16) for f in range(c.NF)]
    s_praw = [sbp(f"s_praw{i}", [128, c.TILE + 3], BF16) for i in range(2)]
    s_hist = sbp("s_hist", [128, 3 * QKV_OC], BF16)
    s_qkvs = sbp("s_qkvs", [128, QKV_OC * c.TILE], BF16)
    s_cacc = [sbp(f"s_cacc{i}", [128, c.TILE], F32) for i in range(2)]
    s_zg = sbp("s_zg", [128, c.VW], F32)
    s_gc = sbp("s_gc", [c.NVH, c.TILE], F32)
    s_bc = sbp("s_bc", [c.NVH, c.TILE], F32)
    s_gtm = sbp("s_gtm", [128, c.NVH], F32)
    s_btm = sbp("s_btm", [128, c.NVH], F32)
    s_G = sbp("s_G", [128, c.NVH], F32)
    s_Gfm = sbp("s_Gfm", [c.NVH, 128], F32)
    s_gam = sbp("s_gam", [128, 4 * c.NVH], F32)
    s_nbf = sbp("s_nbf", [c.NVH, c.TILE], F32)
    s_sks = [sbp(f"s_sks{i}", [1, 256], F32) for i in range(2)]
    s_ktf = [sbp(f"s_ktf{i}", [128, 128], BF16) for i in range(4)]
    s_qtf = [sbp(f"s_qtf{i}", [128, 128], BF16) for i in range(4)]
    s_ktm = [sbp(f"s_ktm{i}", [128, 128], BF16) for i in range(4)]
    s_ksq = [sbp(f"s_ksq{i}", [128, 256], F32) for i in range(2)]
    s_kk = [sbp(f"s_kk{i}", [128, 256], BF16) for i in range(4)]
    s_gbc = sbp("s_gbc", [128, 128 * c.NVH], F32)
    s_bbc = sbp("s_bbc", [128, 128 * c.NVH], BF16)
    s_argE = [sbp(f"s_argE{i}", [128, 128], F32) for i in range(4)]
    s_argN = [sbp(f"s_argN{i}", [128, 128], F32) for i in range(4)]
    s_es = [sbp(f"s_es{i}", [128, 128], F32) for i in range(4)]
    s_ntm = [sbp(f"s_ntm{i}", [128, 128], F32) for i in range(4)]
    s_TT = [[sbp(f"s_TT{i}_{j}", [128, 128], BF16) for j in range(2)]
            for i in range(4)]
    s_P = [[sbp(f"s_P{i}_{j}", [128, 128], BF16) for j in range(2)]
           for i in range(4)]
    s_PT = [[sbp(f"s_PT{i}_{j}", [128, 128], BF16) for j in range(2)]
            for i in range(4)]
    s_A2T = [sbp(f"s_A2T{i}", [128, 128], BF16) for i in range(4)]
    s_RHS = [sbp(f"s_RHS{i}", [128, 128], BF16) for i in range(4)]
    s_Xbf = [sbp(f"s_Xbf{i}", [128, 128], BF16) for i in range(4)]
    s_Xw = [sbp(f"s_Xw{i}", [128, 128], BF16) for i in range(4)]
    s_Otm = [sbp(f"s_Otm{i}", [128, 128], F32) for i in range(4)]
    s_dump = [sbp(f"s_dump{i}", [128, 128], BF16) for i in range(4)]
    s_oc2 = [sbp(f"s_oc2{i}", [128, 2], F32) for i in range(4)]
    s_S = [sbp(f"s_S{h}", [128, 128], F32) for h in range(c.NVH)]
    s_Sbf = [sbp(f"s_Sbf{h}", [128, 128], BF16) for h in range(c.NVH)]
    s_ocb = [sbp(f"s_ocb{i}", [128, 128], BF16) for i in range(4)]

    for h in range(c.NVH):
        k.emit("vector", (lambda e, h=h: e.memset(s_S[h][:, :], 0.0)),
               writes=[f"S{h}"])
        k.emit("vector", (lambda e, h=h: e.memset(s_Sbf[h][:, :], 0.0)),
               writes=[f"Sbf{h}"])
    k.emit("vector", lambda e: e.memset(s_hist[:, :], 0.0),
           writes=[f"hist{oc}" for oc in range(QKV_OC)])

    def load_xnf(t):
        k.dma("sync", "d_xnr", [
            (lambda e, f=f, t=t: e.dma_start(
                out=s_xnf[f][:, :],
                in_=xn_sc[t * c.TILE:(t + 1) * c.TILE, 128 * f:128 * (f + 1)],
                transpose=True))
            for f in range(c.NF)],
            reads=["xn_sc"], writes=[f"xnf{f}" for f in range(c.NF)])

    def load_wg_once():
        k.dma("gpsimd", "d_wq", [
            (lambda e, f=f: e.dma_start(out=s_wgF[f][:, :], in_=wgc[f]))
            for f in range(c.NF)], writes=["wgF"])

    def load_wz(z):
        k.dma("gpsimd", f"d_wz{z}", [
            (lambda e, f=f, z=z: e.dma_start(
                out=s_wz[z][f][:, :], in_=wzc[f, z]))
            for f in range(c.NF)], writes=[f"wz{z}"])

    def load_wab():
        k.dma("gpsimd", "d_wab", [
            (lambda e, f=f: e.dma_start(
                out=s_wab[f][:, :], in_=wabc[f]))
            for f in range(c.NF)], writes=["wab"])

    PBANK = [PP0, PP1, PA, PB]
    PBK = ["PP0", "PP1", "PA", "PB"]
    xnf_keys = [f"xnf{f}" for f in range(c.NF)]

    load_xnf(0)
    load_wg_once()
    for z in range(c.NZC):
        load_wz(z)
    load_wab()

    for t in range(c.NT):
        for g in range(NWG):
            ocs = list(range(4 * g, min(4 * g + 4, QKV_OC)))
            for j, oc in enumerate(ocs):
                bank, bkey = PBANK[j], PBK[j]
                par = oc % 2
                k.emit("vector", (lambda e, oc=oc, par=par: e.tensor_copy(
                    s_praw[par][:, 0:3], s_hist[:, 3 * oc:3 * oc + 3])),
                    reads=[f"hist{oc}"], writes=[f"praw{par}"])
                k.emit("tensor", [
                    (lambda e, f=f, oc=oc, bank=bank: e.matmul(
                        bank[:, 0:c.TILE],
                        s_wgF[f][:, 128 * oc:128 * (oc + 1)],
                        s_xnf[f][:, :],
                        start=(f == 0), stop=(f == c.NF - 1)))
                    for f in range(c.NF)],
                    reads=["wgF"] + xnf_keys, writes=[bkey])
                k.emit("scalar", (lambda e, bank=bank, par=par: e.activation(
                    s_praw[par][:, 3:3 + c.TILE], bank[:, 0:c.TILE],
                    AF.Copy)),
                    reads=[bkey], writes=[f"praw{par}"])
                k.emit("vector", (lambda e, oc=oc, par=par: e.tensor_copy(
                    s_hist[:, 3 * oc:3 * oc + 3],
                    s_praw[par][:, c.TILE:c.TILE + 3])),
                    reads=[f"praw{par}"], writes=[f"hist{oc}"])
                k.emit("vector", (lambda e, oc=oc, par=par: e.tensor_scalar(
                    s_cacc[par][:, :], s_praw[par][:, 0:c.TILE],
                    s_convw[:, 4 * oc:4 * oc + 1], None, ALU.mult)),
                    reads=[f"praw{par}", "consts"], writes=[f"cacc{par}"])
                for tap in range(1, 4):
                    k.emit("vector", (lambda e, oc=oc, par=par, tap=tap:
                        e.scalar_tensor_tensor(
                            s_cacc[par][:, :],
                            s_praw[par][:, tap:tap + c.TILE],
                            s_convw[:, 4 * oc + tap:4 * oc + tap + 1],
                            s_cacc[par][:, :], ALU.mult, ALU.add)),
                        reads=[f"praw{par}", "consts", f"cacc{par}"],
                        writes=[f"cacc{par}"])
                k.emit("scalar", (lambda e, oc=oc, par=par: e.activation(
                    s_qkvs[:, oc * c.TILE:(oc + 1) * c.TILE],
                    s_cacc[par][:, :], AF.Silu)),
                    reads=[f"cacc{par}"], writes=[f"qkvs{oc}"])
        if p2lvl < 12:
            continue
        for h2 in range(c.TILE // c.W2):
            co = c.W2 * h2
            k.emit("tensor", [
                (lambda e, f=f, co=co: e.matmul(
                    PA[0:c.NVH, 0:c.W2], s_wab[f][:, 0:c.NVH],
                    s_xnf[f][:, co:co + c.W2],
                    start=(f == 0), stop=(f == c.NF - 1)))
                for f in range(c.NF)],
                reads=["wab"] + xnf_keys, writes=["PA"])
            k.emit("tensor", [
                (lambda e, f=f, co=co: e.matmul(
                    PB[0:c.NVH, 0:c.W2], s_wab[f][:, c.NVH:c.ABW],
                    s_xnf[f][:, co:co + c.W2],
                    start=(f == 0), stop=(f == c.NF - 1)))
                for f in range(c.NF)],
                reads=["wab"] + xnf_keys, writes=["PB"])
            k.emit("scalar", (lambda e, co=co: e.activation(
                s_gc[:, co:co + c.W2], PA[0:c.NVH, 0:c.W2], AF.Exp,
                bias=s_dtb[:, 0:1])),
                reads=["PA", "consts"], writes=["gbc"])
            k.emit("scalar", (lambda e, co=co: e.add(
                s_gc[:, co:co + c.W2], s_gc[:, co:co + c.W2], 1.0)),
                reads=["gbc", "consts"], writes=["gbc"])
            k.emit("scalar", (lambda e, co=co: e.activation(
                s_gc[:, co:co + c.W2], s_gc[:, co:co + c.W2],
                AF.Ln)), reads=["gbc"], writes=["gbc"])
            k.emit("vector", (lambda e, co=co: e.tensor_scalar(
                s_gc[:, co:co + c.W2], s_gc[:, co:co + c.W2],
                s_nega[:, 0:1], None, ALU.mult)),
                reads=["gbc", "nega"], writes=["gbc"])
            k.emit("scalar", (lambda e, co=co: e.activation(
                s_bc[:, co:co + c.W2], PB[0:c.NVH, 0:c.W2],
                AF.Sigmoid)), reads=["PB"], writes=["gbc"])
            k.emit("vector", (lambda e, co=co: e.tensor_scalar(
                s_nbf[:, co:co + c.W2], s_bc[:, co:co + c.W2],
                -1.0, None, ALU.mult)),
                reads=["gbc"], writes=["nbf"])
        if p2lvl < 13:
            continue
        for cb in range(c.NCH):
            tok0 = cb * 128
            for z in range(c.NZC):
                zb = PA if z % 2 == 0 else PB
                zbk = "PA" if z % 2 == 0 else "PB"
                k.emit("tensor", [
                    (lambda e, f=f, cb=cb, z=z, zb=zb: e.matmul(
                        zb[:, 0:c.ZW],
                        s_xnf[f][:, 128 * cb:128 * (cb + 1)],
                        s_wz[z][f][:, :],
                        start=(f == 0), stop=(f == c.NF - 1)))
                    for f in range(c.NF)],
                    reads=[f"wz{z}"] + xnf_keys, writes=[zbk])
                k.emit("scalar", (lambda e, z=z, zb=zb: e.activation(
                    s_zg[:, c.ZW * z:c.ZW * (z + 1)],
                    zb[:, 0:c.ZW], AF.Silu)),
                    reads=[zbk], writes=["zg"])
                k.emit("vector", (lambda e, z=z: e.tensor_tensor(
                    s_zg[:, c.ZW * z:c.ZW * (z + 1)],
                    s_zg[:, c.ZW * z:c.ZW * (z + 1)],
                    s_normb[:, 0:c.ZW], ALU.mult)),
                    reads=["zg", "normb"], writes=["zg"])
            if cb == c.NCH - 1 and t < c.NT - 1:
                load_xnf(t + 1)
            k.emit("tensor", (lambda e, tok0=tok0: e.transpose(
                PB[0:128, 0:c.NVH], s_gc[:, tok0:tok0 + 128],
                s_eyef[0:c.NVH, 0:c.NVH])),
                reads=["gbc", "consts"], writes=["PB"])
            k.emit("tensor", (lambda e, tok0=tok0: e.transpose(
                PB[0:128, 64:64 + c.NVH], s_bc[:, tok0:tok0 + 128],
                s_eyef[0:c.NVH, 0:c.NVH])),
                reads=["gbc", "consts"], writes=["PB"])
            k.emit("vector", lambda e: e.tensor_copy(
                s_gtm[:, :], PB[0:128, 0:c.NVH]),
                reads=["PB"], writes=["gbtm"])
            k.emit("vector", lambda e: e.tensor_copy(
                s_btm[:, :], PB[0:128, 64:64 + c.NVH]),
                reads=["PB"], writes=["gbtm"])
            k.emit("tensor", lambda e: e.matmul(
                PB[:, 128:128 + c.NVH], s_ltri[:, :], s_gtm[:, :],
                start=True, stop=True),
                reads=["gbtm", "consts"], writes=["PB"])
            k.emit("vector", lambda e: e.tensor_copy(
                s_G[:, :], PB[:, 128:128 + c.NVH]),
                reads=["PB"], writes=["G"])
            k.emit("tensor", lambda e: e.matmul(
                PB[0:c.NVH, 256:384], s_gtm[:, :], s_ltri[:, :],
                start=True, stop=True),
                reads=["gbtm", "consts"], writes=["PB"])
            k.emit("vector", lambda e: e.tensor_copy(
                s_Gfm[:, :], PB[0:c.NVH, 256:384]),
                reads=["PB"], writes=["Gfm"])
            k.emit("tensor", lambda e: e.matmul(
                PB[:, 384:384 + c.NVH], s_onesq[:, :],
                s_gtm[:, :], start=True, stop=True),
                reads=["gbtm", "consts"], writes=["PB"])
            k.emit("vector", lambda e: e.tensor_tensor(
                s_gam[:, 0:c.NVH], PB[:, 384:384 + c.NVH], s_G[:, :],
                ALU.subtract), reads=["G", "PB"], writes=["gam"])
            k.emit("scalar", lambda e: e.activation(
                s_gam[:, 0:c.NVH], s_gam[:, 0:c.NVH], AF.Exp),
                reads=["gam"], writes=["gam"])
            k.emit("scalar", lambda e: e.activation(
                s_gam[:, c.NVH:2 * c.NVH], PB[:, 384:384 + c.NVH], AF.Exp),
                reads=["PB"], writes=["gam"])
            k.emit("scalar", lambda e: e.activation(
                s_gam[:, 2 * c.NVH:3 * c.NVH], s_G[:, :], AF.Exp),
                reads=["G"], writes=["gam"])
            k.emit("vector", lambda e: e.tensor_tensor(
                s_gam[:, 3 * c.NVH:4 * c.NVH], s_gam[:, 2 * c.NVH:3 * c.NVH],
                s_btm[:, :], ALU.mult),
                reads=["gam", "gbtm"], writes=["gam"])
            k.emit("vector", lambda e: e.tensor_scalar(
                s_gam[:, 3 * c.NVH:4 * c.NVH], s_gam[:, 3 * c.NVH:4 * c.NVH],
                -1.0, None, ALU.mult), reads=["gam"], writes=["gam"])

            if p2lvl < 14:
                continue
            for half in range(c.NVH // 4):
                k.emit("tensor", [
                    (lambda e, hh=hh, half=half: e.matmul(
                        PB[:, 128 * (hh - 4 * half):128 * (hh - 4 * half + 1)],
                        s_rsel[:, 128 * hh:128 * (hh + 1)], s_Gfm[:, :],
                        start=True, stop=True))
                    for hh in range(4 * half, 4 * half + 4)],
                    reads=["Gfm", "consts"], writes=["PB"])
                k.emit("vector", (lambda e, half=half: e.tensor_copy(
                    s_gbc[:, 512 * half:512 * (half + 1)], PB[:, 0:512])),
                    reads=["PB"], writes=["gbc"])
                k.emit("tensor", [
                    (lambda e, hh=hh, half=half, tok0=tok0: e.matmul(
                        PB[:, 128 * (hh - 4 * half):128 * (hh - 4 * half + 1)],
                        s_rsel[:, 128 * hh:128 * (hh + 1)],
                        s_nbf[:, tok0:tok0 + 128],
                        start=True, stop=True))
                    for hh in range(4 * half, 4 * half + 4)],
                    reads=["nbf", "consts"], writes=["PB"])
                k.emit("vector", (lambda e, half=half: e.tensor_copy(
                    s_bbc[:, 512 * half:512 * (half + 1)], PB[:, 0:512])),
                    reads=["PB"], writes=["bbc"])
            NB = [PA, PB]
            NBK = ["PA", "PB"]
            LB = [PC, PD, PP0, PP1]
            LBK = ["PC", "PD", "PP0", "PP1"]
            TRB = [PF0, PF1]
            TRK = ["PF0", "PF1"]

            def kh_gen(kh):
                kp = kh % 2
                nb, nbk = NB[kp], NBK[kp]
                qoc, koc = kh, c.NKH + kh
                qsl = s_qkvs[:, qoc * c.TILE + tok0:qoc * c.TILE + tok0 + 128]
                ksl = s_qkvs[:, koc * c.TILE + tok0:koc * c.TILE + tok0 + 128]
                k.emit("scalar", (lambda e, ksl=ksl, kp=kp: e.activation(
                    s_ksq[kp][:, 0:128], ksl, AF.Square)),
                    reads=[f"qkvs{koc}"], writes=[f"ksq{kp}"])
                yield
                k.emit("scalar", (lambda e, qsl=qsl, kp=kp: e.activation(
                    s_ksq[kp][:, 128:256], qsl, AF.Square)),
                    reads=[f"qkvs{qoc}"], writes=[f"ksq{kp}"])
                yield
                k.emit("tensor", [
                    (lambda e, kp=kp, nb=nb: e.matmul(
                        nb[0:1, 256:384], s_onesc[:, :],
                        s_ksq[kp][:, 0:128], start=True, stop=True)),
                    (lambda e, kp=kp, nb=nb: e.matmul(
                        nb[0:1, 384:512], s_onesc[:, :],
                        s_ksq[kp][:, 128:256], start=True, stop=True))],
                    reads=[f"ksq{kp}", "consts"], writes=[nbk])
                yield
                k.emit("scalar", (lambda e, kp=kp, nb=nb: e.activation(
                    s_sks[kp][:, 0:128], nb[0:1, 256:384], AF.Sqrt,
                    bias=cst(1e-6, 1))),
                    reads=[nbk, "consts"], writes=[f"sks{kp}"])
                yield
                k.emit("scalar", (lambda e, kp=kp, nb=nb: e.activation(
                    s_sks[kp][:, 128:256], nb[0:1, 384:512], AF.Sqrt,
                    scale=cst(128.0, 1), bias=cst(128e-6, 1))),
                    reads=[nbk, "consts"], writes=[f"sks{kp}"])
                yield
                k.emit("vector", (lambda e, kp=kp: e.reciprocal(
                    s_sks[kp][:, :], s_sks[kp][:, :])),
                    reads=[f"sks{kp}"], writes=[f"sks{kp}"])
                yield
                k.emit("tensor", [
                    (lambda e, kp=kp, nb=nb: e.matmul(
                        nb[:, 256:384], s_ones1[:, :],
                        s_sks[kp][:, 0:128], start=True, stop=True)),
                    (lambda e, kp=kp, nb=nb: e.matmul(
                        nb[:, 384:512], s_ones1[:, :],
                        s_sks[kp][:, 128:256], start=True, stop=True))],
                    reads=[f"sks{kp}", "consts"], writes=[nbk])
                yield
                k.emit("vector", (lambda e, ksl=ksl, kh=kh, nb=nb:
                    e.tensor_tensor(
                        s_ktf[kh][:, :], ksl, nb[:, 256:384], ALU.mult)),
                    reads=[f"qkvs{koc}", nbk], writes=[f"ktf{kh}"])
                yield
                k.emit("vector", (lambda e, qsl=qsl, kh=kh, nb=nb:
                    e.tensor_tensor(
                        s_qtf[kh][:, :], qsl, nb[:, 384:512], ALU.mult)),
                    reads=[f"qkvs{qoc}", nbk], writes=[f"qtf{kh}"])
                yield
                k.emit("tensor", (lambda e, kh=kh, kp=kp: e.transpose(
                    TRB[kp][:, 256:384], s_ktf[kh][:, :], s_eyebf[:, :])),
                    reads=[f"ktf{kh}", "consts"], writes=[TRK[kp]])
                yield
                k.emit("vector", (lambda e, kh=kh, kp=kp: e.tensor_copy(
                    s_ktm[kh][:, :], TRB[kp][:, 256:384])),
                    reads=[TRK[kp]], writes=[f"ktm{kh}"])
                yield
                k.emit("tensor", [
                    (lambda e, kh=kh, nb=nb: e.matmul(
                        nb[:, 0:128], s_ktf[kh][:, :], s_ktf[kh][:, :],
                        start=True, stop=True)),
                    (lambda e, kh=kh, nb=nb: e.matmul(
                        nb[:, 128:256], s_ktf[kh][:, :], s_qtf[kh][:, :],
                        start=True, stop=True))],
                    reads=[f"ktf{kh}", f"qtf{kh}"], writes=[nbk])
                yield
                k.emit("vector", (lambda e, kh=kh, nb=nb: e.tensor_copy(
                    s_kk[kh][:, :], nb[:, 0:256])),
                    reads=[nbk], writes=[f"kk{kh}"])
                yield

            def head_gen(hh):
                hp = hh % 2
                ln = hh % 4
                kh = hh // 2
                lb, lbk = LB[ln], LBK[ln]
                tr, trk = TRB[hp], TRK[hp]
                voc = 2 * c.NKH + hh
                for _ in range(ln // 2):
                    yield
                gsl = s_gbc[:, 128 * hh:128 * (hh + 1)]
                bsl = s_bbc[:, 128 * hh:128 * (hh + 1)]
                k.emit("vector", (lambda e, hh=hh, ln=ln, gsl=gsl:
                    e.scalar_tensor_tensor(
                        s_argE[ln][:, :], gsl, s_G[:, hh:hh + 1],
                        s_maskE[:, :], ALU.subtract, ALU.add)),
                    reads=["gbc", "G", "consts"], writes=[f"argE{ln}"])
                yield
                k.emit("vector", (lambda e, hh=hh, ln=ln, gsl=gsl:
                    e.scalar_tensor_tensor(
                        s_argN[ln][:, :], gsl, s_G[:, hh:hh + 1],
                        s_mask2[:, :], ALU.subtract, ALU.add)),
                    reads=["gbc", "G", "consts"], writes=[f"argN{ln}"])
                yield
                k.emit("scalar", (lambda e, ln=ln: e.activation(
                    s_argE[ln][:, :], s_argE[ln][:, :], AF.Exp)),
                    reads=[f"argE{ln}"], writes=[f"argE{ln}"])
                yield
                k.emit("scalar", (lambda e, ln=ln: e.activation(
                    s_argN[ln][:, :], s_argN[ln][:, :], AF.Exp,
                    scale=cst(-1.0))),
                    reads=[f"argN{ln}", "consts"], writes=[f"argN{ln}"])
                yield
                if p2lvl < 1520:
                    return
                k.emit("vector", (lambda e, ln=ln, kh=kh: e.tensor_tensor(
                    s_A2T[ln][:, :], s_argE[ln][:, :],
                    s_kk[kh][:, 128:256], ALU.mult)),
                    reads=[f"argE{ln}", f"kk{kh}"], writes=[f"A2T{ln}"])
                yield
                k.emit("vector", (lambda e, ln=ln: e.tensor_tensor(
                    s_es[ln][:, :], s_argE[ln][:, :], s_mask01[:, :],
                    ALU.mult)),
                    reads=[f"argE{ln}", "consts"], writes=[f"es{ln}"])
                yield
                k.emit("vector", (lambda e, ln=ln, kh=kh: e.tensor_tensor(
                    s_ntm[ln][:, :], s_es[ln][:, :], s_kk[kh][:, 0:128],
                    ALU.mult)),
                    reads=[f"es{ln}", f"kk{kh}"], writes=[f"ntm{ln}"])
                yield
                k.emit("vector", (lambda e, ln=ln, bsl=bsl: e.tensor_tensor(
                    s_PT[ln][0][:, :], s_ntm[ln][:, :], bsl, ALU.mult)),
                    reads=[f"ntm{ln}", "bbc"], writes=[f"PT{ln}0"])
                yield
                k.emit("vector", (lambda e, ln=ln, kh=kh: e.tensor_tensor(
                    s_es[ln][:, :], s_argN[ln][:, :], s_kk[kh][:, 0:128],
                    ALU.mult)),
                    reads=[f"argN{ln}", f"kk{kh}"], writes=[f"es{ln}"])
                yield
                k.emit("vector", (lambda e, hh=hh, ln=ln: e.tensor_scalar(
                    s_P[ln][0][:, :], s_es[ln][:, :],
                    s_btm[:, hh:hh + 1], -1.0,
                    ALU.mult, ALU.mult)),
                    reads=[f"es{ln}", "gbtm"], writes=[f"P{ln}0"])
                yield
                if p2lvl < 1530:
                    return
                k.emit("vector", (lambda e, ln=ln: e.tensor_tensor(
                    s_TT[ln][0][:, :], s_PT[ln][0][:, :], s_eyebf[:, :],
                    ALU.add)),
                    reads=[f"PT{ln}0", "consts"], writes=[f"TT{ln}0"])
                yield
                pp = 0
                for lv in range(1, c.NLEV + 1):
                    np_, op_ = 1 - pp, pp
                    last = (lv == c.NLEV)
                    k.emit("tensor", (lambda e, op_=op_, ln=ln, lb=lb:
                        e.matmul(
                            lb[:, 0:128], s_PT[ln][op_][:, :],
                            s_P[ln][op_][:, :], start=True, stop=True)),
                        reads=[f"P{ln}{op_}", f"PT{ln}{op_}"], writes=[lbk])
                    yield
                    if not last:
                        k.emit("tensor", (lambda e, op_=op_, ln=ln, lb=lb:
                            e.matmul(
                                lb[:, 128:256], s_P[ln][op_][:, :],
                                s_PT[ln][op_][:, :],
                                start=True, stop=True)),
                            reads=[f"P{ln}{op_}", f"PT{ln}{op_}"],
                            writes=[lbk])
                        yield
                    k.emit("scalar", (lambda e, np_=np_, ln=ln, lb=lb:
                        e.activation(
                            s_P[ln][np_][:, :], lb[:, 0:128], AF.Copy)),
                        reads=[lbk], writes=[f"P{ln}{np_}"])
                    yield
                    if not last:
                        k.emit("vector", (lambda e, np_=np_, ln=ln, lb=lb:
                            e.tensor_copy(
                                s_PT[ln][np_][:, :], lb[:, 128:256])),
                            reads=[lbk], writes=[f"PT{ln}{np_}"])
                        yield
                    k.emit("tensor", (lambda e, np_=np_, op_=op_, ln=ln,
                                      lb=lb: e.matmul(
                        lb[:, 256:384], s_P[ln][np_][:, :],
                        s_TT[ln][op_][:, :], start=True, stop=True)),
                        reads=[f"P{ln}{np_}", f"TT{ln}{op_}"], writes=[lbk])
                    yield
                    k.emit("vector", (lambda e, np_=np_, op_=op_, ln=ln,
                                      lb=lb: e.tensor_tensor(
                        s_TT[ln][np_][:, :], lb[:, 256:384],
                        s_TT[ln][op_][:, :], ALU.add)),
                        reads=[lbk, f"TT{ln}{op_}"], writes=[f"TT{ln}{np_}"])
                    yield
                    pp = np_
                if p2lvl < 1540:
                    return
                k.emit("tensor", (lambda e, hh=hh, ln=ln, kh=kh, lb=lb:
                    e.matmul(
                        lb[:, 0:128], s_ktf[kh][:, :], s_Sbf[hh][:, :],
                        start=True, stop=True)),
                    reads=[f"ktf{kh}", f"Sbf{hh}"], writes=[lbk])
                yield
                k.emit("tensor", (lambda e, voc=voc, tok0=tok0, tr=tr:
                    e.transpose(
                        tr[:, 0:128],
                        s_qkvs[:, voc * c.TILE + tok0:
                               voc * c.TILE + tok0 + 128],
                        s_eyebf[:, :])),
                    reads=[f"qkvs{voc}", "consts"], writes=[trk])
                yield
                k.emit("scalar", (lambda e, hh=hh, ln=ln, tr=tr:
                    e.activation(
                        s_dump[ln][:, :], tr[:, 0:128], AF.Copy,
                        scale=s_btm[:, hh:hh + 1])),
                    reads=[trk, "gbtm"], writes=[f"dump{ln}"])
                yield
                k.emit("vector", (lambda e, hh=hh, ln=ln, lb=lb:
                    e.scalar_tensor_tensor(
                        s_RHS[ln][:, :], lb[:, 0:128],
                        s_gam[:, 3 * c.NVH + hh:3 * c.NVH + hh + 1],
                        s_dump[ln][:, :], ALU.mult, ALU.add)),
                    reads=[lbk, "gam", f"dump{ln}"], writes=[f"RHS{ln}"])
                yield
                if p2lvl < 1550:
                    return
                k.emit("tensor", (lambda e, pp=pp, ln=ln, lb=lb: e.matmul(
                    lb[:, 384:512], s_TT[ln][pp][:, :], s_RHS[ln][:, :],
                    start=True, stop=True)),
                    reads=[f"TT{ln}{pp}", f"RHS{ln}"], writes=[lbk])
                yield
                k.emit("scalar", (lambda e, ln=ln, lb=lb: e.activation(
                    s_Xbf[ln][:, :], lb[:, 384:512], AF.Copy)),
                    reads=[lbk], writes=[f"Xbf{ln}"])
                yield
                k.emit("vector", (lambda e, hh=hh, ln=ln, lb=lb:
                    e.tensor_scalar(
                        s_Xw[ln][:, :], lb[:, 384:512],
                        s_gam[:, hh:hh + 1], None, ALU.mult)),
                    reads=[lbk, "gam"], writes=[f"Xw{ln}"])
                yield
                if p2lvl < 1560:
                    return
                k.emit("tensor", (lambda e, hh=hh, ln=ln, kh=kh, lb=lb:
                    e.matmul(
                        lb[:, 128:256], s_qtf[kh][:, :], s_Sbf[hh][:, :],
                        start=True, stop=True)),
                    reads=[f"qtf{kh}", f"Sbf{hh}"], writes=[lbk])
                yield
                k.emit("tensor", (lambda e, ln=ln, lb=lb: e.matmul(
                    lb[:, 256:384], s_A2T[ln][:, :], s_Xbf[ln][:, :],
                    start=True, stop=True)),
                    reads=[f"A2T{ln}", f"Xbf{ln}"], writes=[lbk])
                yield
                k.emit("scalar", (lambda e, hh=hh, ln=ln, lb=lb:
                    e.activation(
                        s_es[ln][:, :], lb[:, 128:256], AF.Copy,
                        scale=s_gam[:, 2 * c.NVH + hh:
                                    2 * c.NVH + hh + 1])),
                    reads=[lbk, "gam"], writes=[f"es{ln}"])
                yield
                k.emit("vector", (lambda e, ln=ln, lb=lb: e.tensor_tensor(
                    s_Otm[ln][:, :], s_es[ln][:, :], lb[:, 256:384],
                    ALU.add)),
                    reads=[f"es{ln}", lbk], writes=[f"Otm{ln}"])
                yield
                if p2lvl < 1570:
                    return
                k.emit("tensor", (lambda e, ln=ln, kh=kh, lb=lb: e.matmul(
                    lb[:, 0:128], s_ktm[kh][:, :], s_Xw[ln][:, :],
                    start=True, stop=True)),
                    reads=[f"ktm{kh}", f"Xw{ln}"], writes=[lbk])
                yield
                k.emit("vector", (lambda e, hh=hh, lb=lb:
                    e.scalar_tensor_tensor(
                        s_S[hh][:, :], s_S[hh][:, :],
                        s_gam[:, c.NVH + hh:c.NVH + hh + 1],
                        lb[:, 0:128], ALU.mult, ALU.add)),
                    reads=[f"S{hh}", "gam", lbk], writes=[f"S{hh}"])
                yield
                k.emit("scalar", (lambda e, hh=hh: e.activation(
                    s_Sbf[hh][:, :], s_S[hh][:, :], AF.Copy)),
                    reads=[f"S{hh}"], writes=[f"Sbf{hh}"])
                yield
                if p2lvl < 1580:
                    return
                k.emit("scalar", (lambda e, ln=ln: e.activation(
                    s_dump[ln][:, :], s_Otm[ln][:, :], AF.Square,
                    accum_out=s_oc2[ln][:, 0:1])),
                    reads=[f"Otm{ln}"], writes=[f"dump{ln}", f"oc2{ln}"])
                yield
                k.emit("scalar", (lambda e, ln=ln: e.activation(
                    s_oc2[ln][:, 1:2], s_oc2[ln][:, 0:1], AF.Sqrt,
                    scale=cst(1.0 / 128), bias=cst(1e-6))),
                    reads=[f"oc2{ln}", "consts"], writes=[f"oc2{ln}"])
                yield
                k.emit("vector", (lambda e, ln=ln: e.reciprocal(
                    s_oc2[ln][:, 1:2], s_oc2[ln][:, 1:2])),
                    reads=[f"oc2{ln}"], writes=[f"oc2{ln}"])
                yield
                k.emit("vector", (lambda e, hh=hh, ln=ln:
                    e.scalar_tensor_tensor(
                        s_Xbf[ln][:, :], s_Otm[ln][:, :],
                        s_oc2[ln][:, 1:2],
                        s_zg[:, 128 * hh:128 * (hh + 1)],
                        ALU.mult, ALU.mult)),
                    reads=[f"Otm{ln}", f"oc2{ln}", "zg"],
                    writes=[f"Xbf{ln}"])
                yield
                k.emit("tensor", (lambda e, ln=ln, tr=tr: e.transpose(
                    tr[:, 128:256], s_Xbf[ln][:, :], s_eyebf[:, :])),
                    reads=[f"Xbf{ln}", "consts"], writes=[trk])
                yield
                k.emit("vector", (lambda e, ln=ln, tr=tr: e.tensor_copy(
                    s_ocb[ln][:, :], tr[:, 128:256])),
                    reads=[trk], writes=[f"ocb{ln}"])
                yield
                k.dma("sync", f"d_of{ln}", (lambda e, hh=hh, tok0=tok0,
                                            t=t, ln=ln: e.dma_start(
                    out=ofm_sc[hh, :, t * c.TILE + tok0:
                               t * c.TILE + tok0 + 128],
                    in_=s_ocb[ln][:, :])),
                    reads=[f"ocb{ln}"], writes=["ofm_sc"])
                yield

            def drive(gens):
                alive = list(gens)
                while alive:
                    for g_ in list(alive):
                        try:
                            next(g_)
                        except StopIteration:
                            alive.remove(g_)

            drive([kh_gen(kh) for kh in range(min(2, c.NKH))])
            if p2lvl >= 15:
                grp = [head_gen(h) for h in range(min(4, c.NVH))]
                if c.NKH > 2:
                    grp += [kh_gen(kh) for kh in range(2, c.NKH)]
                drive(grp)
                if c.NVH > 4:
                    drive([head_gen(h) for h in range(4, c.NVH)])
    if upto <= 2:
        k.fence()
        stP2.close()
        return finish_dummy()

    k.fence()
    stP2.close()
    st25 = ExitStack()
    s_wout = [st25.enter_context(
        nc.sbuf_tensor(f"s_wout{v}", [128, c.D], BF16))
        for v in range(c.NVH)]
    s_ofm2 = [[st25.enter_context(
        nc.sbuf_tensor(f"s_ofm2{p}_{v}", [128, c.TILE], BF16))
        for v in range(c.NVH)] for p in range(2)]
    s_sfm = [st25.enter_context(
        nc.sbuf_tensor(f"s_sfm{i}", [128, c.TILE], F32)) for i in range(2)]

    st2 = ExitStack()
    s_rs = st2.enter_context(nc.sbuf_tensor("s_rs", [1, c.TQ], BF16))
    s_gus = st2.enter_context(nc.sbuf_tensor("s_gus", [1, c.TQ], BF16))

    st3 = ExitStack()
    s_gts = st3.enter_context(nc.sbuf_tensor("s_gts", [128, c.TQ], F32))
    s_gud = [st3.enter_context(nc.sbuf_tensor(f"s_gud{i}", [128, c.W4], BF16))
             for i in range(2)]
    s_xnq = [st3.enter_context(nc.sbuf_tensor(f"s_xnq{f}", [128, c.TQ], BF16))
             for f in range(c.NF)]
    NWSL = 3
    s_wgu = [[[st3.enter_context(
        nc.sbuf_tensor(f"s_wgu{p}_{m}_{f}", [128, 512], BF16))
        for f in range(c.NF)] for m in range(2)] for p in range(NWSL)]

    def load_wgu(hb, par):
        for m, wsrc in ((0, wgT), (1, wuT)):
            k.dma("gpsimd", f"d_f{m}{par}", [
                (lambda e, m=m, f=f, hb=hb, par=par, wsrc=wsrc:
                 e.dma_start(
                     out=s_wgu[par][m][f][:, :],
                     in_=wsrc[128 * f:128 * (f + 1),
                              512 * hb:512 * (hb + 1)]))
                for f in range(c.NF)], writes=[f"wgu{par}"])

    k.dma("gpsimd", "d_wo", [
        (lambda e, v=v: e.dma_start(out=s_wout[v][:, :],
                                    in_=wout[128 * v:128 * (v + 1), :]))
        for v in range(c.NVH)], writes=["woutsb"])
    k.dma("sync", "d_xnq", [
        (lambda e, f=f: e.dma_start(
            out=s_xnq[f][:, :],
            in_=xnq_sc[:, 128 * f:128 * (f + 1)], transpose=True))
        for f in range(c.NF)],
        reads=["xnq_sc"], writes=["xnq"])
    for pp_ in range(min(NWSL, c.NHB)):
        load_wgu(pp_, pp_)

    def load_ofm(t, par):
        k.dma("sync", f"d_or{par}", [
            (lambda e, v=v, t=t, par=par: e.dma_start(
                out=s_ofm2[par][v][:, :],
                in_=ofm_sc[v, :, t * c.TILE:(t + 1) * c.TILE]))
            for v in range(c.NVH)],
            reads=["ofm_sc"], writes=[f"ofm2{par}"])

    load_ofm(0, 0)
    if c.NT > 1:
        load_ofm(1, 1)
    for t in range(c.NT):
        par = t % 2
        blk = t // (c.NT // 4)
        colo = (t % (c.NT // 4)) * c.TILE
        for outc in range(c.OUTC):
            opar = outc % 2
            bank, bkey = (PC, "PC") if opar == 0 else (PD, "PD")
            k.emit("tensor", [
                (lambda e, v=v, outc=outc, par=par, bank=bank: e.matmul(
                    bank[:, 0:c.TILE],
                    s_wout[v][:, 128 * outc:128 * (outc + 1)],
                    s_ofm2[par][v][:, :],
                    start=(v == 0), stop=(v == c.NVH - 1)))
                for v in range(c.NVH)],
                reads=["woutsb", f"ofm2{par}"], writes=[bkey])
            k.emit("scalar", (lambda e, bank=bank, opar=opar: e.activation(
                s_sfm[opar][:, 0:c.TILE], bank[:, 0:c.TILE], AF.Copy)),
                reads=[bkey], writes=[f"sfm{opar}"])
            k.dma("sync", f"d_ri{opar}", (lambda e, outc=outc, blk=blk,
                                          colo=colo, opar=opar: e.dma_start(
                out=rsin[blk, 128 * outc:128 * (outc + 1),
                         colo:colo + c.TILE],
                in_=s_sfm[opar][:, 0:c.TILE])),
                reads=[f"sfm{opar}"], writes=["rsin"])
        if t + 2 < c.NT:
            load_ofm(t + 2, par)

    k.emit("gpsimd", (lambda e: e.collective_compute(
        "ReduceScatter", ALU.add,
        replica_groups=[[0, 1, 2, 3], [4, 5, 6, 7]],
        ins=[rsin.ap().opt()], outs=[rsout.ap().opt()])),
        reads=["rsin"], writes=["rsout"])

    if upto <= 3:
        return finish_dummy()

    TQH = c.TQ // c.W4
    GUPB = [(PP0, "PP0"), (PP1, "PP1"), (PA, "PA"), (PB, "PB")]
    for h2 in range(TQH):
        bank, bkey = (PC, "PC") if h2 % 2 == 0 else (PD, "PD")
        k.emit("tensor", [
            (lambda e, f=f, h2=h2, bank=bank: e.matmul(
                bank[0:1, 0:c.W4], s_onesb[:, :],
                s_xnq[f][:, c.W4 * h2:c.W4 * (h2 + 1)],
                start=(f == 0), stop=(f == c.NF - 1)))
            for f in range(c.NF)],
            reads=["xnq", "consts"], writes=[bkey])
        k.emit("vector", (lambda e, h2=h2, bank=bank: e.tensor_copy(
            s_rs[:, c.W4 * h2:c.W4 * (h2 + 1)], bank[0:1, 0:c.W4])),
            reads=[bkey], writes=["rs"])

    for hb in range(c.NHB):
        par = hb % NWSL
        for oc in range(4 if c.HID >= 512 else c.NHC):
            hc = 4 * hb + oc
            for h2 in range(TQH):
                gb, gk = GUPB[2 * (h2 % 2)]
                ub, uk = GUPB[2 * (h2 % 2) + 1]
                k.emit("tensor", [
                    (lambda e, f=f, oc=oc, h2=h2, par=par, gb=gb: e.matmul(
                        gb[:, 0:c.W4],
                        s_wgu[par][0][f][:, 128 * oc:128 * (oc + 1)],
                        s_xnq[f][:, c.W4 * h2:c.W4 * (h2 + 1)],
                        start=(f == 0), stop=False))
                    for f in range(c.NF)] + [
                    (lambda e, h2=h2, gb=gb: e.matmul(
                        gb[:, 0:c.W4], s_m128[:, :],
                        s_rs[:, c.W4 * h2:c.W4 * (h2 + 1)],
                        start=False, stop=True))],
                    reads=[f"wgu{par}", "xnq", "rs", "consts"], writes=[gk])
                k.emit("tensor", [
                    (lambda e, f=f, oc=oc, h2=h2, par=par, ub=ub: e.matmul(
                        ub[:, 0:c.W4],
                        s_wgu[par][1][f][:, 128 * oc:128 * (oc + 1)],
                        s_xnq[f][:, c.W4 * h2:c.W4 * (h2 + 1)],
                        start=(f == 0), stop=False))
                    for f in range(c.NF)] + [
                    (lambda e, h2=h2, ub=ub: e.matmul(
                        ub[:, 0:c.W4], s_m128[:, :],
                        s_rs[:, c.W4 * h2:c.W4 * (h2 + 1)],
                        start=False, stop=True))],
                    reads=[f"wgu{par}", "xnq", "rs", "consts"], writes=[uk])
                k.emit("scalar", (lambda e, hc=hc, h2=h2, gb=gb: e.activation(
                    s_gts[:, c.W4 * h2:c.W4 * (h2 + 1)], gb[:, 0:c.W4],
                    AF.Silu, scale=s_sg2[:, hc:hc + 1])),
                    reads=[gk, "consts"], writes=["gts"])
                gpar = (hc * TQH + h2) % 2
                k.emit("vector", (lambda e, hc=hc, h2=h2, ub=ub, gpar=gpar:
                    e.scalar_tensor_tensor(
                        s_gud[gpar][:, :],
                        ub[:, 0:c.W4], s_su2[:, hc:hc + 1],
                        s_gts[:, c.W4 * h2:c.W4 * (h2 + 1)],
                        ALU.mult, ALU.mult)),
                    reads=[uk, "consts", "gts"], writes=[f"gud{gpar}"])
                gusb, gusk = (PC, "PC") if h2 % 2 == 0 else (PD, "PD")
                k.emit("tensor", (lambda e, gpar=gpar, hc=hc, gusb=gusb:
                    e.matmul(
                        gusb[0:1, 0:c.W4], s_onesb[:, :], s_gud[gpar][:, :],
                        start=(hc == 0), stop=(hc == c.NHC - 1))),
                    reads=[f"gud{gpar}", "consts"], writes=[gusk])
                k.dma("sync", f"d_gw{gpar}", (lambda e, gpar=gpar, hc=hc,
                                              h2=h2: e.dma_start(
                    out=gu_sc[128 * hc:128 * (hc + 1),
                              c.W4 * h2:c.W4 * (h2 + 1)],
                    in_=s_gud[gpar][:, :])),
                    reads=[f"gud{gpar}"], writes=["gu_sc"])
        if hb + NWSL < c.NHB:
            load_wgu(hb + NWSL, par)
    for h2 in range(TQH):
        bank, bkey = (PC, "PC") if h2 % 2 == 0 else (PD, "PD")
        k.emit("vector", (lambda e, h2=h2, bank=bank: e.tensor_copy(
            s_gus[:, c.W4 * h2:c.W4 * (h2 + 1)], bank[0:1, 0:c.W4])),
            reads=[bkey], writes=["gus"])
    k.fence()
    st3.close()

    if upto <= 4:
        return finish_dummy()
    st4 = ExitStack()
    s_acc = st4.enter_context(
        nc.sbuf_tensor("s_acc", [128, c.OUTC * c.TQ], F32))
    s_wd = [[st4.enter_context(
        nc.sbuf_tensor(f"s_wd{p}_{j}", [128, c.D], BF16))
        for j in range(c.GSZ)] for p in range(2)]
    s_gur = [[st4.enter_context(
        nc.sbuf_tensor(f"s_gur{p}_{j}", [128, c.TQ], BF16))
        for j in range(c.GSZ)] for p in range(2)]
    s_fin = [st4.enter_context(
        nc.sbuf_tensor(f"s_fin{i}", [128, c.TQ], F32)) for i in range(4)]

    def load_wd(g, par):
        hcs = list(range(c.GSZ * g, min(c.GSZ * (g + 1), c.NHC)))
        k.dma("gpsimd", f"d_wd{par}", [
            (lambda e, j=j, hc=hc, par=par: e.dma_start(
                out=s_wd[par][j][:, :], in_=wdT[128 * hc:128 * (hc + 1), :]))
            for j, hc in enumerate(hcs)], writes=[f"wd{par}"])
        k.dma("sync", f"d_gr{par}", [
            (lambda e, j=j, hc=hc, par=par: e.dma_start(
                out=s_gur[par][j][:, :],
                in_=gu_sc[128 * hc:128 * (hc + 1), :]))
            for j, hc in enumerate(hcs)],
            reads=["gu_sc"], writes=[f"gur{par}"])

    load_wd(0, 0)
    for g in range(c.NGRP):
        par = g % 2
        if g + 1 < c.NGRP:
            load_wd(g + 1, 1 - par)
        hcs = list(range(c.GSZ * g, min(c.GSZ * (g + 1), c.NHC)))
        for outc in range(c.OUTC):
            for h2 in range(TQH):
                bank, bkey = GUPB[2 * (outc % 2) + h2 % 2]
                fns = [
                    (lambda e, j=j, hc=hc, outc=outc, h2=h2, par=par,
                     bank=bank, hn=len(hcs), g=g: e.matmul(
                        bank[:, 0:c.W4],
                        s_wd[par][j][:, 128 * outc:128 * (outc + 1)],
                        s_gur[par][j][:, c.W4 * h2:c.W4 * (h2 + 1)],
                        start=(j == 0),
                        stop=(j == hn - 1 and g != c.NGRP - 1)))
                    for j, hc in enumerate(hcs)]
                if g == c.NGRP - 1:
                    fns.append(lambda e, h2=h2, bank=bank: e.matmul(
                        bank[:, 0:c.W4], s_m128[:, :],
                        s_gus[:, c.W4 * h2:c.W4 * (h2 + 1)],
                        start=False, stop=True))
                k.emit("tensor", fns,
                       reads=[f"wd{par}", f"gur{par}", "gus", "consts"],
                       writes=[bkey])
                ao = outc * c.TQ + c.W4 * h2
                if g == 0:
                    k.emit("vector", (lambda e, ao=ao, bank=bank: e.tensor_copy(
                        s_acc[:, ao:ao + c.W4], bank[:, 0:c.W4])),
                        reads=[bkey], writes=[f"acc{outc}"])
                else:
                    k.emit("vector", (lambda e, ao=ao, bank=bank, outc=outc:
                        e.tensor_tensor(
                            s_acc[:, ao:ao + c.W4], bank[:, 0:c.W4],
                            s_acc[:, ao:ao + c.W4], ALU.add)),
                        reads=[bkey, f"acc{outc}"], writes=[f"acc{outc}"])

    for outc in range(c.OUTC):
        k.dma("sync", "d_ro", (lambda e, outc=outc: e.dma_start(
            out=s_fin[0][:, :], in_=rsout[128 * outc:128 * (outc + 1), :])),
            reads=["rsout"], writes=["fin0"])
        k.dma("sync", "d_xt", (lambda e, outc=outc: e.dma_start(
            out=s_fin[1][:, :], in_=xqT[128 * outc:128 * (outc + 1), :])),
            writes=["fin1"])
        k.emit("scalar", (lambda e, outc=outc: e.activation(
            s_fin[2][:, :], s_acc[:, outc * c.TQ:(outc + 1) * c.TQ], AF.Silu,
            scale=s_sd2[:, outc:outc + 1])),
            reads=[f"acc{outc}", "consts"], writes=["fin2"])
        k.emit("vector", lambda e: e.tensor_tensor(
            s_fin[3][:, :], s_fin[0][:, :], s_fin[2][:, :], ALU.mult),
            reads=["fin0", "fin2"], writes=["fin3"])
        k.emit("vector", lambda e: e.tensor_tensor(
            s_fin[3][:, :], s_fin[3][:, :], s_fin[1][:, :], ALU.add),
            reads=["fin3", "fin1"], writes=["fin3"])
        k.dma("sync", "d_out", (lambda e, outc=outc: e.dma_start(
            out=outfm[128 * outc:128 * (outc + 1), :], in_=s_fin[3][:, :])),
            reads=["fin3"], writes=[f"out{outc}"])

    k._wait("gpsimd", "d_out", k.count["d_out"])

    semh = {}
    for name in sorted(set(list(k.count.keys()) + list(K.ESEM.values()))):
        semh[name] = st.enter_context(nc.semaphore(name))
    k.replay(nc, semh)
    st4.close()
    st2.close()
    st25.close()
    st.close()
    return nc


def stage(cfg, A):
    c = cfg
    x = np.ascontiguousarray(np.asarray(A["L0_attn_hidden_states"], np.float32))
    Wqkv = np.asarray(A["L0_attn_W_qkv"], np.float32)
    Wz = np.asarray(A["L0_attn_W_z"], np.float32)
    Wb = np.asarray(A["L0_attn_W_b"], np.float32)
    Wa = np.asarray(A["L0_attn_W_a"], np.float32)
    convw = np.asarray(A["L0_attn_conv_w"], np.float32)
    dtb = np.asarray(A["L0_attn_dt_bias"], np.float32)
    alog = np.asarray(A["L0_attn_A_log"], np.float32)
    normw = np.asarray(A["L0_attn_norm_w"], np.float32)
    Wout = np.asarray(A["L0_attn_W_out"], np.float32)
    wqg = np.asarray(A["L0_ffn_wq_gate"], np.int32)
    sg = np.asarray(A["L0_ffn_s_gate"], np.float32)
    wqu = np.asarray(A["L0_ffn_wq_up"], np.int32)
    su = np.asarray(A["L0_ffn_s_up"], np.float32)
    wqd = np.asarray(A["L0_ffn_wq_down"], np.int32)
    sd = np.asarray(A["L0_ffn_s_down"], np.float32)

    KD = 4 * c.QW
    wgT = np.ascontiguousarray(wqg.T)
    wuT = np.ascontiguousarray(wqu.T)
    wdT = np.ascontiguousarray(wqd.T)
    sg2 = np.ascontiguousarray(sg.reshape(c.NHC, 128).T)
    su2 = np.ascontiguousarray(su.reshape(c.NHC, 128).T)
    sd2 = np.ascontiguousarray(sd.reshape(c.OUTC, 128).T)
    idx = np.arange(128)
    consts = dict(
        eyebf=np.eye(128, dtype=BF),
        eyef=np.eye(128, dtype=np.float32),
        ltri=np.triu(np.ones((128, 128), np.float32)),
        maskE=np.where(idx[None, :] >= idx[:, None], 0.0, -1e30).astype(np.float32),
        mask2=np.where(idx[None, :] < idx[:, None], 0.0, 1e30).astype(np.float32),
        mask01=np.triu(np.ones((128, 128), np.float32), 1),
        rowsel=np.ascontiguousarray(np.repeat(
            np.eye(cfg.NVH, dtype=np.float32), 128, axis=1)),
        normw4=np.ascontiguousarray(np.tile(normw, 512 // normw.shape[0])[None, :]).astype(np.float32),
    )
    in_maps = []
    for core in range(8):
        b, hg = core // 4, core % 4
        qsl = slice(c.QW * hg, c.QW * (hg + 1))
        ksl = slice(KD + c.QW * hg, KD + c.QW * (hg + 1))
        vsl = slice(2 * KD + c.VW * hg, 2 * KD + c.VW * (hg + 1))
        zsl = slice(c.VW * hg, c.VW * (hg + 1))
        hsl = slice(c.NVH * hg, c.NVH * (hg + 1))
        wg = np.concatenate(
            [Wqkv[:, qsl], Wqkv[:, ksl], Wqkv[:, vsl]], axis=1)
        wgc = np.ascontiguousarray(wg.reshape(c.NF, 128, c.WC))
        wzc = np.ascontiguousarray(
            Wz[:, zsl].reshape(c.NF, 128, c.NZC, c.ZW).transpose(0, 2, 1, 3))
        wab = np.concatenate([Wa[:, hsl], Wb[:, hsl]], axis=1)
        wabc = np.ascontiguousarray(wab.reshape(c.NF, 128, c.ABW))
        cw = np.concatenate([convw[qsl], convw[ksl], convw[vsl]], axis=0)
        xb = x[b]
        m = dict(
            xfull=xb,
            xq=np.ascontiguousarray(xb[c.TQ * hg:c.TQ * (hg + 1)]),
            xqT=np.ascontiguousarray(xb[c.TQ * hg:c.TQ * (hg + 1)].T),
            wgc=wgc, wzc=wzc, wabc=wabc,
            convw=np.ascontiguousarray(cw),
            alog=np.ascontiguousarray(alog[hsl][:, None]),
            dtb=np.ascontiguousarray(dtb[hsl][:, None]),
            wout=np.ascontiguousarray(Wout[c.VW * hg:c.VW * (hg + 1)]),
            wgT=wgT, wuT=wuT, wdT=wdT, sg2=sg2, su2=su2, sd2=sd2,
            **consts)
        in_maps.append(m)
    return in_maps


def assemble(cfg, outs):
    c = cfg
    out = np.empty((2, c.T, c.D), np.float32)
    for core in range(8):
        b, hg = core // 4, core % 4
        out[b, c.TQ * hg:c.TQ * (hg + 1), :] = outs[core].T
    return out


def kernel(**inputs):
    cfg = REAL
    in_maps = stage(cfg, inputs)
    nc = build(cfg)
    res = run_bass_kernel_spmd(nc, in_maps, core_ids=list(range(8)))
    return assemble(cfg, [np.asarray(r["outfm"]) for r in res.results])


# version 11 (final)
# speedup vs baseline: 1.6131x; 1.0242x over previous
import numpy as np
import ml_dtypes
import concourse.bass as bass
import concourse.mybir as mybir
from concourse.bass_utils import run_bass_kernel_spmd
from contextlib import ExitStack

F32 = mybir.dt.float32
BF16 = mybir.dt.bfloat16
I32 = mybir.dt.int32
AF = mybir.ActivationFunctionType
ALU = mybir.AluOpType
AX = mybir.AxisListType
BF = ml_dtypes.bfloat16


class Cfg:
    def __init__(self, T, TILE, D, NKH, HID):
        self.T, self.TILE, self.D, self.NKH, self.HID = T, TILE, D, NKH, HID
        self.NVH = 2 * NKH
        self.NF = D // 128
        self.NT = T // TILE
        self.NCH = TILE // 128
        self.TQ = T // 4
        self.QW = NKH * 128
        self.VW = self.NVH * 128
        self.ABW = 2 * self.NVH
        self.WC = 2 * self.QW + self.VW
        self.WCOLS = self.WC + self.VW + self.ABW
        self.OUTC = D // 128
        self.NHC = HID // 128
        self.NHB = HID // 512
        self.GSZ = 4
        self.NGRP = (self.NHC + self.GSZ - 1) // self.GSZ
        self.NLEV = 3
        self.NQB = self.TQ // 128
        self.NTB = T // 128
        self.W2 = min(512, TILE)
        self.W4 = min(512, self.TQ)
        self.NZC = self.VW // 512 if self.VW >= 512 else 0
        self.ZW = 512 if self.VW >= 512 else self.VW
        if self.NZC == 0:
            self.NZC = 1


REAL = Cfg(T=4096, TILE=512, D=2048, NKH=4, HID=5632)
TINY = Cfg(T=512, TILE=128, D=256, NKH=2, HID=512)


class K:
    ENGS = ("tensor", "vector", "scalar", "gpsimd", "sync")
    ESEM = {"tensor": "s_pe", "vector": "s_dve", "scalar": "s_act",
            "gpsimd": "s_gp", "sync": "s_sp"}

    def __init__(self):
        self.streams = {e: [] for e in K.ENGS}
        self.count = {}
        self.mark = {e: {} for e in K.ENGS}
        self.writer = {}
        self.readers = {}
        self.dma_pend = {}
        self.sem_names = set(K.ESEM.values())

    def _wait(self, eng, sem, val):
        if val <= 0:
            return
        if self.mark[eng].get(sem, 0) >= val:
            return
        self.mark[eng][sem] = val
        self.streams[eng].append(("w", sem, val))

    def _deps(self, eng, reads, writes):
        for k in reads:
            for s, v in self.writer.get(k, {}).items():
                self._wait(eng, s, v)
        for k in writes:
            for s, v in self.writer.get(k, {}).items():
                self._wait(eng, s, v)
            for s, v in self.readers.get(k, {}).items():
                self._wait(eng, s, v)

    def _post(self, sem, v, reads, writes):
        for k in writes:
            self.writer.setdefault(k, {})[sem] = v
            self.readers[k] = {}
        for k in reads:
            self.readers.setdefault(k, {})[sem] = v

    PSUM_KEYS = frozenset(
        ["PP0", "PP1", "PA", "PB", "PC", "PD", "PF0", "PF1"])

    def _excl(self, reads, writes):
        pr = [k for k in reads if k in K.PSUM_KEYS]
        if pr:
            reads = [k for k in reads if k not in K.PSUM_KEYS]
            writes = list(writes) + pr
        return reads, writes

    def emit(self, eng, fns, reads=(), writes=()):
        if not isinstance(fns, (list, tuple)):
            fns = [fns]
        reads, writes = self._excl(reads, writes)
        self._deps(eng, reads, writes)
        sem = K.ESEM[eng]
        self.count[sem] = self.count.get(sem, 0) + 1
        v = self.count[sem]
        for i, fn in enumerate(fns):
            self.streams[eng].append(
                ("i", fn, sem if i == len(fns) - 1 else None, 1))
        self._post(sem, v, reads, writes)
        return v

    def dma(self, eng, sem, fns, reads=(), writes=()):
        if not isinstance(fns, (list, tuple)):
            fns = [fns]
        reads, writes = self._excl(reads, writes)
        self.sem_names.add(sem)
        if self.dma_pend.get(sem):
            self._wait(eng, sem, self.count.get(sem, 0))
        self._deps(eng, reads, writes)
        for fn in fns:
            self.count[sem] = self.count.get(sem, 0) + 16
            self.streams[eng].append(("i", fn, sem, 16))
        v = self.count[sem]
        self.dma_pend[sem] = True
        self._post(sem, v, reads, writes)
        return v

    def fence(self):
        for eng in K.ENGS:
            own = K.ESEM[eng]
            for sem, cnt in list(self.count.items()):
                if sem != own:
                    self._wait(eng, sem, cnt)

    def replay(self, nc, semh):
        with nc.Block() as block:
            for name in K.ENGS:
                stream = self.streams[name]
                if not stream:
                    continue

                def body(eng, stream=stream):
                    for it in stream:
                        if it[0] == "w":
                            eng.wait_ge(semh[it[1]], it[2])
                        else:
                            bi = it[1](eng)
                            if it[2] is not None:
                                bi.then_inc(semh[it[2]], it[3])
                getattr(block, name)(body)


def build(cfg, upto=5, p2lvl=10**9):
    c = cfg
    nc = bass.Bass("TRN2", target_bir_lowering=False, debug=False)
    k = K()

    dp = nc.declare_dram_parameter
    xfull = dp("xfull", [c.T, c.D], F32, isOutput=False)
    xq = dp("xq", [c.TQ, c.D], F32, isOutput=False)
    xqT = dp("xqT", [c.D, c.TQ], F32, isOutput=False)
    wgc = dp("wgc", [c.NF, 128, c.WC], F32, isOutput=False)
    wzc = dp("wzc", [c.NF, c.NZC, 128, c.ZW], F32, isOutput=False)
    wabc = dp("wabc", [c.NF, 128, c.ABW], F32, isOutput=False)
    convw = dp("convw", [c.WC, 4], F32, isOutput=False)
    alog = dp("alog", [c.NVH, 1], F32, isOutput=False)
    dtb = dp("dtb", [c.NVH, 1], F32, isOutput=False)
    normw4 = dp("normw4", [1, 512], F32, isOutput=False)
    wout = dp("wout", [c.VW, c.D], F32, isOutput=False)
    wgT = dp("wgT", [c.D, c.HID], I32, isOutput=False)
    wuT = dp("wuT", [c.D, c.HID], I32, isOutput=False)
    wdT = dp("wdT", [c.HID, c.D], I32, isOutput=False)
    sg2 = dp("sg2", [128, c.NHC], F32, isOutput=False)
    su2 = dp("su2", [128, c.NHC], F32, isOutput=False)
    sd2 = dp("sd2", [128, c.OUTC], F32, isOutput=False)
    eyebf = dp("eyebf", [128, 128], BF16, isOutput=False)
    eyef = dp("eyef", [128, 128], F32, isOutput=False)
    ltri = dp("ltri", [128, 128], F32, isOutput=False)
    maskE = dp("maskE", [128, 128], F32, isOutput=False)
    mask2 = dp("mask2", [128, 128], F32, isOutput=False)
    mask01 = dp("mask01", [128, 128], F32, isOutput=False)
    rowsel = dp("rowsel", [c.NVH, 128 * c.NVH], F32, isOutput=False)
    outfm = dp("outfm", [c.D, c.TQ], F32, isOutput=True)

    xn_sc = nc.dram_tensor("xn_sc", [c.T, c.D], BF16)
    xnq_sc = nc.dram_tensor("xnq_sc", [c.TQ, c.D], BF16)
    ofm_sc = nc.dram_tensor("ofm_sc", [c.NVH, 128, c.T], BF16)
    rsin = nc.dram_tensor("rsin", [4, c.D, c.TQ], F32)
    gu_sc = nc.dram_tensor("gu_sc", [c.NHC * 128, c.TQ], BF16)
    rsout = nc.dram_tensor("rsout", [c.D, c.TQ], F32)

    st = ExitStack()
    sb = lambda n, s, d: st.enter_context(nc.sbuf_tensor(n, s, d))
    ps = lambda n, s, d: st.enter_context(nc.psum_tensor(n, s, d))

    PP0 = ps("PP0", [128, 512], F32)
    PP1 = ps("PP1", [128, 512], F32)
    PA = ps("PA", [128, 512], F32)
    PB = ps("PB", [128, 512], F32)
    PC = ps("PC", [128, 512], F32)
    PD = ps("PD", [128, 512], F32)
    PF0 = ps("PF0", [128, 1024], BF16)
    PF1 = ps("PF1", [128, 1024], BF16)

    s_eyebf = sb("s_eyebf", [128, 128], BF16)
    s_eyef = sb("s_eyef", [128, 128], F32)
    s_ltri = sb("s_ltri", [128, 128], F32)
    s_maskE = sb("s_maskE", [128, 128], F32)
    s_mask2 = sb("s_mask2", [128, 128], F32)
    s_mask01 = sb("s_mask01", [128, 128], F32)
    s_normw = sb("s_normw", [1, 512], F32)
    s_alog = sb("s_alog", [c.NVH, 1], F32)
    s_dtb = sb("s_dtb", [c.NVH, 1], F32)
    s_nega = sb("s_nega", [c.NVH, 1], F32)
    s_cst = sb("s_cst", [128, 8], F32)
    s_ones1 = sb("s_ones1", [1, 128], F32)
    s_onesc = sb("s_onesc", [128, 1], F32)
    s_onesb = sb("s_onesb", [128, 1], BF16)
    s_m128 = sb("s_m128", [1, 128], BF16)
    s_normb = sb("s_normb", [128, 512], F32)
    s_onesq = sb("s_onesq", [128, 128], F32)
    s_rsel = sb("s_rsel", [c.NVH, 128 * c.NVH], F32)
    s_convw = sb("s_convw", [128, 4 * (c.WC // 128)], F32)
    s_sg2 = sb("s_sg2", [128, c.NHC], F32)
    s_su2 = sb("s_su2", [128, c.NHC], F32)
    s_sd2 = sb("s_sd2", [128, c.OUTC], F32)

    CST = {1e-6: 0, 128.0: 1, 128e-6: 2, 1.0 / c.D: 3, 1e-5: 4,
           1.0 / 128: 5, 1.0: 6, -1.0: 7}

    def cst(v, p=128):
        return s_cst[0:p, CST[v]:CST[v] + 1]

    k.dma("sync", "d_c", [
        (lambda e, o=o_, i=i_: e.dma_start(out=o, in_=i))
        for o_, i_ in [
            (s_eyebf[:, :], eyebf[:, :]), (s_eyef[:, :], eyef[:, :]),
            (s_ltri[:, :], ltri[:, :]), (s_maskE[:, :], maskE[:, :]),
            (s_mask2[:, :], mask2[:, :]), (s_mask01[:, :], mask01[:, :]),
            (s_normw[:, :], normw4[:, :]), (s_alog[:, :], alog[:, :]),
            (s_dtb[:, :], dtb[:, :]), (s_sg2[:, :], sg2[:, :]),
            (s_su2[:, :], su2[:, :]), (s_sd2[:, :], sd2[:, :]),
            (s_rsel[:, :], rowsel[:, :]),
        ] + [(s_convw[:, 4 * f:4 * f + 4], convw[128 * f:128 * (f + 1), :])
             for f in range(c.WC // 128)]
    ], writes=["consts"])
    for val, col in CST.items():
        k.emit("vector", (lambda e, v=val, cl=col: e.memset(
            s_cst[:, cl:cl + 1], v)), writes=["consts"])
    k.emit("vector", lambda e: e.memset(s_ones1[:, :], 1.0), writes=["consts"])
    k.emit("vector", lambda e: e.memset(s_onesc[:, :], 1.0), writes=["consts"])
    k.emit("vector", lambda e: e.memset(s_onesb[:, :], 1.0), writes=["consts"])
    k.emit("vector", lambda e: e.memset(s_m128[:, :], -128.0), writes=["consts"])
    k.emit("vector", lambda e: e.memset(s_onesq[:, :], 1.0), writes=["consts"])
    k.emit("scalar", lambda e: e.activation(s_nega[:, :], s_alog[:, :], AF.Exp),
           reads=["consts"], writes=["nega"])
    k.emit("vector", lambda e: e.tensor_scalar(
        s_nega[:, :], s_nega[:, :], -1.0, None, ALU.mult),
        reads=["nega"], writes=["nega"])
    k.emit("tensor", lambda e: e.matmul(
        PA[:, 0:512], s_ones1[:, :], s_normw[:, :], start=True, stop=True),
        reads=["consts"], writes=["PA"])
    k.emit("vector", lambda e: e.tensor_copy(s_normb[:, :], PA[:, 0:512]),
           reads=["PA"], writes=["normb"])

    stP1 = ExitStack()
    xl = [stP1.enter_context(nc.sbuf_tensor(f"xln{i}", [128, c.D], F32))
          for i in range(2)]
    xb = [stP1.enter_context(nc.sbuf_tensor(f"xnb{i}", [128, c.D], BF16))
          for i in range(2)]
    dumpb = stP1.enter_context(nc.sbuf_tensor("dumpb", [128, c.D], F32))
    lncol = stP1.enter_context(nc.sbuf_tensor("lncol", [128, 4], F32))

    def ln_block(src_ap, dst_ap, p, keys):
        xk, bk = f"xl{p}", f"xb{p}"
        xlp, xbp = xl[p], xb[p]
        k.dma("sync", f"d_x{p}", (lambda e: e.dma_start(
            out=xlp[:, :], in_=src_ap)), writes=[xk])
        k.emit("vector", lambda e: e.reduce_sum(
            lncol[:, 0:1], xlp[:, :], axis=AX.X),
            reads=[xk], writes=["lncol"])
        k.emit("scalar", lambda e: e.activation(
            lncol[:, 1:2], lncol[:, 0:1], AF.Copy, scale=cst(1.0 / c.D)),
            reads=["lncol", "consts"], writes=["lncol"])
        k.emit("vector", lambda e: e.tensor_scalar(
            xlp[:, :], xlp[:, :], lncol[:, 1:2], None, ALU.subtract),
            reads=[xk, "lncol"], writes=[xk])
        k.emit("scalar", lambda e: e.activation(
            dumpb[:, :], xlp[:, :], AF.Square, accum_out=lncol[:, 2:3]),
            reads=[xk], writes=["dumpb", "lncol"])
        k.emit("scalar", lambda e: e.activation(
            lncol[:, 3:4], lncol[:, 2:3], AF.Sqrt,
            scale=cst(1.0 / c.D), bias=cst(1e-5)),
            reads=["lncol", "consts"], writes=["lncol"])
        k.emit("vector", lambda e: e.reciprocal(
            lncol[:, 3:4], lncol[:, 3:4]),
            reads=["lncol"], writes=["lncol"])
        k.emit("vector", lambda e: e.tensor_scalar(
            xbp[:, :], xlp[:, :], lncol[:, 3:4], None, ALU.mult),
            reads=[xk, "lncol"], writes=[bk])
        k.dma("sync", f"d_xw{p}", (lambda e: e.dma_start(
            out=dst_ap, in_=xbp[:, :])), reads=[bk], writes=keys)

    for tb in range(c.NTB):
        ln_block(xfull[128 * tb:128 * (tb + 1), :],
                 xn_sc[128 * tb:128 * (tb + 1), :], tb % 2, ["xn_sc"])
    for qb in range(c.NQB):
        ln_block(xq[128 * qb:128 * (qb + 1), :],
                 xnq_sc[128 * qb:128 * (qb + 1), :], qb % 2, ["xnq_sc"])
    k.fence()
    stP1.close()

    def finish_dummy():
        sfin = st.enter_context(nc.sbuf_tensor("s_dummy", [128, c.TQ], F32))
        k.emit("vector", lambda e: e.memset(sfin[:, :], 0.0), writes=["dumz"])
        k.dma("sync", "d_out", [
            (lambda e, o=o: e.dma_start(
                out=outfm[128 * o:128 * (o + 1), :], in_=sfin[:, :]))
            for o in range(c.OUTC)], reads=["dumz"], writes=["outz"])
        k._wait("gpsimd", "d_out", k.count["d_out"])
        semh = {}
        for name in sorted(set(list(k.count.keys()) + list(K.ESEM.values()))):
            semh[name] = st.enter_context(nc.semaphore(name))
        k.replay(nc, semh)
        st.close()
        return nc

    if upto <= 1:
        return finish_dummy()

    QKV_OC = c.WC // 128
    NWG = (QKV_OC + 3) // 4

    stP2 = ExitStack()
    sbp = lambda n, s, d: stP2.enter_context(nc.sbuf_tensor(n, s, d))
    s_xnf = [sbp(f"s_xnf{f}", [128, c.TILE], BF16) for f in range(c.NF)]
    s_wgF = [sbp(f"s_wgF{f}", [128, c.WC], BF16) for f in range(c.NF)]
    s_wz = [[sbp(f"s_wz{z}_{f}", [128, c.ZW], BF16) for f in range(c.NF)]
            for z in range(c.NZC)]
    s_wab = [sbp(f"s_wab{f}", [128, c.ABW], BF16) for f in range(c.NF)]
    s_praw = [sbp(f"s_praw{i}", [128, c.TILE + 3], BF16) for i in range(2)]
    s_hist = sbp("s_hist", [128, 3 * QKV_OC], BF16)
    s_qkvs = sbp("s_qkvs", [128, QKV_OC * c.TILE], BF16)
    s_cacc = [sbp(f"s_cacc{i}", [128, c.TILE], F32) for i in range(2)]
    s_zg = sbp("s_zg", [128, c.VW], F32)
    s_gc = sbp("s_gc", [c.NVH, c.TILE], F32)
    s_bc = sbp("s_bc", [c.NVH, c.TILE], F32)
    s_gtm = sbp("s_gtm", [128, c.NVH], F32)
    s_btm = sbp("s_btm", [128, c.NVH], F32)
    s_G = sbp("s_G", [128, c.NVH], F32)
    s_Gfm = sbp("s_Gfm", [c.NVH, 128], F32)
    s_gam = sbp("s_gam", [128, 4 * c.NVH], F32)
    s_nbf = sbp("s_nbf", [c.NVH, c.TILE], F32)
    s_sks = [sbp(f"s_sks{i}", [1, 256], F32) for i in range(2)]
    s_ktf = [sbp(f"s_ktf{i}", [128, 128], BF16) for i in range(4)]
    s_qtf = [sbp(f"s_qtf{i}", [128, 128], BF16) for i in range(4)]
    s_ktm = [sbp(f"s_ktm{i}", [128, 128], BF16) for i in range(4)]
    s_ksq = [sbp(f"s_ksq{i}", [128, 256], F32) for i in range(2)]
    s_kk = [sbp(f"s_kk{i}", [128, 256], BF16) for i in range(4)]
    s_gbc = sbp("s_gbc", [128, 128 * c.NVH], F32)
    s_bbc = sbp("s_bbc", [128, 128 * c.NVH], BF16)
    s_argE = [sbp(f"s_argE{i}", [128, 128], F32) for i in range(4)]
    s_argN = [sbp(f"s_argN{i}", [128, 128], F32) for i in range(4)]
    s_es = [sbp(f"s_es{i}", [128, 128], F32) for i in range(4)]
    s_ntm = [sbp(f"s_ntm{i}", [128, 128], F32) for i in range(4)]
    s_TT = [[sbp(f"s_TT{i}_{j}", [128, 128], BF16) for j in range(2)]
            for i in range(4)]
    s_P = [[sbp(f"s_P{i}_{j}", [128, 128], BF16) for j in range(2)]
           for i in range(4)]
    s_PT = [[sbp(f"s_PT{i}_{j}", [128, 128], BF16) for j in range(2)]
            for i in range(4)]
    s_A2T = [sbp(f"s_A2T{i}", [128, 128], BF16) for i in range(4)]
    s_RHS = [sbp(f"s_RHS{i}", [128, 128], BF16) for i in range(4)]
    s_Xbf = [sbp(f"s_Xbf{i}", [128, 128], BF16) for i in range(4)]
    s_Xw = [sbp(f"s_Xw{i}", [128, 128], BF16) for i in range(4)]
    s_Otm = [sbp(f"s_Otm{i}", [128, 128], F32) for i in range(4)]
    s_dump = [sbp(f"s_dump{i}", [128, 128], BF16) for i in range(4)]
    s_oc2 = [sbp(f"s_oc2{i}", [128, 2], F32) for i in range(4)]
    s_S = [sbp(f"s_S{h}", [128, 128], F32) for h in range(c.NVH)]
    s_Sbf = [sbp(f"s_Sbf{h}", [128, 128], BF16) for h in range(c.NVH)]
    s_ocb = [sbp(f"s_ocb{i}", [128, 128], BF16) for i in range(4)]

    for h in range(c.NVH):
        k.emit("vector", (lambda e, h=h: e.memset(s_S[h][:, :], 0.0)),
               writes=[f"S{h}"])
        k.emit("vector", (lambda e, h=h: e.memset(s_Sbf[h][:, :], 0.0)),
               writes=[f"Sbf{h}"])
    k.emit("vector", lambda e: e.memset(s_hist[:, :], 0.0),
           writes=[f"hist{oc}" for oc in range(QKV_OC)])

    def load_xnf(t):
        k.dma("sync", "d_xnr", [
            (lambda e, f=f, t=t: e.dma_start(
                out=s_xnf[f][:, :],
                in_=xn_sc[t * c.TILE:(t + 1) * c.TILE, 128 * f:128 * (f + 1)],
                transpose=True))
            for f in range(c.NF)],
            reads=["xn_sc"], writes=[f"xnf{f}" for f in range(c.NF)])

    def load_wg_once():
        k.dma("gpsimd", "d_wq", [
            (lambda e, f=f: e.dma_start(out=s_wgF[f][:, :], in_=wgc[f]))
            for f in range(c.NF)], writes=["wgF"])

    def load_wz(z):
        k.dma("gpsimd", f"d_wz{z}", [
            (lambda e, f=f, z=z: e.dma_start(
                out=s_wz[z][f][:, :], in_=wzc[f, z]))
            for f in range(c.NF)], writes=[f"wz{z}"])

    def load_wab():
        k.dma("gpsimd", "d_wab", [
            (lambda e, f=f: e.dma_start(
                out=s_wab[f][:, :], in_=wabc[f]))
            for f in range(c.NF)], writes=["wab"])

    PBANK = [PP0, PP1, PA, PB]
    PBK = ["PP0", "PP1", "PA", "PB"]
    xnf_keys = [f"xnf{f}" for f in range(c.NF)]

    load_xnf(0)
    load_wg_once()
    for z in range(c.NZC):
        load_wz(z)
    load_wab()

    for t in range(c.NT):
        for g in range(NWG):
            ocs = list(range(4 * g, min(4 * g + 4, QKV_OC)))
            for j, oc in enumerate(ocs):
                bank, bkey = PBANK[j], PBK[j]
                par = oc % 2
                k.emit("vector", (lambda e, oc=oc, par=par: e.tensor_copy(
                    s_praw[par][:, 0:3], s_hist[:, 3 * oc:3 * oc + 3])),
                    reads=[f"hist{oc}"], writes=[f"praw{par}"])
                k.emit("tensor", [
                    (lambda e, f=f, oc=oc, bank=bank: e.matmul(
                        bank[:, 0:c.TILE],
                        s_wgF[f][:, 128 * oc:128 * (oc + 1)],
                        s_xnf[f][:, :],
                        start=(f == 0), stop=(f == c.NF - 1)))
                    for f in range(c.NF)],
                    reads=["wgF"] + xnf_keys, writes=[bkey])
                k.emit("scalar", (lambda e, bank=bank, par=par: e.activation(
                    s_praw[par][:, 3:3 + c.TILE], bank[:, 0:c.TILE],
                    AF.Copy)),
                    reads=[bkey], writes=[f"praw{par}"])
                k.emit("vector", (lambda e, oc=oc, par=par: e.tensor_copy(
                    s_hist[:, 3 * oc:3 * oc + 3],
                    s_praw[par][:, c.TILE:c.TILE + 3])),
                    reads=[f"praw{par}"], writes=[f"hist{oc}"])
                k.emit("vector", (lambda e, oc=oc, par=par: e.tensor_scalar(
                    s_cacc[par][:, :], s_praw[par][:, 0:c.TILE],
                    s_convw[:, 4 * oc:4 * oc + 1], None, ALU.mult)),
                    reads=[f"praw{par}", "consts"], writes=[f"cacc{par}"])
                for tap in range(1, 4):
                    k.emit("vector", (lambda e, oc=oc, par=par, tap=tap:
                        e.scalar_tensor_tensor(
                            s_cacc[par][:, :],
                            s_praw[par][:, tap:tap + c.TILE],
                            s_convw[:, 4 * oc + tap:4 * oc + tap + 1],
                            s_cacc[par][:, :], ALU.mult, ALU.add)),
                        reads=[f"praw{par}", "consts", f"cacc{par}"],
                        writes=[f"cacc{par}"])
                k.emit("scalar", (lambda e, oc=oc, par=par: e.activation(
                    s_qkvs[:, oc * c.TILE:(oc + 1) * c.TILE],
                    s_cacc[par][:, :], AF.Silu)),
                    reads=[f"cacc{par}"], writes=[f"qkvs{oc}"])
        if p2lvl < 12:
            continue
        for h2 in range(c.TILE // c.W2):
            co = c.W2 * h2
            k.emit("tensor", [
                (lambda e, f=f, co=co: e.matmul(
                    PA[0:c.NVH, 0:c.W2], s_wab[f][:, 0:c.NVH],
                    s_xnf[f][:, co:co + c.W2],
                    start=(f == 0), stop=(f == c.NF - 1)))
                for f in range(c.NF)],
                reads=["wab"] + xnf_keys, writes=["PA"])
            k.emit("tensor", [
                (lambda e, f=f, co=co: e.matmul(
                    PB[0:c.NVH, 0:c.W2], s_wab[f][:, c.NVH:c.ABW],
                    s_xnf[f][:, co:co + c.W2],
                    start=(f == 0), stop=(f == c.NF - 1)))
                for f in range(c.NF)],
                reads=["wab"] + xnf_keys, writes=["PB"])
            k.emit("scalar", (lambda e, co=co: e.activation(
                s_gc[:, co:co + c.W2], PA[0:c.NVH, 0:c.W2], AF.Exp,
                bias=s_dtb[:, 0:1])),
                reads=["PA", "consts"], writes=["gbc"])
            k.emit("scalar", (lambda e, co=co: e.add(
                s_gc[:, co:co + c.W2], s_gc[:, co:co + c.W2], 1.0)),
                reads=["gbc", "consts"], writes=["gbc"])
            k.emit("scalar", (lambda e, co=co: e.activation(
                s_gc[:, co:co + c.W2], s_gc[:, co:co + c.W2],
                AF.Ln)), reads=["gbc"], writes=["gbc"])
            k.emit("vector", (lambda e, co=co: e.tensor_scalar(
                s_gc[:, co:co + c.W2], s_gc[:, co:co + c.W2],
                s_nega[:, 0:1], None, ALU.mult)),
                reads=["gbc", "nega"], writes=["gbc"])
            k.emit("scalar", (lambda e, co=co: e.activation(
                s_bc[:, co:co + c.W2], PB[0:c.NVH, 0:c.W2],
                AF.Sigmoid)), reads=["PB"], writes=["gbc"])
            k.emit("vector", (lambda e, co=co: e.tensor_scalar(
                s_nbf[:, co:co + c.W2], s_bc[:, co:co + c.W2],
                -1.0, None, ALU.mult)),
                reads=["gbc"], writes=["nbf"])
        if p2lvl < 13:
            continue
        for cb in range(c.NCH):
            tok0 = cb * 128
            for z in range(c.NZC):
                zb = PA if z % 2 == 0 else PB
                zbk = "PA" if z % 2 == 0 else "PB"
                k.emit("tensor", [
                    (lambda e, f=f, cb=cb, z=z, zb=zb: e.matmul(
                        zb[:, 0:c.ZW],
                        s_xnf[f][:, 128 * cb:128 * (cb + 1)],
                        s_wz[z][f][:, :],
                        start=(f == 0), stop=(f == c.NF - 1)))
                    for f in range(c.NF)],
                    reads=[f"wz{z}"] + xnf_keys, writes=[zbk])
                k.emit("scalar", (lambda e, z=z, zb=zb: e.activation(
                    s_zg[:, c.ZW * z:c.ZW * (z + 1)],
                    zb[:, 0:c.ZW], AF.Silu)),
                    reads=[zbk], writes=["zg"])
                k.emit("vector", (lambda e, z=z: e.tensor_tensor(
                    s_zg[:, c.ZW * z:c.ZW * (z + 1)],
                    s_zg[:, c.ZW * z:c.ZW * (z + 1)],
                    s_normb[:, 0:c.ZW], ALU.mult)),
                    reads=["zg", "normb"], writes=["zg"])
            if cb == c.NCH - 1 and t < c.NT - 1:
                load_xnf(t + 1)
            k.emit("tensor", (lambda e, tok0=tok0: e.transpose(
                PB[0:128, 0:c.NVH], s_gc[:, tok0:tok0 + 128],
                s_eyef[0:c.NVH, 0:c.NVH])),
                reads=["gbc", "consts"], writes=["PB"])
            k.emit("tensor", (lambda e, tok0=tok0: e.transpose(
                PB[0:128, 64:64 + c.NVH], s_bc[:, tok0:tok0 + 128],
                s_eyef[0:c.NVH, 0:c.NVH])),
                reads=["gbc", "consts"], writes=["PB"])
            k.emit("vector", lambda e: e.tensor_copy(
                s_gtm[:, :], PB[0:128, 0:c.NVH]),
                reads=["PB"], writes=["gbtm"])
            k.emit("vector", lambda e: e.tensor_copy(
                s_btm[:, :], PB[0:128, 64:64 + c.NVH]),
                reads=["PB"], writes=["gbtm"])
            k.emit("tensor", lambda e: e.matmul(
                PB[:, 128:128 + c.NVH], s_ltri[:, :], s_gtm[:, :],
                start=True, stop=True),
                reads=["gbtm", "consts"], writes=["PB"])
            k.emit("vector", lambda e: e.tensor_copy(
                s_G[:, :], PB[:, 128:128 + c.NVH]),
                reads=["PB"], writes=["G"])
            k.emit("tensor", lambda e: e.matmul(
                PB[0:c.NVH, 256:384], s_gtm[:, :], s_ltri[:, :],
                start=True, stop=True),
                reads=["gbtm", "consts"], writes=["PB"])
            k.emit("vector", lambda e: e.tensor_copy(
                s_Gfm[:, :], PB[0:c.NVH, 256:384]),
                reads=["PB"], writes=["Gfm"])
            k.emit("tensor", lambda e: e.matmul(
                PB[:, 384:384 + c.NVH], s_onesq[:, :],
                s_gtm[:, :], start=True, stop=True),
                reads=["gbtm", "consts"], writes=["PB"])
            k.emit("vector", lambda e: e.tensor_tensor(
                s_gam[:, 0:c.NVH], PB[:, 384:384 + c.NVH], s_G[:, :],
                ALU.subtract), reads=["G", "PB"], writes=["gam"])
            k.emit("scalar", lambda e: e.activation(
                s_gam[:, 0:c.NVH], s_gam[:, 0:c.NVH], AF.Exp),
                reads=["gam"], writes=["gam"])
            k.emit("scalar", lambda e: e.activation(
                s_gam[:, c.NVH:2 * c.NVH], PB[:, 384:384 + c.NVH], AF.Exp),
                reads=["PB"], writes=["gam"])
            k.emit("scalar", lambda e: e.activation(
                s_gam[:, 2 * c.NVH:3 * c.NVH], s_G[:, :], AF.Exp),
                reads=["G"], writes=["gam"])
            k.emit("vector", lambda e: e.tensor_tensor(
                s_gam[:, 3 * c.NVH:4 * c.NVH], s_gam[:, 2 * c.NVH:3 * c.NVH],
                s_btm[:, :], ALU.mult),
                reads=["gam", "gbtm"], writes=["gam"])
            k.emit("vector", lambda e: e.tensor_scalar(
                s_gam[:, 3 * c.NVH:4 * c.NVH], s_gam[:, 3 * c.NVH:4 * c.NVH],
                -1.0, None, ALU.mult), reads=["gam"], writes=["gam"])

            if p2lvl < 14:
                continue
            for half in range(c.NVH // 4):
                k.emit("tensor", [
                    (lambda e, hh=hh, half=half: e.matmul(
                        PB[:, 128 * (hh - 4 * half):128 * (hh - 4 * half + 1)],
                        s_rsel[:, 128 * hh:128 * (hh + 1)], s_Gfm[:, :],
                        start=True, stop=True))
                    for hh in range(4 * half, 4 * half + 4)],
                    reads=["Gfm", "consts"], writes=["PB"])
                k.emit("vector", (lambda e, half=half: e.tensor_copy(
                    s_gbc[:, 512 * half:512 * (half + 1)], PB[:, 0:512])),
                    reads=["PB"], writes=["gbc"])
                k.emit("tensor", [
                    (lambda e, hh=hh, half=half, tok0=tok0: e.matmul(
                        PB[:, 128 * (hh - 4 * half):128 * (hh - 4 * half + 1)],
                        s_rsel[:, 128 * hh:128 * (hh + 1)],
                        s_nbf[:, tok0:tok0 + 128],
                        start=True, stop=True))
                    for hh in range(4 * half, 4 * half + 4)],
                    reads=["nbf", "consts"], writes=["PB"])
                k.emit("vector", (lambda e, half=half: e.tensor_copy(
                    s_bbc[:, 512 * half:512 * (half + 1)], PB[:, 0:512])),
                    reads=["PB"], writes=["bbc"])
            NB = [PA, PB]
            NBK = ["PA", "PB"]
            LB = [PC, PD, PP0, PP1]
            LBK = ["PC", "PD", "PP0", "PP1"]
            TRB = [PF0, PF1]
            TRK = ["PF0", "PF1"]

            def kh_gen(kh):
                kp = kh % 2
                nb, nbk = NB[kp], NBK[kp]
                qoc, koc = kh, c.NKH + kh
                qsl = s_qkvs[:, qoc * c.TILE + tok0:qoc * c.TILE + tok0 + 128]
                ksl = s_qkvs[:, koc * c.TILE + tok0:koc * c.TILE + tok0 + 128]
                k.emit("scalar", (lambda e, ksl=ksl, kp=kp: e.activation(
                    s_ksq[kp][:, 0:128], ksl, AF.Square)),
                    reads=[f"qkvs{koc}"], writes=[f"ksq{kp}"])
                yield
                k.emit("scalar", (lambda e, qsl=qsl, kp=kp: e.activation(
                    s_ksq[kp][:, 128:256], qsl, AF.Square)),
                    reads=[f"qkvs{qoc}"], writes=[f"ksq{kp}"])
                yield
                k.emit("tensor", [
                    (lambda e, kp=kp, nb=nb: e.matmul(
                        nb[0:1, 256:384], s_onesc[:, :],
                        s_ksq[kp][:, 0:128], start=True, stop=True)),
                    (lambda e, kp=kp, nb=nb: e.matmul(
                        nb[0:1, 384:512], s_onesc[:, :],
                        s_ksq[kp][:, 128:256], start=True, stop=True))],
                    reads=[f"ksq{kp}", "consts"], writes=[nbk])
                yield
                k.emit("scalar", (lambda e, kp=kp, nb=nb: e.activation(
                    s_sks[kp][:, 0:128], nb[0:1, 256:384], AF.Sqrt,
                    bias=cst(1e-6, 1))),
                    reads=[nbk, "consts"], writes=[f"sks{kp}"])
                yield
                k.emit("scalar", (lambda e, kp=kp, nb=nb: e.activation(
                    s_sks[kp][:, 128:256], nb[0:1, 384:512], AF.Sqrt,
                    scale=cst(128.0, 1), bias=cst(128e-6, 1))),
                    reads=[nbk, "consts"], writes=[f"sks{kp}"])
                yield
                k.emit("vector", (lambda e, kp=kp: e.reciprocal(
                    s_sks[kp][:, :], s_sks[kp][:, :])),
                    reads=[f"sks{kp}"], writes=[f"sks{kp}"])
                yield
                k.emit("tensor", [
                    (lambda e, kp=kp, nb=nb: e.matmul(
                        nb[:, 256:384], s_ones1[:, :],
                        s_sks[kp][:, 0:128], start=True, stop=True)),
                    (lambda e, kp=kp, nb=nb: e.matmul(
                        nb[:, 384:512], s_ones1[:, :],
                        s_sks[kp][:, 128:256], start=True, stop=True))],
                    reads=[f"sks{kp}", "consts"], writes=[nbk])
                yield
                k.emit("vector", (lambda e, ksl=ksl, kh=kh, nb=nb:
                    e.tensor_tensor(
                        s_ktf[kh][:, :], ksl, nb[:, 256:384], ALU.mult)),
                    reads=[f"qkvs{koc}", nbk], writes=[f"ktf{kh}"])
                yield
                k.emit("vector", (lambda e, qsl=qsl, kh=kh, nb=nb:
                    e.tensor_tensor(
                        s_qtf[kh][:, :], qsl, nb[:, 384:512], ALU.mult)),
                    reads=[f"qkvs{qoc}", nbk], writes=[f"qtf{kh}"])
                yield
                k.emit("tensor", (lambda e, kh=kh, kp=kp: e.transpose(
                    TRB[kp][:, 256:384], s_ktf[kh][:, :], s_eyebf[:, :])),
                    reads=[f"ktf{kh}", "consts"], writes=[TRK[kp]])
                yield
                k.emit("vector", (lambda e, kh=kh, kp=kp: e.tensor_copy(
                    s_ktm[kh][:, :], TRB[kp][:, 256:384])),
                    reads=[TRK[kp]], writes=[f"ktm{kh}"])
                yield
                k.emit("tensor", [
                    (lambda e, kh=kh, nb=nb: e.matmul(
                        nb[:, 0:128], s_ktf[kh][:, :], s_ktf[kh][:, :],
                        start=True, stop=True)),
                    (lambda e, kh=kh, nb=nb: e.matmul(
                        nb[:, 128:256], s_ktf[kh][:, :], s_qtf[kh][:, :],
                        start=True, stop=True))],
                    reads=[f"ktf{kh}", f"qtf{kh}"], writes=[nbk])
                yield
                k.emit("vector", (lambda e, kh=kh, nb=nb: e.tensor_copy(
                    s_kk[kh][:, :], nb[:, 0:256])),
                    reads=[nbk], writes=[f"kk{kh}"])
                yield

            def head_gen(hh):
                hp = hh % 2
                ln = hh % 4
                kh = hh // 2
                lb, lbk = LB[ln], LBK[ln]
                tr, trk = TRB[hp], TRK[hp]
                voc = 2 * c.NKH + hh
                for _ in range(ln // 2):
                    yield
                gsl = s_gbc[:, 128 * hh:128 * (hh + 1)]
                bsl = s_bbc[:, 128 * hh:128 * (hh + 1)]
                k.emit("vector", (lambda e, hh=hh, ln=ln, gsl=gsl:
                    e.scalar_tensor_tensor(
                        s_argE[ln][:, :], gsl, s_G[:, hh:hh + 1],
                        s_maskE[:, :], ALU.subtract, ALU.add)),
                    reads=["gbc", "G", "consts"], writes=[f"argE{ln}"])
                yield
                k.emit("vector", (lambda e, hh=hh, ln=ln, gsl=gsl:
                    e.scalar_tensor_tensor(
                        s_argN[ln][:, :], gsl, s_G[:, hh:hh + 1],
                        s_mask2[:, :], ALU.subtract, ALU.add)),
                    reads=["gbc", "G", "consts"], writes=[f"argN{ln}"])
                yield
                k.emit("scalar", (lambda e, ln=ln: e.activation(
                    s_argE[ln][:, :], s_argE[ln][:, :], AF.Exp)),
                    reads=[f"argE{ln}"], writes=[f"argE{ln}"])
                yield
                k.emit("scalar", (lambda e, ln=ln: e.activation(
                    s_argN[ln][:, :], s_argN[ln][:, :], AF.Exp,
                    scale=cst(-1.0))),
                    reads=[f"argN{ln}", "consts"], writes=[f"argN{ln}"])
                yield
                if p2lvl < 1520:
                    return
                k.emit("vector", (lambda e, ln=ln, kh=kh: e.tensor_tensor(
                    s_A2T[ln][:, :], s_argE[ln][:, :],
                    s_kk[kh][:, 128:256], ALU.mult)),
                    reads=[f"argE{ln}", f"kk{kh}"], writes=[f"A2T{ln}"])
                yield
                k.emit("vector", (lambda e, ln=ln: e.tensor_tensor(
                    s_es[ln][:, :], s_argE[ln][:, :], s_mask01[:, :],
                    ALU.mult)),
                    reads=[f"argE{ln}", "consts"], writes=[f"es{ln}"])
                yield
                k.emit("vector", (lambda e, ln=ln, kh=kh: e.tensor_tensor(
                    s_ntm[ln][:, :], s_es[ln][:, :], s_kk[kh][:, 0:128],
                    ALU.mult)),
                    reads=[f"es{ln}", f"kk{kh}"], writes=[f"ntm{ln}"])
                yield
                k.emit("vector", (lambda e, ln=ln, bsl=bsl: e.tensor_tensor(
                    s_PT[ln][0][:, :], s_ntm[ln][:, :], bsl, ALU.mult)),
                    reads=[f"ntm{ln}", "bbc"], writes=[f"PT{ln}0"])
                yield
                k.emit("vector", (lambda e, ln=ln, kh=kh: e.tensor_tensor(
                    s_es[ln][:, :], s_argN[ln][:, :], s_kk[kh][:, 0:128],
                    ALU.mult)),
                    reads=[f"argN{ln}", f"kk{kh}"], writes=[f"es{ln}"])
                yield
                k.emit("vector", (lambda e, hh=hh, ln=ln: e.tensor_scalar(
                    s_P[ln][0][:, :], s_es[ln][:, :],
                    s_btm[:, hh:hh + 1], -1.0,
                    ALU.mult, ALU.mult)),
                    reads=[f"es{ln}", "gbtm"], writes=[f"P{ln}0"])
                yield
                if p2lvl < 1530:
                    return
                k.emit("vector", (lambda e, ln=ln: e.tensor_tensor(
                    s_TT[ln][0][:, :], s_PT[ln][0][:, :], s_eyebf[:, :],
                    ALU.add)),
                    reads=[f"PT{ln}0", "consts"], writes=[f"TT{ln}0"])
                yield
                pp = 0
                for lv in range(1, c.NLEV + 1):
                    np_, op_ = 1 - pp, pp
                    last = (lv == c.NLEV)
                    k.emit("tensor", (lambda e, op_=op_, ln=ln, lb=lb:
                        e.matmul(
                            lb[:, 0:128], s_PT[ln][op_][:, :],
                            s_P[ln][op_][:, :], start=True, stop=True)),
                        reads=[f"P{ln}{op_}", f"PT{ln}{op_}"], writes=[lbk])
                    yield
                    if not last:
                        k.emit("tensor", (lambda e, op_=op_, ln=ln, lb=lb:
                            e.matmul(
                                lb[:, 128:256], s_P[ln][op_][:, :],
                                s_PT[ln][op_][:, :],
                                start=True, stop=True)),
                            reads=[f"P{ln}{op_}", f"PT{ln}{op_}"],
                            writes=[lbk])
                        yield
                    k.emit("scalar", (lambda e, np_=np_, ln=ln, lb=lb:
                        e.activation(
                            s_P[ln][np_][:, :], lb[:, 0:128], AF.Copy)),
                        reads=[lbk], writes=[f"P{ln}{np_}"])
                    yield
                    if not last:
                        k.emit("vector", (lambda e, np_=np_, ln=ln, lb=lb:
                            e.tensor_copy(
                                s_PT[ln][np_][:, :], lb[:, 128:256])),
                            reads=[lbk], writes=[f"PT{ln}{np_}"])
                        yield
                    k.emit("tensor", (lambda e, np_=np_, op_=op_, ln=ln,
                                      lb=lb: e.matmul(
                        lb[:, 256:384], s_P[ln][np_][:, :],
                        s_TT[ln][op_][:, :], start=True, stop=True)),
                        reads=[f"P{ln}{np_}", f"TT{ln}{op_}"], writes=[lbk])
                    yield
                    k.emit("vector", (lambda e, np_=np_, op_=op_, ln=ln,
                                      lb=lb: e.tensor_tensor(
                        s_TT[ln][np_][:, :], lb[:, 256:384],
                        s_TT[ln][op_][:, :], ALU.add)),
                        reads=[lbk, f"TT{ln}{op_}"], writes=[f"TT{ln}{np_}"])
                    yield
                    pp = np_
                if p2lvl < 1540:
                    return
                k.emit("tensor", (lambda e, hh=hh, ln=ln, kh=kh, lb=lb:
                    e.matmul(
                        lb[:, 0:128], s_ktf[kh][:, :], s_Sbf[hh][:, :],
                        start=True, stop=True)),
                    reads=[f"ktf{kh}", f"Sbf{hh}"], writes=[lbk])
                yield
                k.emit("tensor", (lambda e, voc=voc, tok0=tok0, tr=tr:
                    e.transpose(
                        tr[:, 0:128],
                        s_qkvs[:, voc * c.TILE + tok0:
                               voc * c.TILE + tok0 + 128],
                        s_eyebf[:, :])),
                    reads=[f"qkvs{voc}", "consts"], writes=[trk])
                yield
                k.emit("scalar", (lambda e, hh=hh, ln=ln, tr=tr:
                    e.activation(
                        s_dump[ln][:, :], tr[:, 0:128], AF.Copy,
                        scale=s_btm[:, hh:hh + 1])),
                    reads=[trk, "gbtm"], writes=[f"dump{ln}"])
                yield
                k.emit("vector", (lambda e, hh=hh, ln=ln, lb=lb:
                    e.scalar_tensor_tensor(
                        s_RHS[ln][:, :], lb[:, 0:128],
                        s_gam[:, 3 * c.NVH + hh:3 * c.NVH + hh + 1],
                        s_dump[ln][:, :], ALU.mult, ALU.add)),
                    reads=[lbk, "gam", f"dump{ln}"], writes=[f"RHS{ln}"])
                yield
                if p2lvl < 1550:
                    return
                k.emit("tensor", (lambda e, pp=pp, ln=ln, lb=lb: e.matmul(
                    lb[:, 384:512], s_TT[ln][pp][:, :], s_RHS[ln][:, :],
                    start=True, stop=True)),
                    reads=[f"TT{ln}{pp}", f"RHS{ln}"], writes=[lbk])
                yield
                k.emit("scalar", (lambda e, ln=ln, lb=lb: e.activation(
                    s_Xbf[ln][:, :], lb[:, 384:512], AF.Copy)),
                    reads=[lbk], writes=[f"Xbf{ln}"])
                yield
                k.emit("vector", (lambda e, hh=hh, ln=ln, lb=lb:
                    e.tensor_scalar(
                        s_Xw[ln][:, :], lb[:, 384:512],
                        s_gam[:, hh:hh + 1], None, ALU.mult)),
                    reads=[lbk, "gam"], writes=[f"Xw{ln}"])
                yield
                if p2lvl < 1560:
                    return
                k.emit("tensor", (lambda e, hh=hh, ln=ln, kh=kh, lb=lb:
                    e.matmul(
                        lb[:, 128:256], s_qtf[kh][:, :], s_Sbf[hh][:, :],
                        start=True, stop=True)),
                    reads=[f"qtf{kh}", f"Sbf{hh}"], writes=[lbk])
                yield
                k.emit("tensor", (lambda e, ln=ln, lb=lb: e.matmul(
                    lb[:, 256:384], s_A2T[ln][:, :], s_Xbf[ln][:, :],
                    start=True, stop=True)),
                    reads=[f"A2T{ln}", f"Xbf{ln}"], writes=[lbk])
                yield
                k.emit("scalar", (lambda e, hh=hh, ln=ln, lb=lb:
                    e.activation(
                        s_es[ln][:, :], lb[:, 128:256], AF.Copy,
                        scale=s_gam[:, 2 * c.NVH + hh:
                                    2 * c.NVH + hh + 1])),
                    reads=[lbk, "gam"], writes=[f"es{ln}"])
                yield
                k.emit("vector", (lambda e, ln=ln, lb=lb: e.tensor_tensor(
                    s_Otm[ln][:, :], s_es[ln][:, :], lb[:, 256:384],
                    ALU.add)),
                    reads=[f"es{ln}", lbk], writes=[f"Otm{ln}"])
                yield
                if p2lvl < 1570:
                    return
                k.emit("tensor", (lambda e, ln=ln, kh=kh, lb=lb: e.matmul(
                    lb[:, 0:128], s_ktm[kh][:, :], s_Xw[ln][:, :],
                    start=True, stop=True)),
                    reads=[f"ktm{kh}", f"Xw{ln}"], writes=[lbk])
                yield
                k.emit("vector", (lambda e, hh=hh, lb=lb:
                    e.scalar_tensor_tensor(
                        s_S[hh][:, :], s_S[hh][:, :],
                        s_gam[:, c.NVH + hh:c.NVH + hh + 1],
                        lb[:, 0:128], ALU.mult, ALU.add)),
                    reads=[f"S{hh}", "gam", lbk], writes=[f"S{hh}"])
                yield
                k.emit("scalar", (lambda e, hh=hh: e.activation(
                    s_Sbf[hh][:, :], s_S[hh][:, :], AF.Copy)),
                    reads=[f"S{hh}"], writes=[f"Sbf{hh}"])
                yield
                if p2lvl < 1580:
                    return
                k.emit("scalar", (lambda e, ln=ln: e.activation(
                    s_dump[ln][:, :], s_Otm[ln][:, :], AF.Square,
                    accum_out=s_oc2[ln][:, 0:1])),
                    reads=[f"Otm{ln}"], writes=[f"dump{ln}", f"oc2{ln}"])
                yield
                k.emit("scalar", (lambda e, ln=ln: e.activation(
                    s_oc2[ln][:, 1:2], s_oc2[ln][:, 0:1], AF.Sqrt,
                    scale=cst(1.0 / 128), bias=cst(1e-6))),
                    reads=[f"oc2{ln}", "consts"], writes=[f"oc2{ln}"])
                yield
                k.emit("vector", (lambda e, ln=ln: e.reciprocal(
                    s_oc2[ln][:, 1:2], s_oc2[ln][:, 1:2])),
                    reads=[f"oc2{ln}"], writes=[f"oc2{ln}"])
                yield
                k.emit("vector", (lambda e, hh=hh, ln=ln:
                    e.scalar_tensor_tensor(
                        s_Xbf[ln][:, :], s_Otm[ln][:, :],
                        s_oc2[ln][:, 1:2],
                        s_zg[:, 128 * hh:128 * (hh + 1)],
                        ALU.mult, ALU.mult)),
                    reads=[f"Otm{ln}", f"oc2{ln}", "zg"],
                    writes=[f"Xbf{ln}"])
                yield
                k.emit("tensor", (lambda e, ln=ln, tr=tr: e.transpose(
                    tr[:, 128:256], s_Xbf[ln][:, :], s_eyebf[:, :])),
                    reads=[f"Xbf{ln}", "consts"], writes=[trk])
                yield
                k.emit("vector", (lambda e, ln=ln, tr=tr: e.tensor_copy(
                    s_ocb[ln][:, :], tr[:, 128:256])),
                    reads=[trk], writes=[f"ocb{ln}"])
                yield
                k.dma("sync", f"d_of{ln}", (lambda e, hh=hh, tok0=tok0,
                                            t=t, ln=ln: e.dma_start(
                    out=ofm_sc[hh, :, t * c.TILE + tok0:
                               t * c.TILE + tok0 + 128],
                    in_=s_ocb[ln][:, :])),
                    reads=[f"ocb{ln}"], writes=["ofm_sc"])
                yield

            def drive(gens):
                alive = list(gens)
                while alive:
                    for g_ in list(alive):
                        try:
                            next(g_)
                        except StopIteration:
                            alive.remove(g_)

            drive([kh_gen(kh) for kh in range(min(2, c.NKH))])
            if p2lvl >= 15:
                grp = [head_gen(h) for h in range(min(4, c.NVH))]
                if c.NKH > 2:
                    grp += [kh_gen(kh) for kh in range(2, c.NKH)]
                drive(grp)
                if c.NVH > 4:
                    drive([head_gen(h) for h in range(4, c.NVH)])
    if upto <= 2:
        k.fence()
        stP2.close()
        return finish_dummy()

    k.fence()
    stP2.close()
    st25 = ExitStack()
    s_wout = [st25.enter_context(
        nc.sbuf_tensor(f"s_wout{v}", [128, c.D], BF16))
        for v in range(c.NVH)]
    s_ofm2 = [[st25.enter_context(
        nc.sbuf_tensor(f"s_ofm2{p}_{v}", [128, c.TILE], BF16))
        for v in range(c.NVH)] for p in range(2)]
    s_sfm = [st25.enter_context(
        nc.sbuf_tensor(f"s_sfm{i}", [128, c.TILE], F32)) for i in range(2)]

    st2 = ExitStack()
    s_rs = st2.enter_context(nc.sbuf_tensor("s_rs", [1, c.TQ], BF16))
    s_gus = st2.enter_context(nc.sbuf_tensor("s_gus", [1, c.TQ], BF16))

    st3 = ExitStack()
    s_gts = st3.enter_context(nc.sbuf_tensor("s_gts", [128, c.TQ], F32))
    s_gud = [st3.enter_context(nc.sbuf_tensor(f"s_gud{i}", [128, c.W4], BF16))
             for i in range(2)]
    s_xnq = [st3.enter_context(nc.sbuf_tensor(f"s_xnq{f}", [128, c.TQ], BF16))
             for f in range(c.NF)]
    NWSL = 3
    s_wgu = [[[st3.enter_context(
        nc.sbuf_tensor(f"s_wgu{p}_{m}_{f}", [128, 512], BF16))
        for f in range(c.NF)] for m in range(2)] for p in range(NWSL)]

    def load_wgu(hb, par):
        for m, wsrc in ((0, wgT), (1, wuT)):
            k.dma("gpsimd", f"d_f{m}{par}", [
                (lambda e, m=m, f=f, hb=hb, par=par, wsrc=wsrc:
                 e.dma_start(
                     out=s_wgu[par][m][f][:, :],
                     in_=wsrc[128 * f:128 * (f + 1),
                              512 * hb:512 * (hb + 1)]))
                for f in range(c.NF)], writes=[f"wgu{par}"])

    k.dma("gpsimd", "d_wo", [
        (lambda e, v=v: e.dma_start(out=s_wout[v][:, :],
                                    in_=wout[128 * v:128 * (v + 1), :]))
        for v in range(c.NVH)], writes=["woutsb"])
    k.dma("sync", "d_xnq", [
        (lambda e, f=f: e.dma_start(
            out=s_xnq[f][:, :],
            in_=xnq_sc[:, 128 * f:128 * (f + 1)], transpose=True))
        for f in range(c.NF)],
        reads=["xnq_sc"], writes=["xnq"])
    for pp_ in range(min(NWSL, c.NHB)):
        load_wgu(pp_, pp_)

    def load_ofm(t, par):
        k.dma("sync", f"d_or{par}", [
            (lambda e, v=v, t=t, par=par: e.dma_start(
                out=s_ofm2[par][v][:, :],
                in_=ofm_sc[v, :, t * c.TILE:(t + 1) * c.TILE]))
            for v in range(c.NVH)],
            reads=["ofm_sc"], writes=[f"ofm2{par}"])

    load_ofm(0, 0)
    if c.NT > 1:
        load_ofm(1, 1)
    for t in range(c.NT):
        par = t % 2
        blk = t // (c.NT // 4)
        colo = (t % (c.NT // 4)) * c.TILE
        for outc in range(c.OUTC):
            opar = outc % 2
            bank, bkey = (PC, "PC") if opar == 0 else (PD, "PD")
            k.emit("tensor", [
                (lambda e, v=v, outc=outc, par=par, bank=bank: e.matmul(
                    bank[:, 0:c.TILE],
                    s_wout[v][:, 128 * outc:128 * (outc + 1)],
                    s_ofm2[par][v][:, :],
                    start=(v == 0), stop=(v == c.NVH - 1)))
                for v in range(c.NVH)],
                reads=["woutsb", f"ofm2{par}"], writes=[bkey])
            k.emit("scalar", (lambda e, bank=bank, opar=opar: e.activation(
                s_sfm[opar][:, 0:c.TILE], bank[:, 0:c.TILE], AF.Copy)),
                reads=[bkey], writes=[f"sfm{opar}"])
            k.dma("sync", f"d_ri{opar}", (lambda e, outc=outc, blk=blk,
                                          colo=colo, opar=opar: e.dma_start(
                out=rsin[blk, 128 * outc:128 * (outc + 1),
                         colo:colo + c.TILE],
                in_=s_sfm[opar][:, 0:c.TILE])),
                reads=[f"sfm{opar}"], writes=["rsin"])
        if t + 2 < c.NT:
            load_ofm(t + 2, par)

    k.emit("gpsimd", (lambda e: e.collective_compute(
        "ReduceScatter", ALU.add,
        replica_groups=[[0, 1, 2, 3], [4, 5, 6, 7]],
        ins=[rsin.ap().opt()], outs=[rsout.ap().opt()])),
        reads=["rsin"], writes=["rsout"])

    if upto <= 3:
        return finish_dummy()

    TQH = c.TQ // c.W4
    GUPB = [(PP0, "PP0"), (PP1, "PP1"), (PA, "PA"), (PB, "PB")]
    for h2 in range(TQH):
        bank, bkey = (PC, "PC") if h2 % 2 == 0 else (PD, "PD")
        k.emit("tensor", [
            (lambda e, f=f, h2=h2, bank=bank: e.matmul(
                bank[0:1, 0:c.W4], s_onesb[:, :],
                s_xnq[f][:, c.W4 * h2:c.W4 * (h2 + 1)],
                start=(f == 0), stop=(f == c.NF - 1)))
            for f in range(c.NF)],
            reads=["xnq", "consts"], writes=[bkey])
        k.emit("vector", (lambda e, h2=h2, bank=bank: e.tensor_copy(
            s_rs[:, c.W4 * h2:c.W4 * (h2 + 1)], bank[0:1, 0:c.W4])),
            reads=[bkey], writes=["rs"])

    for hb in range(c.NHB):
        par = hb % NWSL
        for oc in range(4 if c.HID >= 512 else c.NHC):
            hc = 4 * hb + oc
            for h2 in range(TQH):
                gb, gk = GUPB[2 * (h2 % 2)]
                ub, uk = GUPB[2 * (h2 % 2) + 1]
                k.emit("tensor", [
                    (lambda e, f=f, oc=oc, h2=h2, par=par, gb=gb: e.matmul(
                        gb[:, 0:c.W4],
                        s_wgu[par][0][f][:, 128 * oc:128 * (oc + 1)],
                        s_xnq[f][:, c.W4 * h2:c.W4 * (h2 + 1)],
                        start=(f == 0), stop=False))
                    for f in range(c.NF)] + [
                    (lambda e, h2=h2, gb=gb: e.matmul(
                        gb[:, 0:c.W4], s_m128[:, :],
                        s_rs[:, c.W4 * h2:c.W4 * (h2 + 1)],
                        start=False, stop=True))],
                    reads=[f"wgu{par}", "xnq", "rs", "consts"], writes=[gk])
                k.emit("tensor", [
                    (lambda e, f=f, oc=oc, h2=h2, par=par, ub=ub: e.matmul(
                        ub[:, 0:c.W4],
                        s_wgu[par][1][f][:, 128 * oc:128 * (oc + 1)],
                        s_xnq[f][:, c.W4 * h2:c.W4 * (h2 + 1)],
                        start=(f == 0), stop=False))
                    for f in range(c.NF)] + [
                    (lambda e, h2=h2, ub=ub: e.matmul(
                        ub[:, 0:c.W4], s_m128[:, :],
                        s_rs[:, c.W4 * h2:c.W4 * (h2 + 1)],
                        start=False, stop=True))],
                    reads=[f"wgu{par}", "xnq", "rs", "consts"], writes=[uk])
                k.emit("scalar", (lambda e, hc=hc, h2=h2, gb=gb: e.activation(
                    s_gts[:, c.W4 * h2:c.W4 * (h2 + 1)], gb[:, 0:c.W4],
                    AF.Silu, scale=s_sg2[:, hc:hc + 1])),
                    reads=[gk, "consts"], writes=["gts"])
                gpar = (hc * TQH + h2) % 2
                k.emit("vector", (lambda e, hc=hc, h2=h2, ub=ub, gpar=gpar:
                    e.scalar_tensor_tensor(
                        s_gud[gpar][:, :],
                        ub[:, 0:c.W4], s_su2[:, hc:hc + 1],
                        s_gts[:, c.W4 * h2:c.W4 * (h2 + 1)],
                        ALU.mult, ALU.mult)),
                    reads=[uk, "consts", "gts"], writes=[f"gud{gpar}"])
                gusb, gusk = (PC, "PC") if h2 % 2 == 0 else (PD, "PD")
                k.emit("tensor", (lambda e, gpar=gpar, hc=hc, gusb=gusb:
                    e.matmul(
                        gusb[0:1, 0:c.W4], s_onesb[:, :], s_gud[gpar][:, :],
                        start=(hc == 0), stop=(hc == c.NHC - 1))),
                    reads=[f"gud{gpar}", "consts"], writes=[gusk])
                k.dma("sync", f"d_gw{gpar}", (lambda e, gpar=gpar, hc=hc,
                                              h2=h2: e.dma_start(
                    out=gu_sc[128 * hc:128 * (hc + 1),
                              c.W4 * h2:c.W4 * (h2 + 1)],
                    in_=s_gud[gpar][:, :])),
                    reads=[f"gud{gpar}"], writes=["gu_sc"])
        if hb + NWSL < c.NHB:
            load_wgu(hb + NWSL, par)
    for h2 in range(TQH):
        bank, bkey = (PC, "PC") if h2 % 2 == 0 else (PD, "PD")
        k.emit("vector", (lambda e, h2=h2, bank=bank: e.tensor_copy(
            s_gus[:, c.W4 * h2:c.W4 * (h2 + 1)], bank[0:1, 0:c.W4])),
            reads=[bkey], writes=["gus"])
    k.fence()
    st3.close()

    if upto <= 4:
        return finish_dummy()
    st4 = ExitStack()
    s_acc = st4.enter_context(
        nc.sbuf_tensor("s_acc", [128, c.OUTC * c.TQ], F32))
    s_wd = [[st4.enter_context(
        nc.sbuf_tensor(f"s_wd{p}_{j}", [128, c.D], BF16))
        for j in range(c.GSZ)] for p in range(2)]
    s_gur = [[st4.enter_context(
        nc.sbuf_tensor(f"s_gur{p}_{j}", [128, c.TQ], BF16))
        for j in range(c.GSZ)] for p in range(2)]
    s_fin = [st4.enter_context(
        nc.sbuf_tensor(f"s_fin{i}", [128, c.TQ], F32)) for i in range(4)]

    def load_wd(g, par):
        hcs = list(range(c.GSZ * g, min(c.GSZ * (g + 1), c.NHC)))
        k.dma("gpsimd", f"d_wd{par}", [
            (lambda e, j=j, hc=hc, par=par: e.dma_start(
                out=s_wd[par][j][:, :], in_=wdT[128 * hc:128 * (hc + 1), :]))
            for j, hc in enumerate(hcs)], writes=[f"wd{par}"])
        k.dma("sync", f"d_gr{par}", [
            (lambda e, j=j, hc=hc, par=par: e.dma_start(
                out=s_gur[par][j][:, :],
                in_=gu_sc[128 * hc:128 * (hc + 1), :]))
            for j, hc in enumerate(hcs)],
            reads=["gu_sc"], writes=[f"gur{par}"])

    load_wd(0, 0)
    for g in range(c.NGRP):
        par = g % 2
        if g + 1 < c.NGRP:
            load_wd(g + 1, 1 - par)
        hcs = list(range(c.GSZ * g, min(c.GSZ * (g + 1), c.NHC)))
        for outc in range(c.OUTC):
            for h2 in range(TQH):
                bank, bkey = GUPB[2 * (outc % 2) + h2 % 2]
                fns = [
                    (lambda e, j=j, hc=hc, outc=outc, h2=h2, par=par,
                     bank=bank, hn=len(hcs), g=g: e.matmul(
                        bank[:, 0:c.W4],
                        s_wd[par][j][:, 128 * outc:128 * (outc + 1)],
                        s_gur[par][j][:, c.W4 * h2:c.W4 * (h2 + 1)],
                        start=(j == 0),
                        stop=(j == hn - 1 and g != c.NGRP - 1)))
                    for j, hc in enumerate(hcs)]
                if g == c.NGRP - 1:
                    fns.append(lambda e, h2=h2, bank=bank: e.matmul(
                        bank[:, 0:c.W4], s_m128[:, :],
                        s_gus[:, c.W4 * h2:c.W4 * (h2 + 1)],
                        start=False, stop=True))
                k.emit("tensor", fns,
                       reads=[f"wd{par}", f"gur{par}", "gus", "consts"],
                       writes=[bkey])
                ao = outc * c.TQ + c.W4 * h2
                if g == 0:
                    k.emit("vector", (lambda e, ao=ao, bank=bank: e.tensor_copy(
                        s_acc[:, ao:ao + c.W4], bank[:, 0:c.W4])),
                        reads=[bkey], writes=[f"acc{outc}"])
                else:
                    k.emit("vector", (lambda e, ao=ao, bank=bank, outc=outc:
                        e.tensor_tensor(
                            s_acc[:, ao:ao + c.W4], bank[:, 0:c.W4],
                            s_acc[:, ao:ao + c.W4], ALU.add)),
                        reads=[bkey, f"acc{outc}"], writes=[f"acc{outc}"])

    for outc in range(c.OUTC):
        k.dma("sync", "d_ro", (lambda e, outc=outc: e.dma_start(
            out=s_fin[0][:, :], in_=rsout[128 * outc:128 * (outc + 1), :])),
            reads=["rsout"], writes=["fin0"])
        k.dma("sync", "d_xt", (lambda e, outc=outc: e.dma_start(
            out=s_fin[1][:, :], in_=xqT[128 * outc:128 * (outc + 1), :])),
            writes=["fin1"])
        k.emit("scalar", (lambda e, outc=outc: e.activation(
            s_fin[2][:, :], s_acc[:, outc * c.TQ:(outc + 1) * c.TQ], AF.Silu,
            scale=s_sd2[:, outc:outc + 1])),
            reads=[f"acc{outc}", "consts"], writes=["fin2"])
        k.emit("vector", lambda e: e.tensor_tensor(
            s_fin[3][:, :], s_fin[0][:, :], s_fin[2][:, :], ALU.mult),
            reads=["fin0", "fin2"], writes=["fin3"])
        k.emit("vector", lambda e: e.tensor_tensor(
            s_fin[3][:, :], s_fin[3][:, :], s_fin[1][:, :], ALU.add),
            reads=["fin3", "fin1"], writes=["fin3"])
        k.dma("sync", "d_out", (lambda e, outc=outc: e.dma_start(
            out=outfm[128 * outc:128 * (outc + 1), :], in_=s_fin[3][:, :])),
            reads=["fin3"], writes=[f"out{outc}"])

    k._wait("gpsimd", "d_out", k.count["d_out"])

    semh = {}
    for name in sorted(set(list(k.count.keys()) + list(K.ESEM.values()))):
        semh[name] = st.enter_context(nc.semaphore(name))
    k.replay(nc, semh)
    st4.close()
    st2.close()
    st25.close()
    st.close()
    return nc


def stage(cfg, A):
    c = cfg
    x = np.ascontiguousarray(np.asarray(A["L0_attn_hidden_states"], np.float32))
    Wqkv = np.asarray(A["L0_attn_W_qkv"], np.float32)
    Wz = np.asarray(A["L0_attn_W_z"], np.float32)
    Wb = np.asarray(A["L0_attn_W_b"], np.float32)
    Wa = np.asarray(A["L0_attn_W_a"], np.float32)
    convw = np.asarray(A["L0_attn_conv_w"], np.float32)
    dtb = np.asarray(A["L0_attn_dt_bias"], np.float32)
    alog = np.asarray(A["L0_attn_A_log"], np.float32)
    normw = np.asarray(A["L0_attn_norm_w"], np.float32)
    Wout = np.asarray(A["L0_attn_W_out"], np.float32)
    wqg = np.asarray(A["L0_ffn_wq_gate"], np.int32)
    sg = np.asarray(A["L0_ffn_s_gate"], np.float32)
    wqu = np.asarray(A["L0_ffn_wq_up"], np.int32)
    su = np.asarray(A["L0_ffn_s_up"], np.float32)
    wqd = np.asarray(A["L0_ffn_wq_down"], np.int32)
    sd = np.asarray(A["L0_ffn_s_down"], np.float32)

    KD = 4 * c.QW
    wgT = np.ascontiguousarray(wqg.T)
    wuT = np.ascontiguousarray(wqu.T)
    wdT = np.ascontiguousarray(wqd.T)
    sg2 = np.ascontiguousarray(sg.reshape(c.NHC, 128).T)
    su2 = np.ascontiguousarray(su.reshape(c.NHC, 128).T)
    sd2 = np.ascontiguousarray(sd.reshape(c.OUTC, 128).T)
    idx = np.arange(128)
    consts = dict(
        eyebf=np.eye(128, dtype=BF),
        eyef=np.eye(128, dtype=np.float32),
        ltri=np.triu(np.ones((128, 128), np.float32)),
        maskE=np.where(idx[None, :] >= idx[:, None], 0.0, -1e30).astype(np.float32),
        mask2=np.where(idx[None, :] < idx[:, None], 0.0, 1e30).astype(np.float32),
        mask01=np.triu(np.ones((128, 128), np.float32), 1),
        rowsel=np.ascontiguousarray(np.repeat(
            np.eye(cfg.NVH, dtype=np.float32), 128, axis=1)),
        normw4=np.ascontiguousarray(np.tile(normw, 512 // normw.shape[0])[None, :]).astype(np.float32),
    )
    in_maps = []
    for core in range(8):
        b, hg = core // 4, core % 4
        qsl = slice(c.QW * hg, c.QW * (hg + 1))
        ksl = slice(KD + c.QW * hg, KD + c.QW * (hg + 1))
        vsl = slice(2 * KD + c.VW * hg, 2 * KD + c.VW * (hg + 1))
        zsl = slice(c.VW * hg, c.VW * (hg + 1))
        hsl = slice(c.NVH * hg, c.NVH * (hg + 1))
        wg = np.concatenate(
            [Wqkv[:, qsl], Wqkv[:, ksl], Wqkv[:, vsl]], axis=1)
        wgc = np.ascontiguousarray(wg.reshape(c.NF, 128, c.WC))
        wzc = np.ascontiguousarray(
            Wz[:, zsl].reshape(c.NF, 128, c.NZC, c.ZW).transpose(0, 2, 1, 3))
        wab = np.concatenate([Wa[:, hsl], Wb[:, hsl]], axis=1)
        wabc = np.ascontiguousarray(wab.reshape(c.NF, 128, c.ABW))
        cw = np.concatenate([convw[qsl], convw[ksl], convw[vsl]], axis=0)
        xb = x[b]
        m = dict(
            xfull=xb,
            xq=np.ascontiguousarray(xb[c.TQ * hg:c.TQ * (hg + 1)]),
            xqT=np.ascontiguousarray(xb[c.TQ * hg:c.TQ * (hg + 1)].T),
            wgc=wgc, wzc=wzc, wabc=wabc,
            convw=np.ascontiguousarray(cw),
            alog=np.ascontiguousarray(alog[hsl][:, None]),
            dtb=np.ascontiguousarray(dtb[hsl][:, None]),
            wout=np.ascontiguousarray(Wout[c.VW * hg:c.VW * (hg + 1)]),
            wgT=wgT, wuT=wuT, wdT=wdT, sg2=sg2, su2=su2, sd2=sd2,
            **consts)
        in_maps.append(m)
    return in_maps


def assemble(cfg, outs):
    c = cfg
    out = np.empty((2, c.T, c.D), np.float32)
    for core in range(8):
        b, hg = core // 4, core % 4
        out[b, c.TQ * hg:c.TQ * (hg + 1), :] = outs[core].T
    return out


def kernel(**inputs):
    cfg = REAL
    in_maps = stage(cfg, inputs)
    nc = build(cfg)
    res = run_bass_kernel_spmd(nc, in_maps, core_ids=list(range(8)))
    return assemble(cfg, [np.asarray(r["outfm"]) for r in res.results])
